# version 12
# baseline (speedup 1.0000x reference)
import numpy as np
import contextlib
import concourse.bass as bass
import concourse.mybir as mybir
from concourse.bass_utils import run_bass_kernel_spmd

F32 = mybir.dt.float32
AF = mybir.ActivationFunctionType
ALU = mybir.AluOpType
AX = mybir.AxisListType
MAGIC = 12582912.0
NEG_BIG = -3.0e38

B, T, C, H, D, I = 4, 4096, 1024, 16, 64, 3072
TL = 2048
RG_PAIR = [[0, 1], [2, 3], [4, 5], [6, 7]]
RG_ALL = [[0, 1, 2, 3, 4, 5, 6, 7]]


class Sched:
    def __init__(self):
        self.ops = {k: [] for k in ("sync", "tensor", "vector", "scalar", "gpsimd")}
        self.cnt = {}

    def inc(self, name, by):
        self.cnt[name] = self.cnt.get(name, 0) + by
        return self.cnt[name]

    def val(self, name):
        return self.cnt.get(name, 0)


def build_program(debug=(), stop_after=None):
    nc = bass.Bass("TRN2", target_bir_lowering=False)

    def din(name, shape):
        return nc.dram_tensor(name, shape, F32, kind="ExternalInput").ap()

    def dint(name, shape, **kw):
        return nc.dram_tensor(name, shape, F32, kind="Internal", **kw).ap()

    xbT = din("xbT", [C, T])
    xtokT = din("xtokT", [C, TL])
    wq = din("wq", [4, C, 128])
    wk = din("wk", [4, C, 128])
    wv = din("wv", [4, C, 128])
    wp = din("wp", [C, C])
    bpc = din("bpc", [C, 1])
    rtT = din("rtT", [16, 33])
    wgate = din("wgate", [I, C])
    wup = din("wup", [I, C])
    wdown = din("wdown", [C, I])
    ident = din("ident", [128, 128])
    dmasks = din("dmasks", [17, 128, 144])
    maskneg = din("maskneg", [128, 144])
    gmask = din("gmask", [1, 2])
    out_d = nc.dram_tensor("out", [C, TL], F32, kind="ExternalOutput").ap()

    xhat_d = dint("xhat_d", [C, T])
    rsin = [dint(f"rsin{p}", [2, 256, TL]) for p in range(4)]
    rsout = [dint(f"rsout{p}", [256, TL]) for p in range(4)]
    arin_d = [dint(f"arin{k}", [1, 128]) for k in range(6)]
    arout_d = [dint(f"arout{k}", [1, 128], addr_space="Shared") for k in range(6)]
    gate_d = dint("gate_d", [I, TL])
    up_d = dint("up_d", [I, TL])
    g_d = dint("g_d", [I, TL])
    opre_d = dint("opre_d", [I, TL])
    dbg_d = {nm: nc.dram_tensor("dbg_" + nm, list(shp), F32, kind="ExternalOutput").ap()
             for nm, shp in debug}

    S = Sched()
    es = contextlib.ExitStack()
    with es:
        A1 = es.enter_context(nc.sbuf_tensor("A1", [128, 16384], F32))
        A2 = es.enter_context(nc.sbuf_tensor("A2", [128, 16384], F32))
        A3 = es.enter_context(nc.sbuf_tensor("A3", [128, 8192], F32))
        WA = es.enter_context(nc.sbuf_tensor("WA", [128, 6208], F32))
        SM = es.enter_context(nc.sbuf_tensor("SM", [128, 1600], F32))
        RW = es.enter_context(nc.sbuf_tensor("RW", [128, 4300], F32))
        PB = [es.enter_context(nc.psum_tensor(f"pb{i}", [128, 512], F32)) for i in range(8)]
        sems = {nm: es.enter_context(nc.semaphore(nm))
                for nm in ("LD0", "LD1", "LD2", "LD3", "LD4", "LD5",
                           "SP0", "SP1", "SP2", "GD0", "GD1", "TE", "VE", "SE", "CC")}
        rot = {"sync": ["LD0", "LD1", "LD2", "LD3", "LD4", "LD5"],
               "scalar": ["SP0", "SP1", "SP2"], "gpsimd": ["GD0", "GD1"]}
        rot_i = {"sync": 0, "scalar": 0, "gpsimd": 0}
        blk = es.enter_context(nc.Block())

        def waits(e, need):
            flat = {}
            for it in need:
                if it is None:
                    continue
                items = it if isinstance(it, list) else [it]
                for ent in items:
                    if ent is None:
                        continue
                    nm, v = ent
                    if v > 0:
                        flat[nm] = max(flat.get(nm, 0), v)
            for nm, v in flat.items():
                e.wait_ge(sems[nm], v)

        def dma(queue, out, in_, need=(), sem=None, slow=False):
            if sem is None:
                sem = rot[queue][rot_i[queue]]
                rot_i[queue] = (rot_i[queue] + 1) % len(rot[queue])
            prev = S.val(sem)
            tgt = S.inc(sem, 16)
            nd = tuple(need)

            def fn(e, out=out, in_=in_, nd=nd, sem=sem, slow=slow, prev=prev):
                waits(e, nd)
                if prev > 0:
                    e.wait_ge(sems[sem], prev)
                e.dma_start(out=out, in_=in_,
                            allow_slow_non_contiguous=slow).then_inc(sems[sem], 16)
            S.ops[queue].append(fn)
            return (sem, tgt)

        def qall(queue):
            return [(s, S.val(s)) for s in rot[queue] if S.val(s) > 0]

        def te(body, need=()):
            tgt = S.inc("TE", 1)
            nd = tuple(need)

            def fn(e, nd=nd, body=body):
                waits(e, nd)
                body(e).then_inc(sems["TE"], 1)
            S.ops["tensor"].append(fn)
            return tgt

        def ve(body, need=(), pre_drain=True):
            tgt = S.inc("VE", 1)
            nd = tuple(need)

            def fn(e, nd=nd, body=body, pre_drain=pre_drain):
                waits(e, nd)
                if pre_drain:
                    e.drain()
                body(e).then_inc(sems["VE"], 1)
            S.ops["vector"].append(fn)
            return tgt

        def ve0(body, need=(), pre_drain=False):
            nd = tuple(need)

            def fn(e, nd=nd, body=body, pre_drain=pre_drain):
                waits(e, nd)
                if pre_drain:
                    e.drain()
                body(e)
            S.ops["vector"].append(fn)

        def se(body, need=()):
            tgt = S.inc("SE", 1)
            nd = tuple(need)

            def fn(e, nd=nd, body=body):
                waits(e, nd)
                body(e).then_inc(sems["SE"], 1)
            S.ops["scalar"].append(fn)
            return tgt

        def gp(body, need=()):
            tgt = S.inc("CC", 1)
            nd = tuple(need)

            def fn(e, nd=nd, body=body):
                waits(e, nd)
                body(e).then_inc(sems["CC"], 1)
            S.ops["gpsimd"].append(fn)
            return tgt

        class _StopEmit(Exception):
            pass

        ID_ = SM[:, 0:128]
        BANDS = 128
        RTT = SM[:, 1280:1313]
        DMAT = SM[:, 1313:1330]
        ONES = SM[:, 1330:1331]
        BP0 = 1331
        EB0 = 1344
        DT0 = 1346
        SC0 = 1362
        EPS = SM[:, 1394:1395]
        ONESR = SM[0:1, 1440:1568]
        GMB = SM[:, 1396:1398]
        QS0 = 1400
        WMX = 1412

        truncated = [False]
        try:
            dma("sync", ID_, ident)
            dma("sync", RTT[0:16, :], rtT)
            ld_masks = dma("sync", A3[:, 0:2448].rearrange("p (v j) -> p v j", j=144),
                           dmasks.rearrange("v p j -> p v j"))
            ld_mneg = dma("sync", A3[:, 2448:2592], maskneg)
            ld_gm = dma("sync", RW[0:1, 4224:4226], gmask)
            dma("sync", SM[:, BP0:BP0 + 8], bpc.rearrange("(cc p) o -> p (cc o)", p=128), slow=True)
            ld_const = qall("sync")

            ve(lambda e: e.memset(ONES, 1.0), pre_drain=False)
            ve(lambda e: e.memset(ONESR, 1.0), pre_drain=False)
            ve(lambda e: e.memset(EPS, 1e-5), pre_drain=False)
            v_neg = ve(lambda e: e.memset(RW[0:1, 4096:4224], NEG_BIG), pre_drain=False)
            t_gm = te(lambda e: e.matmul(PB[7][:, 0:2], ONESR, RW[0:1, 4224:4226],
                                         start=True, stop=True),
                      need=[ld_const, ("VE", v_neg)])
            s_gm = se(lambda e: e.copy(GMB, PB[7][:, 0:2]), need=[("TE", t_gm)])

            v_dmat = ve(lambda e: e.tensor_scalar(
                out=DMAT[0:16, :], in0=RTT[0:16, 16::-1], scalar1=RTT[0:16, 0:1],
                scalar2=None, op0=ALU.subtract), need=[ld_const])
            v_mh = ve(lambda e: e.reduce_max(SM[0:16, SC0 + 20:SC0 + 21], RTT[0:16, 0:17],
                                             axis=AX.X))
            v_pre = ve(lambda e: e.tensor_tensor(
                out=SM[0:16, SC0 + 21:SC0 + 22], in0=RTT[0:16, 0:1],
                in1=SM[0:16, SC0 + 20:SC0 + 21], op=ALU.subtract))
            t_dt = te(lambda e: e.transpose(PB[7][0:17, 0:16], DMAT[0:16, :], ID_[0:16, 0:16]),
                      need=[("VE", v_dmat), ("SE", s_gm)])
            s_dt = se(lambda e: e.copy(SM[0:17, DT0:DT0 + 16], PB[7][0:17, 0:16]),
                      need=[("TE", t_dt)])
            t_pre = te(lambda e: e.transpose(PB[7][0:1, 0:16], SM[0:16, SC0 + 21:SC0 + 22], ID_[0:16, 0:16]),
                       need=[("VE", v_pre), ("SE", s_dt)])
            s_pre = se(lambda e: e.copy(RW[0:1, 4243:4259], PB[7][0:1, 0:16]),
                       need=[("TE", t_pre)])
            PREROW = RW[0:1, 4243:4259]

            pa, pb_ = A3[:, 2592:2736], A3[:, 2736:2880]
            s_r_prev = s_pre
            band_v = 0
            for hl in range(8):
                t_r = te(lambda e, hl=hl: e.transpose(PB[7][0:1, 0:17],
                                                      SM[0:17, DT0 + hl:DT0 + hl + 1],
                                                      ID_[0:17, 0:17]),
                         need=[("SE", s_r_prev)])
                s_r = se(lambda e: e.copy(RW[0:1, 4226:4243], PB[7][0:1, 0:17]),
                         need=[("TE", t_r)])
                s_r_prev = s_r
                t_b = te(lambda e: e.matmul(PB[6][:, 0:17], ONESR,
                                            RW[0:1, 4226:4243], start=True, stop=True),
                         need=[("SE", s_r)])
                s_b = se(lambda e: e.copy(SM[:, SC0:SC0 + 17], PB[6][:, 0:17]),
                         need=[("TE", t_b), ("VE", band_v)])
                ve(lambda e: e.tensor_copy(pa, A3[:, 2448:2592]),
                   need=[ld_const, ("SE", s_b)])
                for v_i in range(17):
                    src, dst = (pa, pb_) if v_i % 2 == 0 else (pb_, pa)
                    ve(lambda e, v_i=v_i, src=src, dst=dst: e.scalar_tensor_tensor(
                        out=dst, in0=A3[:, v_i * 144:(v_i + 1) * 144],
                        scalar=SM[:, SC0 + v_i:SC0 + v_i + 1], in1=src,
                        op0=ALU.mult, op1=ALU.add))
                band_v = ve(lambda e, hl=hl: e.tensor_copy(
                    SM[:, BANDS + hl * 144:BANDS + (hl + 1) * 144], pb_))

            d_w0 = dma("sync", arin_d[5], RW[0:1, 4096:4224], need=[("VE", v_neg)])
            gp(lambda e: e.collective_compute("AllReduce", ALU.max, ins=[arin_d[5]],
                                              outs=[arout_d[5]], replica_groups=RG_ALL),
               need=[d_w0])

            if stop_after == "startup": raise _StopEmit
            st_prev_A = {0: None, 1: None}
            prev_apply = {0: 0, 1: 0}
            prev_bc_se = 0
            last_tq = 0
            ap_i = 0
            for tt in range(8):
                par = tt % 2
                base3 = par * 4096
                nd = [("VE", band_v)] if tt == 0 else [("VE", prev_apply[par])]
                ld_batch = dma("sync", A3[:, base3:base3 + 4096].rearrange("p (cc t) -> p cc t", t=512),
                               xbT.rearrange("(cc p) t -> p cc t", p=128)[:, :, 512 * tt:512 * (tt + 1)],
                               need=nd)
                tiles = [A3[:, base3 + cc * 512:base3 + cc * 512 + 512] for cc in range(8)]
                t_s = t_q = 0
                for cc in range(8):
                    buf = tiles[cc]
                    t_s = te(lambda e, buf=buf, cc=cc: e.matmul(
                        PB[5][0:1, :], ONES[:, 0:1], buf, start=(cc == 0), stop=(cc == 7)),
                        need=[ld_batch] + ([("SE", prev_bc_se)] if cc == 0 else []))
                    s_q = se(lambda e, buf=buf: e.activation(
                        out=A1[:, 0:512], in_=buf, func=AF.Square),
                        need=[ld_batch, ("TE", t_q if cc > 0 else last_tq)])
                    t_q = te(lambda e, cc=cc: e.matmul(
                        PB[6][0:1, :], ONES[:, 0:1], A1[:, 0:512],
                        start=(cc == 0), stop=(cc == 7)), need=[("SE", s_q)])
                last_tq = t_q
                s_e1 = se(lambda e: e.copy(RW[0:1, 0:512], PB[5][0:1, :]), need=[("TE", t_q)])
                s_e2 = se(lambda e: e.copy(RW[0:1, 512:1024], PB[6][0:1, :]))
                prev_bc_se = s_e2
                t_b1 = te(lambda e: e.matmul(PB[7][:, :], ONESR, RW[0:1, 0:512],
                                             start=True, stop=True), need=[("SE", s_e2)])
                t_b2 = te(lambda e: e.matmul(PB[4][:, :], ONESR, RW[0:1, 512:1024],
                                             start=True, stop=True))
                mslc = A1[:, 512 + par * 1024:1024 + par * 1024]
                rslc = A1[:, 1024 + par * 1024:1536 + par * 1024]
                s_m = se(lambda e, mslc=mslc: e.mul(mslc, PB[7][:, :], 1.0 / 1024.0),
                         need=[("TE", t_b2), ("VE", prev_apply[par])])
                s_v2 = se(lambda e, rslc=rslc: e.mul(rslc, PB[4][:, :], 1.0 / 1024.0))
                ve(lambda e, mslc=mslc: e.tensor_mul(A1[:, 2560:3072], mslc, mslc),
                   need=[("SE", s_v2)])
                v_t2 = ve(lambda e, rslc=rslc: e.tensor_tensor(
                    out=rslc, in0=rslc, in1=A1[:, 2560:3072], op=ALU.subtract))
                s_sd = se(lambda e, rslc=rslc: e.activation(
                    out=rslc, in_=rslc, func=AF.Sqrt, bias=EPS, scale=1.0),
                    need=[("VE", v_t2)])
                v_r = ve(lambda e, rslc=rslc: e.reciprocal(rslc, rslc), need=[("SE", s_sd)])
                for cc in range(8):
                    buf = tiles[cc]
                    ob = A1[:, 3072 + (ap_i % 2) * 512:3072 + (ap_i % 2) * 512 + 512]
                    ve(lambda e, buf=buf, ob=ob, mslc=mslc: e.tensor_tensor(
                        out=ob, in0=buf, in1=mslc, op=ALU.subtract),
                       need=([("VE", v_r)] if cc == 0 else []) + [st_prev_A[ap_i % 2]],
                       pre_drain=(cc == 0))
                    v_a2 = ve(lambda e, ob=ob, rslc=rslc: e.tensor_tensor(
                        out=ob, in0=ob, in1=rslc, op=ALU.mult))
                    prev_apply[par] = v_a2
                    st_prev_A[ap_i % 2] = dma(
                        "sync", xhat_d[128 * cc:128 * (cc + 1), 512 * tt:512 * (tt + 1)],
                        ob, need=[("VE", v_a2)])
                    ap_i += 1

            if stop_after == "A": raise _StopEmit
            xhat_all = qall("sync")
            Q0, V0, AO0 = 0, 4096, 8256
            KR = RW[:, 0:4096]
            rs_cc = []
            prev_pair_se = 0
            prev_pair_te = 0
            prev_stg_refs = []
            for p in range(4):
                lw = []
                for wsrc, off in ((wq, 0), (wk, 1024), (wv, 2048)):
                    lw.append(dma("sync", WA[:, off:off + 1024].rearrange("p (cc m) -> p cc m", m=128),
                                  wsrc[p].rearrange("(cc p) m -> p cc m", p=128),
                                  need=[("TE", prev_pair_te)]))
                v_wq = ve(lambda e: e.tensor_scalar(out=WA[:, 0:1024], in0=WA[:, 0:1024],
                                                    scalar1=0.125, scalar2=None, op0=ALU.mult),
                          need=[lw[0]])
                for ch in range(32):
                    ve0(lambda e, ch=ch: e.memset(A1[:, V0 + ch * 130 + 64:V0 + ch * 130 + 65], 1.0),
                        need=[("TE", prev_pair_te)] if ch == 0 else ())
                    ve0(lambda e, ch=ch: e.memset(A1[:, V0 + ch * 130 + 129:V0 + ch * 130 + 130], 1.0))
                v_vones = ve(lambda e: e.nop(), pre_drain=True)

                prev_evac_se = {0: prev_pair_se, 1: prev_pair_se}
                for tt in range(8):
                    if p == 0 and tt == 0:
                        nd = [xhat_all]
                    else:
                        nd = [("SE", prev_evac_se[tt % 2])]
                    if tt < 2:
                        nd = nd + prev_stg_refs
                    base3 = (tt % 2) * 4096
                    pbq, pbk, pbv = (PB[0], PB[1], PB[2]) if tt % 2 == 0 else (PB[3], PB[6], PB[7])
                    ld_batch = dma(
                        "sync", A3[:, base3:base3 + 4096].rearrange("p (cc t) -> p cc t", t=512),
                        xhat_d.rearrange("(cc p) t -> p cc t", p=128)[:, :, 512 * tt:512 * (tt + 1)],
                        need=nd)
                    xt_tiles = [A3[:, base3 + cc * 512:base3 + cc * 512 + 512] for cc in range(8)]
                    q_need = [("VE", v_wq), ("SE", prev_evac_se[tt % 2])] + lw
                    for woff, pbi in ((0, pbq), (1024, pbk)):
                        for cc in range(8):
                            buf = xt_tiles[cc]
                            te(lambda e, buf=buf, woff=woff, cc=cc, pbi=pbi: e.matmul(
                                pbi[:, :], WA[:, woff + cc * 128:woff + cc * 128 + 128], buf,
                                start=(cc == 0), stop=(cc == 7)),
                               need=[ld_batch] + (q_need if cc == 0 else []))
                    for j in range(4):
                        for cc in range(8):
                            buf = xt_tiles[cc]
                            te(lambda e, buf=buf, j=j, cc=cc, pbv=pbv: e.matmul(
                                pbv[:, j * 128:(j + 1) * 128], buf[:, j * 128:(j + 1) * 128],
                                WA[:, 2048 + cc * 128:2048 + cc * 128 + 128],
                                start=(cc == 0), stop=(cc == 7)),
                               need=[ld_batch] + ([("SE", prev_evac_se[tt % 2])] if cc == 0 and j == 0 else []))
                    t_all = S.val("TE")
                    se(lambda e, tt=tt, pbq=pbq: e.copy(
                        A1[:, Q0 + 512 * tt:Q0 + 512 * (tt + 1)], pbq[:, :]),
                       need=[("TE", t_all), ("VE", v_vones)])
                    se(lambda e, tt=tt, pbk=pbk: e.copy(KR[:, 512 * tt:512 * (tt + 1)], pbk[:, :]))
                    for j in range(4):
                        ch = tt * 4 + j
                        se(lambda e, j=j, ch=ch, pbv=pbv: e.copy(
                            A1[:, V0 + ch * 130:V0 + ch * 130 + 64], pbv[:, j * 128:j * 128 + 64]))
                        se(lambda e, j=j, ch=ch, pbv=pbv: e.copy(
                            A1[:, V0 + ch * 130 + 65:V0 + ch * 130 + 129],
                            pbv[:, j * 128 + 64:j * 128 + 128]))
                    prev_evac_se[tt % 2] = S.val("SE")
                qkv_se = S.val("SE")
                if stop_after == f"qkv{p}":
                    raise _StopEmit

                t_n = 0
                s_sq_prev = 0
                for src_i, tag in enumerate(("q", "k")):
                    s_sq = se(lambda e, src_i=src_i: e.activation(
                        out=A3[:, 4096:8192],
                        in_=(A1[:, Q0:Q0 + 4096] if src_i == 0 else KR[:, 0:4096]),
                        func=AF.Square),
                        need=([("SE", qkv_se), ("VE", S.val("VE")), ("TE", S.val("TE"))]
                              if src_i == 0 else [("TE", t_n)]))
                    for hh in range(2):
                        base = 64 * hh
                        for tc in range(32):
                            col = 64 * src_i + 32 * hh + tc
                            t_n = te(lambda e, base=base, tc=tc, col=col: e.matmul(
                                PB[4][0:128, col:col + 1],
                                A3[base:base + 64, 4096 + 128 * tc:4096 + 128 * (tc + 1)],
                                ONES[base:base + 64, 0:1], start=True, stop=True),
                                need=[("SE", s_sq)] if hh == 0 and tc == 0 else [])
                    s_sq_prev = s_sq
                s_nc = se(lambda e: e.copy(A3[:, 3840:3968], PB[4][:, 0:128]),
                          need=[("TE", t_n)])
                v_red = 0
                for g4 in range(4):
                    v_red = ve(lambda e, g4=g4: e.reduce_max(
                        SM[:, SC0 + 20 + g4:SC0 + 21 + g4],
                        A3[:, 3840 + 32 * g4:3840 + 32 * (g4 + 1)], axis=AX.X),
                        need=[("SE", s_nc)] if g4 == 0 else [], pre_drain=(g4 == 0))
                t_t1 = te(lambda e: e.transpose(PB[5][0:4, 0:128], SM[:, SC0 + 20:SC0 + 24],
                                                ID_), need=[("VE", v_red)])
                s_t1 = se(lambda e: e.copy(A3[0:4, 3968:4096], PB[5][0:4, 0:128]),
                          need=[("TE", t_t1)])
                v_c = ve(lambda e: e.reduce_max(A3[0:4, 4000 - 32:4000 - 31][0:4, 0:1]
                                                if False else SM[0:4, SC0 + 24:SC0 + 25],
                                                A3[0:4, 3968:4096], axis=AX.X),
                         need=[("SE", s_t1)])
                t_t2 = te(lambda e: e.transpose(PB[5][0:1, 0:4], SM[0:4, SC0 + 24:SC0 + 25],
                                                ID_[0:4, 0:4]), need=[("VE", v_c)])
                s_t2 = se(lambda e: e.copy(RW[0:1, 4280:4284], PB[5][0:1, 0:4]),
                          need=[("TE", t_t2)])
                v_p1 = ve(lambda e: e.tensor_tensor(out=RW[0:1, 4284:4286],
                                                    in0=RW[0:1, 4280:4282],
                                                    in1=RW[0:1, 4282:4284], op=ALU.mult),
                          need=[("SE", s_t2)])
                s_sq2 = se(lambda e: e.activation(out=RW[0:1, 4286:4288], in_=RW[0:1, 4284:4286],
                                                  func=AF.Sqrt), need=[("VE", v_p1)])
                v_eb2 = ve(lambda e, p=p: e.tensor_tensor(
                    out=RW[0:1, 4288:4290], in0=PREROW[0:1, 2 * p:2 * p + 2],
                    in1=RW[0:1, 4286:4288], op=ALU.subtract), need=[("SE", s_sq2)])
                t_eb = te(lambda e: e.matmul(PB[5][:, 0:2], ONESR, RW[0:1, 4288:4290],
                                             start=True, stop=True), need=[("VE", v_eb2)])
                s_eb = se(lambda e: e.copy(SM[:, EB0:EB0 + 2], PB[5][:, 0:2]),
                          need=[("TE", t_eb)])
                eb_v = ve(lambda e: e.nop(), need=[("SE", s_eb)], pre_drain=False)
                head_ebias_v = [eb_v, eb_v]
                if stop_after == f"stab{p}":
                    raise _StopEmit

                den = [A3[0:1, 0:4096], A3[0:1, 4096:8192]]
                exp_hist = {}
                av_hist = {}
                prev_evac = S.val("SE")
                sc_banks = {0: (PB[0], PB[1], PB[4]), 1: (PB[6], PB[7], PB[5])}
                av_banks = {0: PB[2], 1: PB[3]}
                EXP0 = 12352
                LOOK = 2
                den_rd = 0
                for tt in range(8):
                    n_sc = 4 * (tt + 1)
                    last_av = {0: 0, 1: 0}

                    def emit_score(sc, hh, tt=tt):
                        so = 128 * sc
                        t_start = max(512 * tt, so)
                        N = 512 * (tt + 1) - t_start
                        bi = sc % 3
                        off = t_start - 512 * tt
                        base = 64 * hh
                        hl = 2 * p + hh
                        band = SM[:, BANDS + hl * 144:BANDS + (hl + 1) * 144]
                        pb_s = sc_banks[hh][bi]
                        t_sc = te(lambda e, base=base, so=so, t_start=t_start, N=N,
                                  pb_s=pb_s: e.matmul(
                            pb_s[:, 0:N], KR[base:base + 64, so:so + 128],
                            A1[base:base + 64, Q0 + t_start:Q0 + t_start + N],
                            start=True, stop=True),
                            need=[("SE", max(exp_hist.get((hh, bi), 0), prev_evac))])
                        v_band = 0
                        if so >= 512 * tt:
                            w = min(144, N)
                            v_band = ve(lambda e, pb_s=pb_s, w=w, band=band: e.tensor_tensor(
                                out=pb_s[:, 0:w], in0=pb_s[:, 0:w], in1=band[:, 0:w],
                                op=ALU.add), need=[("TE", t_sc)], pre_drain=False)
                        elif so == 512 * tt - 128:
                            v_band = ve(lambda e, pb_s=pb_s, band=band: e.tensor_tensor(
                                out=pb_s[:, 0:16], in0=pb_s[:, 0:16], in1=band[:, 128:144],
                                op=ALU.add), need=[("TE", t_sc)], pre_drain=False)
                        ebuf = A1[:, EXP0 + (3 * hh + bi) * 512:EXP0 + (3 * hh + bi) * 512 + 512]
                        v_z = 0
                        if off > 0:
                            v_z = ve(lambda e, ebuf=ebuf, off=off: e.memset(ebuf[:, 0:off], 0.0),
                                     need=[("TE", av_hist.get((hh, bi), 0))], pre_drain=False)
                        nd = [("TE", max(t_sc, av_hist.get((hh, bi), 0))),
                              ("VE", max(v_band, v_z, head_ebias_v[hh]))]
                        s_e = se(lambda e, ebuf=ebuf, off=off, N=N, pb_s=pb_s, hh=hh:
                                 e.activation(out=ebuf[:, off:off + N], in_=pb_s[:, 0:N],
                                              func=AF.Exp,
                                              bias=SM[:, EB0 + hh:EB0 + hh + 1],
                                              scale=1.0), need=nd)
                        exp_hist[(hh, bi)] = s_e
                        return (sc, hh, bi, ebuf, s_e)

                    def emit_av(tok, tt=tt, n_sc=n_sc):
                        sc, hh, bi, ebuf, s_e = tok
                        vch = V0 + sc * 130
                        lhs_off = vch if hh == 0 else vch + 65
                        t_av = te(lambda e, lhs_off=lhs_off, hh=hh, ebuf=ebuf, sc=sc,
                                  n_sc=n_sc: e.matmul(
                            av_banks[hh][0:65, :], A1[:, lhs_off:lhs_off + 65], ebuf,
                            start=(sc == 0), stop=(sc == n_sc - 1)),
                            need=[("SE", s_e)])
                        av_hist[(hh, bi)] = t_av
                        last_av[hh] = t_av

                    pend = []
                    for sc in range(n_sc):
                        for hh in range(2):
                            pend.append(emit_score(sc, hh))
                        if sc >= LOOK:
                            emit_av(pend.pop(0))
                            emit_av(pend.pop(0))
                    while pend:
                        emit_av(pend.pop(0))
                    for hh in range(2):
                        base = 64 * hh
                        se(lambda e, hh=hh, tt=tt, base=base: e.copy(
                            A1[base:base + 64, AO0 + 512 * tt:AO0 + 512 * (tt + 1)],
                            av_banks[hh][0:64, :]), need=[("TE", last_av[hh])])
                        s_dn = se(lambda e, hh=hh, tt=tt: e.copy(
                            den[hh][0:1, 512 * tt:512 * (tt + 1)], av_banks[hh][64:65, :]))
                        den_rd = ve(lambda e, hh=hh, tt=tt: e.reciprocal(
                            den[hh][0:1, 512 * tt:512 * (tt + 1)],
                            den[hh][0:1, 512 * tt:512 * (tt + 1)]),
                            need=[("SE", s_dn)])
                    prev_evac = S.val("SE")
                if stop_after == f"sc{p}":
                    raise _StopEmit

                RCP0 = EXP0 + 3072
                s_rb_prev = 0
                for tt in range(8):
                    t_b0 = te(lambda e, tt=tt: e.matmul(PB[4][0:64, :], ONESR[0:1, 0:64],
                                                        den[0][0:1, 512 * tt:512 * (tt + 1)],
                                                        start=True, stop=True),
                              need=[("VE", den_rd)] + ([("SE", s_rb_prev)] if s_rb_prev else []))
                    t_b1 = te(lambda e, tt=tt: e.matmul(PB[4][64:128, :], ONESR[0:1, 0:64],
                                                        den[1][0:1, 512 * tt:512 * (tt + 1)],
                                                        start=True, stop=True))
                    s_rb = se(lambda e: e.copy(A1[:, RCP0:RCP0 + 512], PB[4][:, :]),
                              need=[("TE", t_b1), ("VE", S.val("VE"))])
                    s_rb_prev = s_rb
                    ve0(lambda e, tt=tt: e.tensor_tensor(
                        out=A1[0:64, AO0 + 512 * tt:AO0 + 512 * (tt + 1)],
                        in0=A1[0:64, AO0 + 512 * tt:AO0 + 512 * (tt + 1)],
                        in1=A1[0:64, RCP0:RCP0 + 512], op=ALU.mult), need=[("SE", s_rb)])
                    v_n1 = ve(lambda e, tt=tt: e.tensor_tensor(
                        out=A1[64:128, AO0 + 512 * tt:AO0 + 512 * (tt + 1)],
                        in0=A1[64:128, AO0 + 512 * tt:AO0 + 512 * (tt + 1)],
                        in1=A1[64:128, RCP0:RCP0 + 512], op=ALU.mult), pre_drain=False)
                v_norm = S.val("VE")
                st_stg = {0: None, 1: None}
                stg_refs = []
                for r in range(2):
                    for blkb in range(2):
                        stg = A3[:, blkb * 2048:blkb * 2048 + 2048]
                        v_m = ve(lambda e, stg=stg, r=r, blkb=blkb: e.tensor_scalar(
                            out=stg, in0=A1[:, AO0 + 2048 * r:AO0 + 2048 * r + 2048],
                            scalar1=GMB[:, blkb:blkb + 1], scalar2=None, op0=ALU.mult),
                            need=[("VE", v_norm), st_stg[blkb]])
                        st_stg[blkb] = dma("sync", rsin[p][r, 128 * blkb:128 * (blkb + 1), :],
                                           stg, need=[("VE", v_m)])
                        stg_refs.append(st_stg[blkb])
                rs_cc.append(gp(lambda e, p=p: e.collective_compute(
                    "ReduceScatter", ALU.add, ins=[rsin[p]], outs=[rsout[p]],
                    replica_groups=RG_PAIR), need=list(stg_refs)))
                prev_pair_se = S.val("SE")
                prev_pair_te = S.val("TE")
                prev_stg_refs = list(stg_refs)
                if stop_after == f"pair{p}":
                    raise _StopEmit

            if stop_after == "BC": raise _StopEmit
            ld_wp = dma("sync", A1[:, 8192:16384].rearrange("p (jc cc m) -> p jc cc m", cc=8, m=128),
                        wp.rearrange("(jc p) (cc m) -> p jc cc m", p=128, m=128),
                        need=[("VE", S.val("VE"))])

            def ln_stats(src_arena, need0_se=0, need0_v=0):
                t_last = 0
                for cc in range(8):
                    nd = [("SE", need0_se), ("TE", t_last)]
                    if cc == 0 and need0_v:
                        nd.append(("VE", need0_v))
                    s_sq = se(lambda e, cc=cc: e.activation(
                        out=A3[:, 0:2048], in_=src_arena[:, cc * TL:(cc + 1) * TL],
                        func=AF.Square), need=nd)
                    for tc in range(16):
                        te(lambda e, cc=cc, tc=tc: e.matmul(
                            PB[5][:, 16 * cc + tc:16 * cc + tc + 1],
                            src_arena[:, cc * TL + 128 * tc:cc * TL + 128 * (tc + 1)],
                            ONES[:, 0:1], start=True, stop=True),
                           need=[("SE", s_sq)] if tc == 0 and cc == 0 else [])
                        te(lambda e, tc=tc, cc=cc: e.matmul(
                            PB[5][:, 256 + 16 * cc + tc:256 + 16 * cc + tc + 1],
                            A3[:, 128 * tc:128 * (tc + 1)],
                            ONES[:, 0:1], start=True, stop=True),
                           need=[("SE", s_sq)] if tc == 0 else [])
                    t_last = S.val("TE")
                s_ev = se(lambda e: e.copy(RW[:, 3072:3328] if False else RW[:, 3392:3904],
                                           PB[5][:, 0:512]), need=[("TE", t_last)])
                v_f = 0
                for half, base0 in ((64, 3392), (64, 3648)):
                    b0 = base0
                    for wdt in (64, 32, 16):
                        v_f = ve(lambda e, b0=b0, wdt=wdt: e.tensor_tensor(
                            out=RW[:, b0:b0 + wdt], in0=RW[:, b0:b0 + wdt],
                            in1=RW[:, b0 + wdt:b0 + 2 * wdt], op=ALU.add),
                            need=[("SE", s_ev)] if wdt == 64 else [])
                ve(lambda e: e.tensor_scalar(out=RW[:, 3072:3088], in0=RW[:, 3392:3408],
                                             scalar1=1.0 / 1024.0, scalar2=None,
                                             op0=ALU.mult))
                ve(lambda e: e.tensor_scalar(out=RW[:, 3088:3104], in0=RW[:, 3648:3664],
                                             scalar1=1.0 / 1024.0, scalar2=None,
                                             op0=ALU.mult), pre_drain=False)
                ve(lambda e: e.tensor_mul(RW[:, 3104:3120], RW[:, 3072:3088],
                                          RW[:, 3072:3088]))
                v_var = ve(lambda e: e.tensor_tensor(out=RW[:, 3088:3104],
                                                     in0=RW[:, 3088:3104],
                                                     in1=RW[:, 3104:3120],
                                                     op=ALU.subtract))
                s_sd = se(lambda e: e.activation(out=RW[:, 3088:3104], in_=RW[:, 3088:3104],
                                                 func=AF.Sqrt, bias=EPS, scale=1.0),
                          need=[("VE", v_var)])
                v_rc = ve(lambda e: e.reciprocal(RW[:, 3088:3104], RW[:, 3088:3104]),
                          need=[("SE", s_sd)])
                t_t1 = te(lambda e: e.transpose(PB[5][0:16, 0:128], RW[:, 3072:3088], ID_),
                          need=[("VE", v_rc)])
                t_t2 = te(lambda e: e.transpose(PB[6][0:16, 0:128], RW[:, 3088:3104], ID_))
                s_t1 = se(lambda e: e.copy(RW[0:16, 3136:3264], PB[5][0:16, 0:128]),
                          need=[("TE", t_t2)])
                s_t2 = se(lambda e: e.copy(RW[0:16, 3264:3392], PB[6][0:16, 0:128]))
                d_m = dma("sync", A3[0:1, 0:2048], RW[0:16, 3136:3264],
                          need=[("SE", s_t2)])
                d_r = dma("sync", A3[0:1, 2048:4096], RW[0:16, 3264:3392])
                s_bc_prev = 0
                last_v = 0
                for tt4 in range(4):
                    t_b1 = te(lambda e, tt4=tt4: e.matmul(
                        PB[5][:, :], ONESR, A3[0:1, 512 * tt4:512 * (tt4 + 1)],
                        start=True, stop=True),
                        need=[d_m, d_r] + ([("SE", s_bc_prev)] if s_bc_prev else [("SE", s_t2)]))
                    t_b2 = te(lambda e, tt4=tt4: e.matmul(
                        PB[6][:, :], ONESR, A3[0:1, 2048 + 512 * tt4:2048 + 512 * (tt4 + 1)],
                        start=True, stop=True))
                    s_m = se(lambda e, tt4=tt4: e.copy(
                        A3[:, 4096 + 512 * tt4:4096 + 512 * (tt4 + 1)], PB[5][:, :]),
                        need=[("TE", t_b2)])
                    s_bc_prev = se(lambda e, tt4=tt4: e.copy(
                        A3[:, 6144 + 512 * tt4:6144 + 512 * (tt4 + 1)], PB[6][:, :]))
                last_v = ve(lambda e: e.nop(), need=[("SE", s_bc_prev)], pre_drain=False)
                return last_v

            rs_all = rs_cc[-1]
            s_prev = 0
            for tt4 in range(4):
                ld_refs = []
                for p4 in range(4):
                    nd = [("CC", rs_all)] if tt4 == 0 and p4 == 0 else \
                         ([("SE", s_prev)] if p4 == 0 else [])
                    ld_refs.append(dma(
                        "sync",
                        A3[:, 0:4096].rearrange("p (blk j t) -> p blk j t",
                                                blk=2, j=4, t=512)[:, :, p4, :],
                        rsout[p4].rearrange("(blk p) t -> p blk t", p=128)[:, :, 512 * tt4:512 * (tt4 + 1)],
                        need=nd))
                tls = [A3[:, j * 512:j * 512 + 512] for j in range(8)]
                for cc in range(8):
                    t_m = 0
                    for j in range(8):
                        buf = tls[j]
                        t_m = te(lambda e, buf=buf, j=j, cc=cc: e.matmul(
                            PB[4][:, :], A1[:, 8192 + (j * 8 + cc) * 128:8192 + (j * 8 + cc) * 128 + 128],
                            buf, start=(j == 0), stop=(j == 7)),
                            need=[ld_wp] + ld_refs + ([("SE", s_prev)] if j == 0 else []))
                    s_prev = se(lambda e, cc=cc, tt4=tt4: e.add(
                        A2[:, cc * TL + 512 * tt4:cc * TL + 512 * tt4 + 512], PB[4][:, :],
                        SM[:, BP0 + cc:BP0 + cc + 1]), need=[("TE", t_m)])

            ln3_v = ln_stats(A2, need0_se=s_prev)
            x1_v = 0
            ld_prev_v = ln3_v
            for cc in range(8):
                xb = A3[:, 1024:3072]
                ld = dma("sync", xb, xtokT[128 * cc:128 * (cc + 1), :],
                         need=[("VE", ld_prev_v)])
                ve(lambda e, cc=cc: e.tensor_tensor(
                    out=A1[:, cc * TL:(cc + 1) * TL], in0=A2[:, cc * TL:(cc + 1) * TL],
                    in1=A3[:, 4096:6144], op=ALU.subtract),
                   need=[("VE", ln3_v), ld], pre_drain=(cc == 0))
                ve(lambda e, cc=cc: e.tensor_tensor(
                    out=A1[:, cc * TL:(cc + 1) * TL], in0=A1[:, cc * TL:(cc + 1) * TL],
                    in1=A3[:, 6144:8192], op=ALU.mult))
                x1_v = ve(lambda e, cc=cc, xb=xb: e.tensor_tensor(
                    out=A1[:, cc * TL:(cc + 1) * TL], in0=A1[:, cc * TL:(cc + 1) * TL],
                    in1=xb, op=ALU.add))
                ld_prev_v = x1_v

            ln2_v = ln_stats(A1, need0_v=x1_v)
            v_mx = ve(lambda e: e.memset(SM[:, WMX:WMX + 1], 0.0), pre_drain=False)
            xln_v = 0
            for cc in range(8):
                ve(lambda e, cc=cc: e.tensor_tensor(
                    out=A2[:, cc * TL:(cc + 1) * TL], in0=A1[:, cc * TL:(cc + 1) * TL],
                    in1=A3[:, 4096:6144], op=ALU.subtract),
                   need=[("VE", ln2_v)] if cc == 0 else (), pre_drain=(cc == 0))
                ve(lambda e, cc=cc: e.tensor_tensor(
                    out=A2[:, cc * TL:(cc + 1) * TL], in0=A2[:, cc * TL:(cc + 1) * TL],
                    in1=A3[:, 6144:8192], op=ALU.mult))
                ve0(lambda e, cc=cc: e.reduce_max(SM[:, WMX + 8:WMX + 9],
                                                  A2[:, cc * TL:(cc + 1) * TL],
                                                  axis=AX.X, apply_absolute_value=True),
                    pre_drain=True)
                xln_v = ve(lambda e: e.tensor_max(SM[:, WMX:WMX + 1], SM[:, WMX:WMX + 1],
                                                  SM[:, WMX + 8:WMX + 9]))

            ar_state = {"s_b": 0}

            def allreduce_scale(round_i, colmax_col, qs_off, extra_pre_scale=None, need_v=0):
                t_t = te(lambda e, colmax_col=colmax_col: e.transpose(
                    PB[7][0:1, 0:128], SM[:, colmax_col:colmax_col + 1], ID_),
                    need=[("VE", need_v), ("SE", ar_state["s_b"])])
                s_t = se(lambda e: e.copy(RW[0:1, 4096:4224], PB[7][0:1, 0:128]),
                         need=[("TE", t_t)])
                d0 = dma("sync", arin_d[round_i], RW[0:1, 4096:4224], need=[("SE", s_t)])
                cc_i = gp(lambda e, round_i=round_i: e.collective_compute(
                    "AllReduce", ALU.max, ins=[arin_d[round_i]], outs=[arout_d[round_i]],
                    replica_groups=RG_ALL), need=[d0])
                d1 = dma("sync", RW[0:1, 4096:4224], arout_d[round_i], need=[("CC", cc_i)])
                v1 = ve(lambda e: e.reduce_max(RW[0:1, 4270:4271], RW[0:1, 4096:4224], axis=AX.X),
                        need=[d1])
                if extra_pre_scale is not None:
                    v1 = ve(lambda e, x=extra_pre_scale: e.tensor_scalar(
                        out=RW[0:1, 4270:4271], in0=RW[0:1, 4270:4271], scalar1=x,
                        scalar2=None, op0=ALU.mult))
                v2 = ve(lambda e: e.tensor_scalar(out=RW[0:1, 4271:4272], in0=RW[0:1, 4270:4271],
                                                  scalar1=1.0 / 32767.0, scalar2=1e-12,
                                                  op0=ALU.mult, op1=ALU.max))
                v3 = ve(lambda e: e.reciprocal(RW[0:1, 4272:4273], RW[0:1, 4271:4272]))
                t_b = te(lambda e: e.matmul(PB[7][:, 0:2], ONESR, RW[0:1, 4271:4273],
                                            start=True, stop=True),
                         need=[("VE", v3), ("SE", s_t)])
                s_b = se(lambda e, qs_off=qs_off: e.copy(SM[:, QS0 + qs_off:QS0 + qs_off + 2],
                                                         PB[7][:, 0:2]), need=[("TE", t_b)])
                ar_state["s_b"] = s_b
                return ve(lambda e: e.nop(), need=[("SE", s_b)], pre_drain=False)

            ar0_v = allreduce_scale(0, WMX, 0, need_v=xln_v)

            xq_v = 0
            for cc in range(8):
                ve(lambda e, cc=cc: e.tensor_scalar(
                    out=A2[:, cc * TL:(cc + 1) * TL], in0=A2[:, cc * TL:(cc + 1) * TL],
                    scalar1=SM[:, QS0 + 1:QS0 + 2], scalar2=MAGIC, op0=ALU.mult, op1=ALU.add),
                   need=[("VE", ar0_v)] if cc == 0 else (), pre_drain=(cc == 0))
                xq_v = ve(lambda e, cc=cc: e.tensor_scalar(
                    out=A2[:, cc * TL:(cc + 1) * TL], in0=A2[:, cc * TL:(cc + 1) * TL],
                    scalar1=MAGIC, scalar2=SM[:, QS0:QS0 + 1], op0=ALU.subtract, op1=ALU.mult))

            if stop_after == "D": raise _StopEmit
            SB_ = WA[:, 6144:6176]
            SI_ = WA[:, 6176:6208]

            SIE = RW[:, 1024:2048]
            SBE = RW[:, 2048:3072]

            def dequant_1024(nat, need0, need_mults=(), out=None):
                s_abs = se(lambda e, nat=nat: e.activation(out=WA[:, 2048:3072], in_=nat,
                                                           func=AF.Abs), need=list(need0))
                w_ = 512
                nd_f = [("SE", s_abs)]
                while w_ >= 32:
                    ve(lambda e, w_=w_: e.tensor_tensor(
                        out=WA[:, 2048:2048 + w_], in0=WA[:, 2048:2048 + w_],
                        in1=WA[:, 2048 + w_:2048 + 2 * w_], op=ALU.max), need=nd_f)
                    nd_f = []
                    w_ //= 2
                ve(lambda e: e.tensor_scalar(out=SB_[:, :], in0=WA[:, 2048:2080],
                                             scalar1=1.0 / 7.0,
                                             scalar2=1e-12, op0=ALU.mult, op1=ALU.max))
                v_rc = ve(lambda e: e.reciprocal(SI_[:, :], SB_[:, :]))
                d_e1 = dma("sync", SIE[:].rearrange("p (j n) -> p j n", n=32),
                           SI_[:].rearrange("p n -> p () n").broadcast_to((128, 32, 32)),
                           need=[("VE", v_rc), ("VE", st["dq_last"])])
                d_e2 = dma("sync", SBE[:].rearrange("p (j n) -> p j n", n=32),
                           SB_[:].rearrange("p n -> p () n").broadcast_to((128, 32, 32)))
                if out is None:
                    out = WA[:, 3072:4096]
                ve(lambda e, nat=nat, out=out: e.tensor_tensor(out=out, in0=nat,
                                                               in1=SIE[:], op=ALU.mult),
                   need=[d_e1, d_e2] + list(need_mults), pre_drain=False)
                ve(lambda e, out=out: e.tensor_scalar(out=out, in0=out,
                                                      scalar1=MAGIC, scalar2=MAGIC,
                                                      op0=ALU.add, op1=ALU.subtract))
                ve(lambda e, out=out: e.tensor_tensor(out=out, in0=out,
                                                      in1=SBE[:], op=ALU.mult))
                return ve(lambda e: e.nop(), pre_drain=True)

            sp_prev = {0: None, 1: None}
            v_red_prev = {0: 0, 1: 0}
            wmax_done = {}
            st = {"dq_hist": {0: 0, 1: 0}, "dq_last": 0, "stg_mm": {0: 0, 1: 0},
                  "tr_te": 0, "stg_se": 0, "s_ev": 0, "dq": {}}

            def emit_f2_prep(wsrc, ic, first):
                par = ic % 2
                nat = WA[:, par * 1024:par * 1024 + 1024]
                ld = dma("sync", nat, wsrc[128 * ic:128 * (ic + 1), :],
                         need=[("VE", st["dq_hist"][par])] if st["dq_hist"][par] else
                              ([("VE", xq_v)] if first else []))
                dq_v = dequant_1024(nat, [ld, ("VE", st["dq_last"])],
                                    need_mults=[("TE", st["tr_te"])])
                st["dq_hist"][par] = dq_v
                st["dq_last"] = dq_v
                st["dq"][ic] = dq_v

            def emit_f2_stage(ic):
                par = ic % 2
                dq_v = st["dq"][ic]
                stg0 = 4096 + par * 1024
                s_t_hist = {}
                for cc in range(8):
                    pbt = PB[5 + (cc % 2)]
                    t_t = te(lambda e, cc=cc, pbt=pbt: e.transpose(
                        pbt[:, 0:128], WA[:, 3072 + cc * 128:3072 + cc * 128 + 128], ID_),
                        need=[("VE", dq_v),
                              ("SE", s_t_hist.get(cc - 2, st["stg_se"]))])
                    s_t = se(lambda e, cc=cc, stg0=stg0, pbt=pbt: e.copy(
                        WA[:, stg0 + cc * 128:stg0 + cc * 128 + 128], pbt[:, 0:128]),
                        need=[("TE", max(t_t, st["stg_mm"][par] if cc == 0 else 0))])
                    s_t_hist[cc] = s_t
                st["stg_se"] = S.val("SE")
                st["tr_te"] = S.val("TE")

            def emit_f2_mms(wdst, mxc, ic):
                par = ic % 2
                stg0 = 4096 + par * 1024
                s_stg = st["stg_se"]
                for tt4 in range(4):
                    t_m = 0
                    for cc in range(8):
                        t_m = te(lambda e, cc=cc, tt4=tt4, stg0=stg0: e.matmul(
                            PB[4][:, :], WA[:, stg0 + cc * 128:stg0 + cc * 128 + 128],
                            A2[:, cc * TL + 512 * tt4:cc * TL + 512 * tt4 + 512],
                            start=(cc == 0), stop=(cc == 7)),
                            need=([("SE", max(s_stg, st["s_ev"])), ("VE", xq_v)] if cc == 0 else []))
                    sp = RW[:, (tt4 % 2) * 512:(tt4 % 2) * 512 + 512]
                    s_ev = se(lambda e, sp=sp: e.copy(sp, PB[4][:, :]),
                              need=[("TE", t_m), sp_prev[tt4 % 2],
                                    ("VE", v_red_prev[tt4 % 2])])
                    st["s_ev"] = s_ev
                    ve0(lambda e, sp=sp, mxc=mxc: e.reduce_max(
                        SM[:, mxc + 8:mxc + 9], sp, axis=AX.X, apply_absolute_value=True),
                        need=[("SE", s_ev)], pre_drain=True)
                    v_red_prev[tt4 % 2] = ve(lambda e, mxc=mxc: e.tensor_max(
                        SM[:, mxc:mxc + 1], SM[:, mxc:mxc + 1], SM[:, mxc + 8:mxc + 9]))
                    sp_prev[tt4 % 2] = dma("scalar", wdst[128 * ic:128 * (ic + 1),
                                                          512 * tt4:512 * (tt4 + 1)],
                                           sp, need=[("SE", s_ev)])
                st["stg_mm"][par] = S.val("TE")

            ve(lambda e: e.memset(SM[:, WMX + 1:WMX + 2], 0.0), pre_drain=False)
            emit_f2_prep(wgate, 0, first=True)
            for ic in range(24):
                emit_f2_stage(ic)
                if ic + 1 < 24:
                    emit_f2_prep(wgate, ic + 1, first=False)
                else:
                    emit_f2_prep(wup, 0, first=False)
                emit_f2_mms(gate_d, WMX + 1, ic)
            wmax_done[0] = S.val("VE")
            gate_sp_refs = qall("scalar")
            ve(lambda e: e.memset(SM[:, WMX + 2:WMX + 3], 0.0), pre_drain=False)
            ve(lambda e: e.memset(SM[:, WMX + 3:WMX + 4], 0.0), pre_drain=False)
            v_hist3 = {0: 0, 1: 0}
            st_h3 = {0: None, 1: None}

            f3_gq = {}

            def emit_f3_head(ic):
                par = ic % 2
                gb = A3[:, par * 2048:par * 2048 + 2048]
                gq = A3[:, 4096 + par * 2048:4096 + par * 2048 + 2048]
                ld = dma("sync", gb, gate_d[128 * ic:128 * (ic + 1), :],
                         need=gate_sp_refs if ic == 0 else [("VE", v_hist3[par])])
                ve(lambda e, gb=gb, gq=gq: e.tensor_scalar(
                    out=gq, in0=gb, scalar1=SM[:, QS0 + 3:QS0 + 4], scalar2=MAGIC,
                    op0=ALU.mult, op1=ALU.add),
                   need=[ld, ("VE", ar1_v), st_h3[par]], pre_drain=False)
                f3_gq[ic] = ve(lambda e, gq=gq: e.tensor_scalar(
                    out=gq, in0=gq, scalar1=MAGIC, scalar2=SM[:, QS0 + 2:QS0 + 3],
                    op0=ALU.subtract, op1=ALU.mult))

            def emit_f3_ic(ic):
                par = ic % 2
                gb = A3[:, par * 2048:par * 2048 + 2048]
                gq = A3[:, 4096 + par * 2048:4096 + par * 2048 + 2048]
                sg = gb
                v_gq = f3_gq[ic]
                s_sig = se(lambda e, gq=gq, sg=sg: e.activation(out=sg, in_=gq,
                                                                func=AF.Sigmoid),
                           need=[("VE", v_gq)])
                ve(lambda e, sg=sg: e.tensor_scalar(
                    out=sg, in0=sg, scalar1=65536.0, scalar2=MAGIC, op0=ALU.mult,
                    op1=ALU.add), need=[("SE", s_sig)], pre_drain=False)
                ve(lambda e, sg=sg: e.tensor_scalar(
                    out=sg, in0=sg, scalar1=MAGIC, scalar2=65535.0, op0=ALU.subtract,
                    op1=ALU.min))
                ve(lambda e, gq=gq, sg=sg: e.tensor_tensor(out=gq, in0=gq, in1=sg,
                                                           op=ALU.mult))
                ve0(lambda e, gq=gq: e.reduce_max(SM[:, WMX + 9:WMX + 10], gq, axis=AX.X,
                                                  apply_absolute_value=True), pre_drain=True)
                v_m3 = ve(lambda e: e.tensor_max(SM[:, WMX + 3:WMX + 4],
                                                 SM[:, WMX + 3:WMX + 4],
                                                 SM[:, WMX + 9:WMX + 10]))
                v_hist3[par] = v_m3
                st_h3[par] = dma("gpsimd", g_d[128 * ic:128 * (ic + 1), :], gq,
                                 need=[("VE", v_m3)])

            ar1_v = None
            for ic in range(24):
                emit_f2_stage(ic)
                if ic + 1 < 24:
                    emit_f2_prep(wup, ic + 1, first=False)
                if ic == 0:
                    ar1_v = allreduce_scale(1, WMX + 1, 2, need_v=wmax_done[0])
                emit_f3_head(ic)
                emit_f2_mms(up_d, WMX + 2, ic)
                emit_f3_ic(ic)
            wmax_done[1] = S.val("VE")
            ar2_v = allreduce_scale(2, WMX + 2, 4, need_v=wmax_done[1])
            ar3_v = allreduce_scale(3, WMX + 3, 6, extra_pre_scale=1.0 / 65536.0,
                                    need_v=S.val("VE"))
            v_adj = ve(lambda e: e.tensor_scalar(out=SM[:, QS0 + 7:QS0 + 8],
                                                 in0=SM[:, QS0 + 7:QS0 + 8],
                                                 scalar1=1.0 / 65536.0, scalar2=None,
                                                 op0=ALU.mult), need=[("VE", ar3_v)])

            if stop_after == "F2": raise _StopEmit
            if stop_after == "F3": raise _StopEmit
            st5 = {"nat": {0: 0, 1: 0}, "ote": {0: 0, 1: 0, 2: 0}, "mm": None,
                   "dq": {}, "acc_se": 0, "acc_v": 0, "pb56": None}
            WDQ2 = 6144

            def emit_f5_prep_h(hg, part, hh):
                hc = 2 * hg + hh
                nat = WA[:, hh * 1024:hh * 1024 + 1024]
                ld = dma("sync", nat,
                         wdown[128 * hc:128 * (hc + 1), 1024 * part:1024 * (part + 1)],
                         need=[("VE", st5["nat"][hh] if st5["nat"][hh]
                                else st["dq_last"])])
                dq_v = dequant_1024(
                    nat, [ld, ("VE", st["dq_last"])],
                    need_mults=[("TE", st["tr_te"])],
                    out=(WA[:, 3072:4096] if hh == 0 else A3[:, WDQ2:WDQ2 + 1024]))
                st5["nat"][hh] = dq_v
                st["dq_last"] = dq_v
                st5["dq"][hh] = dq_v

            def emit_f5_stage(part):
                mm_g = st5["mm"] if st5["mm"] is not None else S.val("TE")
                s_t_hist = {}
                for u in range(16):
                    hh, q8 = u // 8, u % 8
                    srcb = (WA[:, 3072 + q8 * 128:3072 + q8 * 128 + 128] if hh == 0
                            else A3[:, WDQ2 + q8 * 128:WDQ2 + q8 * 128 + 128])
                    pbt = PB[5 + (u % 2)]
                    nd = [("VE", st5["dq"][hh])] if q8 == 0 else []
                    if u < 2:
                        nd.append(st5["pb56"] if st5["pb56"] is not None
                                  else ("SE", st["stg_se"]))
                    else:
                        nd.append(("SE", s_t_hist[u - 2]))
                    t_t = te(lambda e, pbt=pbt, srcb=srcb: e.transpose(
                        pbt[:, 0:128], srcb, ID_), need=nd)
                    s_t = se(lambda e, u=u, hh=hh, q8=q8, pbt=pbt: e.copy(
                        WA[:, 4096 + hh * 1024 + q8 * 128:
                              4096 + hh * 1024 + q8 * 128 + 128], pbt[:, 0:128]),
                        need=[("TE", max(t_t, mm_g if u == 0 else 0))])
                    s_t_hist[u] = s_t
                st["stg_se"] = S.val("SE")
                st["tr_te"] = S.val("TE")

            def emit_f5_mms(hg, part):
                for q8 in range(8):
                    ici = part * 8 + q8
                    slot = ici % 3
                    obuf = A3[:, slot * 2048:slot * 2048 + 2048]
                    nd_ld = ([("TE", st5["ote"][slot])] if st5["ote"][slot]
                             else qall("scalar") + qall("gpsimd"))
                    ld_o = dma("sync", obuf, opre_d[128 * ici:128 * (ici + 1), :],
                               need=nd_ld)
                    ve(lambda e, obuf=obuf: e.tensor_scalar(
                        out=obuf, in0=obuf, scalar1=SM[:, QS0 + 9:QS0 + 10],
                        scalar2=MAGIC, op0=ALU.mult, op1=ALU.add),
                       need=[ld_o, ("VE", ar4_v)], pre_drain=False)
                    v_q = ve(lambda e, obuf=obuf: e.tensor_scalar(
                        out=obuf, in0=obuf, scalar1=MAGIC,
                        scalar2=SM[:, QS0 + 8:QS0 + 9],
                        op0=ALU.subtract, op1=ALU.mult))
                    for hh in range(2):
                        for tt4 in range(4):
                            nd = [("VE", v_q)] if hh == 0 and tt4 == 0 else []
                            if q8 == 0 and hh == 0 and tt4 == 0:
                                nd += [("SE", st["stg_se"]), ("SE", st5["acc_se"]),
                                       ("VE", st5["acc_v"])]
                            t_m = te(lambda e, obuf=obuf, hh=hh, q8=q8, tt4=tt4:
                                     e.matmul(
                                PB[hh * 4 + tt4][:, :],
                                WA[:, 4096 + hh * 1024 + q8 * 128:
                                      4096 + hh * 1024 + q8 * 128 + 128],
                                obuf[:, 512 * tt4:512 * (tt4 + 1)],
                                start=(q8 == 0), stop=(q8 == 7)), need=nd)
                    st5["ote"][slot] = t_m
                    if q8 == 2:
                        emit_f5_prep_next(hg, part, 0)
                    elif q8 == 5:
                        emit_f5_prep_next(hg, part, 1)
                st5["mm"] = S.val("TE")

            def emit_f5_prep_next(hg, part, hh):
                if part < 2:
                    emit_f5_prep_h(hg, part + 1, hh)
                elif hg < 3:
                    emit_f5_prep_h(hg + 1, 0, hh)

            F5_BANKS = [(1, 1), (1, 2), (1, 0), (1, 3), (0, 0), (0, 1), (0, 2), (0, 3)]

            def emit_f5_acc(hg, part):
                first = True
                v56 = 0
                for hh, tt4 in F5_BANKS:
                    hc = 2 * hg + hh
                    dst = A2[:, hc * TL + 512 * tt4:hc * TL + 512 * tt4 + 512]
                    if part == 0:
                        r = st5["acc_se"] = se(
                            lambda e, dst=dst, hh=hh, tt4=tt4: e.copy(
                                dst, PB[hh * 4 + tt4][:, :]),
                            need=[("TE", st5["mm"])] if first else [])
                    else:
                        r = st5["acc_v"] = ve(
                            lambda e, dst=dst, hh=hh, tt4=tt4: e.tensor_tensor(
                                out=dst, in0=PB[hh * 4 + tt4][:, :], in1=dst,
                                op=ALU.add),
                            need=[("TE", st5["mm"])] if first else [],
                            pre_drain=False)
                    if (hh, tt4) == (1, 2):
                        v56 = r
                    first = False
                st5["pb56"] = (("SE" if part == 0 else "VE"), v56)

            ve(lambda e: e.memset(SM[:, WMX + 4:WMX + 5], 0.0), pre_drain=False)
            v_hist4 = {0: 0, 1: 0}
            gp_hist4 = {0: 0, 1: 0}
            st_h4 = {0: None, 1: None}
            for ic in range(24):
                par = ic % 2
                if ic == 22:
                    emit_f5_prep_h(0, 0, 0)
                Gb = A3[:, par * 2048:par * 2048 + 2048]
                ub = A3[:, 4096 + par * 2048:4096 + par * 2048 + 2048]
                ld1 = dma("sync", Gb, g_d[128 * ic:128 * (ic + 1), :],
                          need=(qall("scalar") + qall("gpsimd") if ic == 0 else
                                [("VE", v_hist4[par]), ("CC", gp_hist4[par])]) +
                               [st_h4[par]])
                ld2 = dma("sync", ub, up_d[128 * ic:128 * (ic + 1), :])
                ve(lambda e, Gb=Gb: e.tensor_scalar(
                    out=Gb, in0=Gb, scalar1=SM[:, QS0 + 7:QS0 + 8], scalar2=MAGIC,
                    op0=ALU.mult, op1=ALU.add),
                   need=[ld1, ld2, ("VE", max(v_adj, ar2_v))], pre_drain=False)
                ve(lambda e, Gb=Gb: e.tensor_scalar(
                    out=Gb, in0=Gb, scalar1=MAGIC, scalar2=SM[:, QS0 + 6:QS0 + 7],
                    op0=ALU.subtract, op1=ALU.mult))
                gp(lambda e, ub=ub: e.tensor_scalar(
                    out=ub, in0=ub, scalar1=SM[:, QS0 + 5:QS0 + 6], scalar2=MAGIC,
                    op0=ALU.mult, op1=ALU.add), need=[ld2, ("VE", ar2_v)])
                gp_q = gp(lambda e, ub=ub: (e.drain(), e.tensor_scalar(
                    out=ub, in0=ub, scalar1=MAGIC, scalar2=SM[:, QS0 + 4:QS0 + 5],
                    op0=ALU.subtract, op1=ALU.mult))[-1])
                gp_hist4[par] = gp_q
                ve(lambda e, Gb=Gb, ub=ub: e.tensor_tensor(out=Gb, in0=Gb, in1=ub, op=ALU.mult),
                   need=[("CC", gp_q)])
                ve0(lambda e, Gb=Gb: e.reduce_max(SM[:, WMX + 10:WMX + 11], Gb, axis=AX.X,
                                                  apply_absolute_value=True), pre_drain=True)
                v_m4 = ve(lambda e: e.tensor_max(SM[:, WMX + 4:WMX + 5], SM[:, WMX + 4:WMX + 5],
                                                 SM[:, WMX + 10:WMX + 11]))
                v_hist4[par] = v_m4
                st_h4[par] = dma("scalar", opre_d[128 * ic:128 * (ic + 1), :], Gb,
                                 need=[("VE", v_m4)])
            ar4_v = allreduce_scale(4, WMX + 4, 8, need_v=S.val("VE"))

            if stop_after == "F4": raise _StopEmit
            emit_f5_prep_h(0, 0, 1)
            for hg in range(4):
                for part in range(3):
                    emit_f5_stage(part)
                    emit_f5_mms(hg, part)
                    emit_f5_acc(hg, part)

            if stop_after == "F5b": raise _StopEmit
            ln4_v = ln_stats(A2, need0_se=st5["acc_se"], need0_v=st5["acc_v"])
            st_out = None
            out_v_prev = ln4_v
            for cc in range(8):
                ob = A3[:, 1024:3072]
                ve(lambda e, cc=cc, ob=ob: e.tensor_tensor(
                    out=ob, in0=A2[:, cc * TL:(cc + 1) * TL], in1=A3[:, 4096:6144],
                    op=ALU.subtract),
                   need=[("VE", out_v_prev), st_out], pre_drain=True)
                ve(lambda e, ob=ob: e.tensor_tensor(out=ob, in0=ob, in1=A3[:, 6144:8192],
                                                    op=ALU.mult))
                v_o = ve(lambda e, cc=cc, ob=ob: e.tensor_tensor(
                    out=ob, in0=ob, in1=A1[:, cc * TL:(cc + 1) * TL], op=ALU.add))
                out_v_prev = v_o
                st_out = dma("sync", out_d[128 * cc:128 * (cc + 1), :], ob,
                             need=[("VE", v_o)])


        except _StopEmit:
            truncated[0] = True
            st_out = dma("sync", out_d[0:128, 0:128], SM[:, 0:128], need=())
        tapsrc = {"xhat": xhat_d, "gate": gate_d, "up": up_d, "G": g_d, "opre": opre_d,
                  "rsout0": rsout[0], "rsout1": rsout[1],
                  "rsout2": rsout[2], "rsout3": rsout[3]}
        for nm in dbg_d:
            st_out = dma("sync", dbg_d[nm], tapsrc[nm], need=[st_out])

        final_refs = qall("sync") + qall("scalar") + qall("gpsimd")

        def fin(e, refs=tuple(final_refs)):
            waits(e, refs)
        S.ops["sync"].append(fin)

        @blk.sync
        def _(eng):
            for fn in S.ops["sync"]:
                fn(eng)

        @blk.tensor
        def _(eng):
            for fn in S.ops["tensor"]:
                fn(eng)

        @blk.vector
        def _(eng):
            for fn in S.ops["vector"]:
                fn(eng)

        @blk.scalar
        def _(eng):
            for fn in S.ops["scalar"]:
                fn(eng)

        @blk.gpsimd
        def _(eng):
            for fn in S.ops["gpsimd"]:
                fn(eng)

    return nc


PC = np.arange(1024).reshape(32, 32).T.reshape(-1).copy()
PI3 = np.concatenate([PC, PC + 1024, PC + 2048])


def make_inputs(inp, c):
    b, g = c // 2, c % 2
    x = np.asarray(inp["L0_attn_x"], np.float32)
    Wq = np.asarray(inp["L0_attn_Wq"], np.float32)[:, PC, :]
    Wk = np.asarray(inp["L0_attn_Wk"], np.float32)[:, PC, :]
    Wv = np.asarray(inp["L0_attn_Wv"], np.float32)[:, PC, :]
    Wp = np.asarray(inp["L0_attn_Wp"], np.float32)[:, PC]
    bp = np.asarray(inp["L0_attn_bp"], np.float32)[PC]
    rt = np.asarray(inp["L0_attn_rel_table"], np.float32)
    dmv = np.zeros((17, 128, 144), np.float32)
    for v in range(17):
        for i in range(128):
            if i + v < 144:
                dmv[v, i, i + v] = 1.0
    mneg = np.zeros((128, 144), np.float32)
    for i in range(128):
        mneg[i, :i] = -1.0e30
    rt16 = np.zeros((16, 33), np.float32)
    rt16[0:8] = rt.T[8 * g:8 * g + 8]
    gm = np.zeros((1, 2), np.float32)
    gm[0, g] = 1.0

    def pair(W):
        return np.stack([np.concatenate([W[8 * g + 2 * p], W[8 * g + 2 * p + 1]], axis=1)
                         for p in range(4)])
    return {
        "xbT": np.ascontiguousarray(x[b].T[PC]),
        "xtokT": np.ascontiguousarray(x[b, 2048 * g:2048 * (g + 1)].T[PC]),
        "wq": pair(Wq), "wk": pair(Wk), "wv": pair(Wv),
        "wp": np.ascontiguousarray(Wp),
        "bpc": np.ascontiguousarray(bp[:, None]),
        "rtT": rt16,
        "wgate": np.ascontiguousarray(np.asarray(inp["L0_ffn_w_gate"], np.float32)[PI3][:, PC]),
        "wup": np.ascontiguousarray(np.asarray(inp["L0_ffn_w_up"], np.float32)[PI3][:, PC]),
        "wdown": np.ascontiguousarray(np.asarray(inp["L0_ffn_w_down"], np.float32)[PC][:, PI3]),
        "ident": np.eye(128, dtype=np.float32),
        "dmasks": dmv, "maskneg": mneg, "gmask": gm,
    }


_NC_CACHE = {}


def kernel(**inputs):
    if "main" not in _NC_CACHE:
        _NC_CACHE["main"] = build_program()
    nc = _NC_CACHE["main"]
    in_maps = [make_inputs(inputs, c) for c in range(8)]
    res = run_bass_kernel_spmd(nc, in_maps, core_ids=list(range(8)))
    out = np.zeros((B, T, C), np.float32)
    for c in range(8):
        b, g = c // 2, c % 2
        out[b, 2048 * g:2048 * (g + 1), :] = np.asarray(res.results[c]["out"])[PC].T
    return out


# version 13
# speedup vs baseline: 1.0941x; 1.0941x over previous
import numpy as np
import contextlib
import concourse.bass as bass
import concourse.mybir as mybir
from concourse.bass_utils import run_bass_kernel_spmd

F32 = mybir.dt.float32
AF = mybir.ActivationFunctionType
ALU = mybir.AluOpType
AX = mybir.AxisListType
MAGIC = 12582912.0
NEG_BIG = -3.0e38

B, T, C, H, D, I = 4, 4096, 1024, 16, 64, 3072
TL = 2048
RG_PAIR = [[0, 1], [2, 3], [4, 5], [6, 7]]
RG_ALL = [[0, 1, 2, 3, 4, 5, 6, 7]]


class Sched:
    def __init__(self):
        self.ops = {k: [] for k in ("sync", "tensor", "vector", "scalar", "gpsimd")}
        self.cnt = {}

    def inc(self, name, by):
        self.cnt[name] = self.cnt.get(name, 0) + by
        return self.cnt[name]

    def val(self, name):
        return self.cnt.get(name, 0)


def build_program(debug=(), stop_after=None):
    nc = bass.Bass("TRN2", target_bir_lowering=False)

    def din(name, shape):
        return nc.dram_tensor(name, shape, F32, kind="ExternalInput").ap()

    def dint(name, shape, **kw):
        return nc.dram_tensor(name, shape, F32, kind="Internal", **kw).ap()

    xbT = din("xbT", [C, T])
    xtokT = din("xtokT", [C, TL])
    wq = din("wq", [4, C, 128])
    wk = din("wk", [4, C, 128])
    wv = din("wv", [4, C, 128])
    wp = din("wp", [C, C])
    bpc = din("bpc", [C, 1])
    rtT = din("rtT", [16, 33])
    wgate = din("wgate", [I, C])
    wup = din("wup", [I, C])
    wdown = din("wdown", [C, I])
    ident = din("ident", [128, 128])
    dmasks = din("dmasks", [17, 128, 144])
    maskneg = din("maskneg", [128, 144])
    gmask = din("gmask", [1, 2])
    out_d = nc.dram_tensor("out", [C, TL], F32, kind="ExternalOutput").ap()

    xhat_d = dint("xhat_d", [C, T])
    rsin = [dint(f"rsin{p}", [2, 256, TL]) for p in range(4)]
    rsout = [dint(f"rsout{p}", [256, TL]) for p in range(4)]
    arin_d = [dint(f"arin{k}", [1, 128]) for k in range(6)]
    arout_d = [dint(f"arout{k}", [1, 128], addr_space="Shared") for k in range(6)]
    gate_d = dint("gate_d", [I, TL])
    up_d = dint("up_d", [I, TL])
    g_d = dint("g_d", [I, TL])
    opre_d = dint("opre_d", [I, TL])
    dbg_d = {nm: nc.dram_tensor("dbg_" + nm, list(shp), F32, kind="ExternalOutput").ap()
             for nm, shp in debug}

    S = Sched()
    es = contextlib.ExitStack()
    with es:
        A1 = es.enter_context(nc.sbuf_tensor("A1", [128, 16384], F32))
        A2 = es.enter_context(nc.sbuf_tensor("A2", [128, 16384], F32))
        A3 = es.enter_context(nc.sbuf_tensor("A3", [128, 8192], F32))
        WA = es.enter_context(nc.sbuf_tensor("WA", [128, 6208], F32))
        SM = es.enter_context(nc.sbuf_tensor("SM", [128, 1600], F32))
        RW = es.enter_context(nc.sbuf_tensor("RW", [128, 4300], F32))
        PB = [es.enter_context(nc.psum_tensor(f"pb{i}", [128, 512], F32)) for i in range(8)]
        sems = {nm: es.enter_context(nc.semaphore(nm))
                for nm in ("LD0", "LD1", "LD2", "LD3", "LD4", "LD5",
                           "SP0", "SP1", "SP2", "GD0", "GD1", "TE", "VE", "SE", "CC")}
        rot = {"sync": ["LD0", "LD1", "LD2", "LD3", "LD4", "LD5"],
               "scalar": ["SP0", "SP1", "SP2"], "gpsimd": ["GD0", "GD1"]}
        rot_i = {"sync": 0, "scalar": 0, "gpsimd": 0}
        blk = es.enter_context(nc.Block())

        def waits(e, need):
            flat = {}
            for it in need:
                if it is None:
                    continue
                items = it if isinstance(it, list) else [it]
                for ent in items:
                    if ent is None:
                        continue
                    nm, v = ent
                    if v > 0:
                        flat[nm] = max(flat.get(nm, 0), v)
            for nm, v in flat.items():
                e.wait_ge(sems[nm], v)

        def dma(queue, out, in_, need=(), sem=None, slow=False):
            if sem is None:
                sem = rot[queue][rot_i[queue]]
                rot_i[queue] = (rot_i[queue] + 1) % len(rot[queue])
            prev = S.val(sem)
            tgt = S.inc(sem, 16)
            nd = tuple(need)

            def fn(e, out=out, in_=in_, nd=nd, sem=sem, slow=slow, prev=prev):
                waits(e, nd)
                if prev > 0:
                    e.wait_ge(sems[sem], prev)
                e.dma_start(out=out, in_=in_,
                            allow_slow_non_contiguous=slow).then_inc(sems[sem], 16)
            S.ops[queue].append(fn)
            return (sem, tgt)

        def qall(queue):
            return [(s, S.val(s)) for s in rot[queue] if S.val(s) > 0]

        def te(body, need=()):
            tgt = S.inc("TE", 1)
            nd = tuple(need)

            def fn(e, nd=nd, body=body):
                waits(e, nd)
                body(e).then_inc(sems["TE"], 1)
            S.ops["tensor"].append(fn)
            return tgt

        def ve(body, need=(), pre_drain=True):
            tgt = S.inc("VE", 1)
            nd = tuple(need)

            def fn(e, nd=nd, body=body, pre_drain=pre_drain):
                waits(e, nd)
                if pre_drain:
                    e.drain()
                body(e).then_inc(sems["VE"], 1)
            S.ops["vector"].append(fn)
            return tgt

        def ve0(body, need=(), pre_drain=False):
            nd = tuple(need)

            def fn(e, nd=nd, body=body, pre_drain=pre_drain):
                waits(e, nd)
                if pre_drain:
                    e.drain()
                body(e)
            S.ops["vector"].append(fn)

        def se(body, need=()):
            tgt = S.inc("SE", 1)
            nd = tuple(need)

            def fn(e, nd=nd, body=body):
                waits(e, nd)
                body(e).then_inc(sems["SE"], 1)
            S.ops["scalar"].append(fn)
            return tgt

        def gp(body, need=()):
            tgt = S.inc("CC", 1)
            nd = tuple(need)

            def fn(e, nd=nd, body=body):
                waits(e, nd)
                body(e).then_inc(sems["CC"], 1)
            S.ops["gpsimd"].append(fn)
            return tgt

        class _StopEmit(Exception):
            pass

        ID_ = SM[:, 0:128]
        BANDS = 128
        RTT = SM[:, 1280:1313]
        DMAT = SM[:, 1313:1330]
        ONES = SM[:, 1330:1331]
        BP0 = 1331
        EB0 = 1344
        DT0 = 1346
        SC0 = 1362
        EPS = SM[:, 1394:1395]
        ONESR = SM[0:1, 1440:1568]
        GMB = SM[:, 1396:1398]
        QS0 = 1400
        WMX = 1412

        truncated = [False]
        try:
            dma("sync", ID_, ident)
            dma("sync", RTT[0:16, :], rtT)
            ld_masks = dma("sync", A3[:, 0:2448].rearrange("p (v j) -> p v j", j=144),
                           dmasks.rearrange("v p j -> p v j"))
            ld_mneg = dma("sync", A3[:, 2448:2592], maskneg)
            ld_gm = dma("sync", RW[0:1, 4224:4226], gmask)
            dma("sync", SM[:, BP0:BP0 + 8], bpc.rearrange("(cc p) o -> p (cc o)", p=128), slow=True)
            ld_const = qall("sync")

            ve(lambda e: e.memset(ONES, 1.0), pre_drain=False)
            ve(lambda e: e.memset(ONESR, 1.0), pre_drain=False)
            ve(lambda e: e.memset(EPS, 1e-5), pre_drain=False)
            v_neg = ve(lambda e: e.memset(RW[0:1, 4096:4224], NEG_BIG), pre_drain=False)
            t_gm = te(lambda e: e.matmul(PB[7][:, 0:2], ONESR, RW[0:1, 4224:4226],
                                         start=True, stop=True),
                      need=[ld_const, ("VE", v_neg)])
            s_gm = se(lambda e: e.copy(GMB, PB[7][:, 0:2]), need=[("TE", t_gm)])

            v_dmat = ve(lambda e: e.tensor_scalar(
                out=DMAT[0:16, :], in0=RTT[0:16, 16::-1], scalar1=RTT[0:16, 0:1],
                scalar2=None, op0=ALU.subtract), need=[ld_const])
            v_mh = ve(lambda e: e.reduce_max(SM[0:16, SC0 + 20:SC0 + 21], RTT[0:16, 0:17],
                                             axis=AX.X))
            v_pre = ve(lambda e: e.tensor_tensor(
                out=SM[0:16, SC0 + 21:SC0 + 22], in0=RTT[0:16, 0:1],
                in1=SM[0:16, SC0 + 20:SC0 + 21], op=ALU.subtract))
            t_dt = te(lambda e: e.transpose(PB[7][0:17, 0:16], DMAT[0:16, :], ID_[0:16, 0:16]),
                      need=[("VE", v_dmat), ("SE", s_gm)])
            s_dt = se(lambda e: e.copy(SM[0:17, DT0:DT0 + 16], PB[7][0:17, 0:16]),
                      need=[("TE", t_dt)])
            t_pre = te(lambda e: e.transpose(PB[7][0:1, 0:16], SM[0:16, SC0 + 21:SC0 + 22], ID_[0:16, 0:16]),
                       need=[("VE", v_pre), ("SE", s_dt)])
            s_pre = se(lambda e: e.copy(RW[0:1, 4243:4259], PB[7][0:1, 0:16]),
                       need=[("TE", t_pre)])
            PREROW = RW[0:1, 4243:4259]

            pa, pb_ = A3[:, 2592:2736], A3[:, 2736:2880]
            s_r_prev = s_pre
            band_v = 0
            for hl in range(8):
                t_r = te(lambda e, hl=hl: e.transpose(PB[7][0:1, 0:17],
                                                      SM[0:17, DT0 + hl:DT0 + hl + 1],
                                                      ID_[0:17, 0:17]),
                         need=[("SE", s_r_prev)])
                s_r = se(lambda e: e.copy(RW[0:1, 4226:4243], PB[7][0:1, 0:17]),
                         need=[("TE", t_r)])
                s_r_prev = s_r
                t_b = te(lambda e: e.matmul(PB[6][:, 0:17], ONESR,
                                            RW[0:1, 4226:4243], start=True, stop=True),
                         need=[("SE", s_r)])
                s_b = se(lambda e: e.copy(SM[:, SC0:SC0 + 17], PB[6][:, 0:17]),
                         need=[("TE", t_b), ("VE", band_v)])
                ve(lambda e: e.tensor_copy(pa, A3[:, 2448:2592]),
                   need=[ld_const, ("SE", s_b)])
                for v_i in range(17):
                    src, dst = (pa, pb_) if v_i % 2 == 0 else (pb_, pa)
                    ve(lambda e, v_i=v_i, src=src, dst=dst: e.scalar_tensor_tensor(
                        out=dst, in0=A3[:, v_i * 144:(v_i + 1) * 144],
                        scalar=SM[:, SC0 + v_i:SC0 + v_i + 1], in1=src,
                        op0=ALU.mult, op1=ALU.add))
                band_v = ve(lambda e, hl=hl: e.tensor_copy(
                    SM[:, BANDS + hl * 144:BANDS + (hl + 1) * 144], pb_))

            d_w0 = dma("sync", arin_d[5], RW[0:1, 4096:4224], need=[("VE", v_neg)])
            gp(lambda e: e.collective_compute("AllReduce", ALU.max, ins=[arin_d[5]],
                                              outs=[arout_d[5]], replica_groups=RG_ALL),
               need=[d_w0])

            if stop_after == "startup": raise _StopEmit
            st_prev_A = {0: None, 1: None}
            prev_apply = {0: 0, 1: 0}
            prev_bc_se = 0
            last_tq = 0
            ap_i = 0
            for tt in range(8):
                par = tt % 2
                base3 = par * 4096
                nd = [("VE", band_v)] if tt == 0 else [("VE", prev_apply[par])]
                ld_batch = dma("sync", A3[:, base3:base3 + 4096].rearrange("p (cc t) -> p cc t", t=512),
                               xbT.rearrange("(cc p) t -> p cc t", p=128)[:, :, 512 * tt:512 * (tt + 1)],
                               need=nd)
                tiles = [A3[:, base3 + cc * 512:base3 + cc * 512 + 512] for cc in range(8)]
                t_s = t_q = 0
                for cc in range(8):
                    buf = tiles[cc]
                    t_s = te(lambda e, buf=buf, cc=cc: e.matmul(
                        PB[5][0:1, :], ONES[:, 0:1], buf, start=(cc == 0), stop=(cc == 7)),
                        need=[ld_batch] + ([("SE", prev_bc_se)] if cc == 0 else []))
                    s_q = se(lambda e, buf=buf: e.activation(
                        out=A1[:, 0:512], in_=buf, func=AF.Square),
                        need=[ld_batch, ("TE", t_q if cc > 0 else last_tq)])
                    t_q = te(lambda e, cc=cc: e.matmul(
                        PB[6][0:1, :], ONES[:, 0:1], A1[:, 0:512],
                        start=(cc == 0), stop=(cc == 7)), need=[("SE", s_q)])
                last_tq = t_q
                s_e1 = se(lambda e: e.copy(RW[0:1, 0:512], PB[5][0:1, :]), need=[("TE", t_q)])
                s_e2 = se(lambda e: e.copy(RW[0:1, 512:1024], PB[6][0:1, :]))
                prev_bc_se = s_e2
                t_b1 = te(lambda e: e.matmul(PB[7][:, :], ONESR, RW[0:1, 0:512],
                                             start=True, stop=True), need=[("SE", s_e2)])
                t_b2 = te(lambda e: e.matmul(PB[4][:, :], ONESR, RW[0:1, 512:1024],
                                             start=True, stop=True))
                mslc = A1[:, 512 + par * 1024:1024 + par * 1024]
                rslc = A1[:, 1024 + par * 1024:1536 + par * 1024]
                s_m = se(lambda e, mslc=mslc: e.mul(mslc, PB[7][:, :], 1.0 / 1024.0),
                         need=[("TE", t_b2), ("VE", prev_apply[par])])
                s_v2 = se(lambda e, rslc=rslc: e.mul(rslc, PB[4][:, :], 1.0 / 1024.0))
                ve(lambda e, mslc=mslc: e.tensor_mul(A1[:, 2560:3072], mslc, mslc),
                   need=[("SE", s_v2)])
                v_t2 = ve(lambda e, rslc=rslc: e.tensor_tensor(
                    out=rslc, in0=rslc, in1=A1[:, 2560:3072], op=ALU.subtract))
                s_sd = se(lambda e, rslc=rslc: e.activation(
                    out=rslc, in_=rslc, func=AF.Sqrt, bias=EPS, scale=1.0),
                    need=[("VE", v_t2)])
                v_r = ve(lambda e, rslc=rslc: e.reciprocal(rslc, rslc), need=[("SE", s_sd)])
                for cc in range(8):
                    buf = tiles[cc]
                    ob = A1[:, 3072 + (ap_i % 2) * 512:3072 + (ap_i % 2) * 512 + 512]
                    ve(lambda e, buf=buf, ob=ob, mslc=mslc: e.tensor_tensor(
                        out=ob, in0=buf, in1=mslc, op=ALU.subtract),
                       need=([("VE", v_r)] if cc == 0 else []) + [st_prev_A[ap_i % 2]],
                       pre_drain=(cc == 0))
                    v_a2 = ve(lambda e, ob=ob, rslc=rslc: e.tensor_tensor(
                        out=ob, in0=ob, in1=rslc, op=ALU.mult))
                    prev_apply[par] = v_a2
                    st_prev_A[ap_i % 2] = dma(
                        "sync", xhat_d[128 * cc:128 * (cc + 1), 512 * tt:512 * (tt + 1)],
                        ob, need=[("VE", v_a2)])
                    ap_i += 1

            if stop_after == "A": raise _StopEmit
            xhat_all = qall("sync")
            Q0, V0, AO0 = 0, 4096, 8256
            KR = RW[:, 0:4096]
            rs_cc = []
            prev_pair_se = 0
            prev_pair_te = 0
            prev_stg_refs = []
            for p in range(4):
                lw = []
                for wsrc, off in ((wq, 0), (wk, 1024), (wv, 2048)):
                    lw.append(dma("sync", WA[:, off:off + 1024].rearrange("p (cc m) -> p cc m", m=128),
                                  wsrc[p].rearrange("(cc p) m -> p cc m", p=128),
                                  need=[("TE", prev_pair_te)]))
                v_wq = ve(lambda e: e.tensor_scalar(out=WA[:, 0:1024], in0=WA[:, 0:1024],
                                                    scalar1=0.125, scalar2=None, op0=ALU.mult),
                          need=[lw[0]])
                for ch in range(32):
                    ve0(lambda e, ch=ch: e.memset(A1[:, V0 + ch * 130 + 64:V0 + ch * 130 + 65], 1.0),
                        need=[("TE", prev_pair_te)] if ch == 0 else ())
                    ve0(lambda e, ch=ch: e.memset(A1[:, V0 + ch * 130 + 129:V0 + ch * 130 + 130], 1.0))
                v_vones = ve(lambda e: e.nop(), pre_drain=True)

                prev_evac_se = {0: prev_pair_se, 1: prev_pair_se}
                for tt in range(8):
                    if p == 0 and tt == 0:
                        nd = [xhat_all]
                    else:
                        nd = [("SE", prev_evac_se[tt % 2])]
                    if tt < 2:
                        nd = nd + prev_stg_refs
                    base3 = (tt % 2) * 4096
                    pbq, pbk, pbv = (PB[0], PB[1], PB[2]) if tt % 2 == 0 else (PB[3], PB[6], PB[7])
                    ld_batch = dma(
                        "sync", A3[:, base3:base3 + 4096].rearrange("p (cc t) -> p cc t", t=512),
                        xhat_d.rearrange("(cc p) t -> p cc t", p=128)[:, :, 512 * tt:512 * (tt + 1)],
                        need=nd)
                    xt_tiles = [A3[:, base3 + cc * 512:base3 + cc * 512 + 512] for cc in range(8)]
                    q_need = [("VE", v_wq), ("SE", prev_evac_se[tt % 2])] + lw
                    for woff, pbi in ((0, pbq), (1024, pbk)):
                        for cc in range(8):
                            buf = xt_tiles[cc]
                            te(lambda e, buf=buf, woff=woff, cc=cc, pbi=pbi: e.matmul(
                                pbi[:, :], WA[:, woff + cc * 128:woff + cc * 128 + 128], buf,
                                start=(cc == 0), stop=(cc == 7)),
                               need=[ld_batch] + (q_need if cc == 0 else []))
                    for j in range(4):
                        for cc in range(8):
                            buf = xt_tiles[cc]
                            te(lambda e, buf=buf, j=j, cc=cc, pbv=pbv: e.matmul(
                                pbv[:, j * 128:(j + 1) * 128], buf[:, j * 128:(j + 1) * 128],
                                WA[:, 2048 + cc * 128:2048 + cc * 128 + 128],
                                start=(cc == 0), stop=(cc == 7)),
                               need=[ld_batch] + ([("SE", prev_evac_se[tt % 2])] if cc == 0 and j == 0 else []))
                    t_all = S.val("TE")
                    se(lambda e, tt=tt, pbq=pbq: e.copy(
                        A1[:, Q0 + 512 * tt:Q0 + 512 * (tt + 1)], pbq[:, :]),
                       need=[("TE", t_all), ("VE", v_vones)])
                    se(lambda e, tt=tt, pbk=pbk: e.copy(KR[:, 512 * tt:512 * (tt + 1)], pbk[:, :]))
                    for j in range(4):
                        ch = tt * 4 + j
                        se(lambda e, j=j, ch=ch, pbv=pbv: e.copy(
                            A1[:, V0 + ch * 130:V0 + ch * 130 + 64], pbv[:, j * 128:j * 128 + 64]))
                        se(lambda e, j=j, ch=ch, pbv=pbv: e.copy(
                            A1[:, V0 + ch * 130 + 65:V0 + ch * 130 + 129],
                            pbv[:, j * 128 + 64:j * 128 + 128]))
                    prev_evac_se[tt % 2] = S.val("SE")
                qkv_se = S.val("SE")
                if stop_after == f"qkv{p}":
                    raise _StopEmit

                t_n = 0
                s_sq_prev = 0
                for src_i, tag in enumerate(("q", "k")):
                    s_sq = se(lambda e, src_i=src_i: e.activation(
                        out=A3[:, 4096:8192],
                        in_=(A1[:, Q0:Q0 + 4096] if src_i == 0 else KR[:, 0:4096]),
                        func=AF.Square),
                        need=([("SE", qkv_se), ("VE", S.val("VE")), ("TE", S.val("TE"))]
                              if src_i == 0 else [("TE", t_n)]))
                    for hh in range(2):
                        base = 64 * hh
                        for tc in range(32):
                            col = 64 * src_i + 32 * hh + tc
                            t_n = te(lambda e, base=base, tc=tc, col=col: e.matmul(
                                PB[4][0:128, col:col + 1],
                                A3[base:base + 64, 4096 + 128 * tc:4096 + 128 * (tc + 1)],
                                ONES[base:base + 64, 0:1], start=True, stop=True),
                                need=[("SE", s_sq)] if hh == 0 and tc == 0 else [])
                    s_sq_prev = s_sq
                s_nc = se(lambda e: e.copy(A3[:, 3840:3968], PB[4][:, 0:128]),
                          need=[("TE", t_n)])
                v_red = 0
                for g4 in range(4):
                    v_red = ve(lambda e, g4=g4: e.reduce_max(
                        SM[:, SC0 + 20 + g4:SC0 + 21 + g4],
                        A3[:, 3840 + 32 * g4:3840 + 32 * (g4 + 1)], axis=AX.X),
                        need=[("SE", s_nc)] if g4 == 0 else [], pre_drain=(g4 == 0))
                t_t1 = te(lambda e: e.transpose(PB[5][0:4, 0:128], SM[:, SC0 + 20:SC0 + 24],
                                                ID_), need=[("VE", v_red)])
                s_t1 = se(lambda e: e.copy(A3[0:4, 3968:4096], PB[5][0:4, 0:128]),
                          need=[("TE", t_t1)])
                v_c = ve(lambda e: e.reduce_max(A3[0:4, 4000 - 32:4000 - 31][0:4, 0:1]
                                                if False else SM[0:4, SC0 + 24:SC0 + 25],
                                                A3[0:4, 3968:4096], axis=AX.X),
                         need=[("SE", s_t1)])
                t_t2 = te(lambda e: e.transpose(PB[5][0:1, 0:4], SM[0:4, SC0 + 24:SC0 + 25],
                                                ID_[0:4, 0:4]), need=[("VE", v_c)])
                s_t2 = se(lambda e: e.copy(RW[0:1, 4280:4284], PB[5][0:1, 0:4]),
                          need=[("TE", t_t2)])
                v_p1 = ve(lambda e: e.tensor_tensor(out=RW[0:1, 4284:4286],
                                                    in0=RW[0:1, 4280:4282],
                                                    in1=RW[0:1, 4282:4284], op=ALU.mult),
                          need=[("SE", s_t2)])
                s_sq2 = se(lambda e: e.activation(out=RW[0:1, 4286:4288], in_=RW[0:1, 4284:4286],
                                                  func=AF.Sqrt), need=[("VE", v_p1)])
                v_eb2 = ve(lambda e, p=p: e.tensor_tensor(
                    out=RW[0:1, 4288:4290], in0=PREROW[0:1, 2 * p:2 * p + 2],
                    in1=RW[0:1, 4286:4288], op=ALU.subtract), need=[("SE", s_sq2)])
                t_eb = te(lambda e: e.matmul(PB[5][:, 0:2], ONESR, RW[0:1, 4288:4290],
                                             start=True, stop=True), need=[("VE", v_eb2)])
                s_eb = se(lambda e: e.copy(SM[:, EB0:EB0 + 2], PB[5][:, 0:2]),
                          need=[("TE", t_eb)])
                eb_v = ve(lambda e: e.nop(), need=[("SE", s_eb)], pre_drain=False)
                head_ebias_v = [eb_v, eb_v]
                if stop_after == f"stab{p}":
                    raise _StopEmit

                den = [A3[0:1, 0:4096], A3[0:1, 4096:8192]]
                exp_hist = {}
                av_hist = {}
                prev_evac = S.val("SE")
                sc_banks = {0: (PB[0], PB[1], PB[4]), 1: (PB[6], PB[7], PB[5])}
                av_banks = {0: PB[2], 1: PB[3]}
                EXP0 = 12352
                LOOK = 2
                den_rd = 0
                for tt in range(8):
                    n_sc = 4 * (tt + 1)
                    last_av = {0: 0, 1: 0}

                    def emit_score(sc, hh, tt=tt):
                        so = 128 * sc
                        t_start = max(512 * tt, so)
                        N = 512 * (tt + 1) - t_start
                        bi = sc % 3
                        off = t_start - 512 * tt
                        base = 64 * hh
                        hl = 2 * p + hh
                        band = SM[:, BANDS + hl * 144:BANDS + (hl + 1) * 144]
                        pb_s = sc_banks[hh][bi]
                        t_sc = te(lambda e, base=base, so=so, t_start=t_start, N=N,
                                  pb_s=pb_s: e.matmul(
                            pb_s[:, 0:N], KR[base:base + 64, so:so + 128],
                            A1[base:base + 64, Q0 + t_start:Q0 + t_start + N],
                            start=True, stop=True),
                            need=[("SE", max(exp_hist.get((hh, bi), 0), prev_evac))])
                        v_band = 0
                        if so >= 512 * tt:
                            w = min(144, N)
                            v_band = ve(lambda e, pb_s=pb_s, w=w, band=band: e.tensor_tensor(
                                out=pb_s[:, 0:w], in0=pb_s[:, 0:w], in1=band[:, 0:w],
                                op=ALU.add), need=[("TE", t_sc)], pre_drain=False)
                        elif so == 512 * tt - 128:
                            v_band = ve(lambda e, pb_s=pb_s, band=band: e.tensor_tensor(
                                out=pb_s[:, 0:16], in0=pb_s[:, 0:16], in1=band[:, 128:144],
                                op=ALU.add), need=[("TE", t_sc)], pre_drain=False)
                        ebuf = A1[:, EXP0 + (3 * hh + bi) * 512:EXP0 + (3 * hh + bi) * 512 + 512]
                        v_z = 0
                        if off > 0:
                            v_z = ve(lambda e, ebuf=ebuf, off=off: e.memset(ebuf[:, 0:off], 0.0),
                                     need=[("TE", av_hist.get((hh, bi), 0))], pre_drain=False)
                        nd = [("TE", max(t_sc, av_hist.get((hh, bi), 0))),
                              ("VE", max(v_band, v_z, head_ebias_v[hh]))]
                        s_e = se(lambda e, ebuf=ebuf, off=off, N=N, pb_s=pb_s, hh=hh:
                                 e.activation(out=ebuf[:, off:off + N], in_=pb_s[:, 0:N],
                                              func=AF.Exp,
                                              bias=SM[:, EB0 + hh:EB0 + hh + 1],
                                              scale=1.0), need=nd)
                        exp_hist[(hh, bi)] = s_e
                        return (sc, hh, bi, ebuf, s_e)

                    def emit_av(tok, tt=tt, n_sc=n_sc):
                        sc, hh, bi, ebuf, s_e = tok
                        vch = V0 + sc * 130
                        lhs_off = vch if hh == 0 else vch + 65
                        t_av = te(lambda e, lhs_off=lhs_off, hh=hh, ebuf=ebuf, sc=sc,
                                  n_sc=n_sc: e.matmul(
                            av_banks[hh][0:65, :], A1[:, lhs_off:lhs_off + 65], ebuf,
                            start=(sc == 0), stop=(sc == n_sc - 1)),
                            need=[("SE", s_e)])
                        av_hist[(hh, bi)] = t_av
                        last_av[hh] = t_av

                    pend = []
                    for sc in range(n_sc):
                        for hh in range(2):
                            pend.append(emit_score(sc, hh))
                        if sc >= LOOK:
                            emit_av(pend.pop(0))
                            emit_av(pend.pop(0))
                    while pend:
                        emit_av(pend.pop(0))
                    for hh in range(2):
                        base = 64 * hh
                        se(lambda e, hh=hh, tt=tt, base=base: e.copy(
                            A1[base:base + 64, AO0 + 512 * tt:AO0 + 512 * (tt + 1)],
                            av_banks[hh][0:64, :]), need=[("TE", last_av[hh])])
                        s_dn = se(lambda e, hh=hh, tt=tt: e.copy(
                            den[hh][0:1, 512 * tt:512 * (tt + 1)], av_banks[hh][64:65, :]))
                        den_rd = ve(lambda e, hh=hh, tt=tt: e.reciprocal(
                            den[hh][0:1, 512 * tt:512 * (tt + 1)],
                            den[hh][0:1, 512 * tt:512 * (tt + 1)]),
                            need=[("SE", s_dn)])
                    prev_evac = S.val("SE")
                if stop_after == f"sc{p}":
                    raise _StopEmit

                RCP0 = EXP0 + 3072
                s_rb_prev = 0
                for tt in range(8):
                    t_b0 = te(lambda e, tt=tt: e.matmul(PB[4][0:64, :], ONESR[0:1, 0:64],
                                                        den[0][0:1, 512 * tt:512 * (tt + 1)],
                                                        start=True, stop=True),
                              need=[("VE", den_rd)] + ([("SE", s_rb_prev)] if s_rb_prev else []))
                    t_b1 = te(lambda e, tt=tt: e.matmul(PB[4][64:128, :], ONESR[0:1, 0:64],
                                                        den[1][0:1, 512 * tt:512 * (tt + 1)],
                                                        start=True, stop=True))
                    s_rb = se(lambda e: e.copy(A1[:, RCP0:RCP0 + 512], PB[4][:, :]),
                              need=[("TE", t_b1), ("VE", S.val("VE"))])
                    s_rb_prev = s_rb
                    ve0(lambda e, tt=tt: e.tensor_tensor(
                        out=A1[0:64, AO0 + 512 * tt:AO0 + 512 * (tt + 1)],
                        in0=A1[0:64, AO0 + 512 * tt:AO0 + 512 * (tt + 1)],
                        in1=A1[0:64, RCP0:RCP0 + 512], op=ALU.mult), need=[("SE", s_rb)])
                    v_n1 = ve(lambda e, tt=tt: e.tensor_tensor(
                        out=A1[64:128, AO0 + 512 * tt:AO0 + 512 * (tt + 1)],
                        in0=A1[64:128, AO0 + 512 * tt:AO0 + 512 * (tt + 1)],
                        in1=A1[64:128, RCP0:RCP0 + 512], op=ALU.mult), pre_drain=False)
                v_norm = S.val("VE")
                st_stg = {0: None, 1: None}
                stg_refs = []
                for r in range(2):
                    for blkb in range(2):
                        stg = A3[:, blkb * 2048:blkb * 2048 + 2048]
                        v_m = ve(lambda e, stg=stg, r=r, blkb=blkb: e.tensor_scalar(
                            out=stg, in0=A1[:, AO0 + 2048 * r:AO0 + 2048 * r + 2048],
                            scalar1=GMB[:, blkb:blkb + 1], scalar2=None, op0=ALU.mult),
                            need=[("VE", v_norm), st_stg[blkb]])
                        st_stg[blkb] = dma("sync", rsin[p][r, 128 * blkb:128 * (blkb + 1), :],
                                           stg, need=[("VE", v_m)])
                        stg_refs.append(st_stg[blkb])
                rs_cc.append(gp(lambda e, p=p: e.collective_compute(
                    "ReduceScatter", ALU.add, ins=[rsin[p]], outs=[rsout[p]],
                    replica_groups=RG_PAIR), need=list(stg_refs)))
                prev_pair_se = S.val("SE")
                prev_pair_te = S.val("TE")
                prev_stg_refs = list(stg_refs)
                if stop_after == f"pair{p}":
                    raise _StopEmit

            if stop_after == "BC": raise _StopEmit
            ld_wp = dma("sync", A1[:, 8192:16384].rearrange("p (jc cc m) -> p jc cc m", cc=8, m=128),
                        wp.rearrange("(jc p) (cc m) -> p jc cc m", p=128, m=128),
                        need=[("VE", S.val("VE"))])

            def ln_stats(src_arena, need0_se=0, need0_v=0):
                t_last = 0
                for cc in range(8):
                    nd = [("SE", need0_se), ("TE", t_last)]
                    if cc == 0 and need0_v:
                        nd.append(("VE", need0_v))
                    s_sq = se(lambda e, cc=cc: e.activation(
                        out=A3[:, 0:2048], in_=src_arena[:, cc * TL:(cc + 1) * TL],
                        func=AF.Square), need=nd)
                    for tc in range(16):
                        te(lambda e, cc=cc, tc=tc: e.matmul(
                            PB[5][:, 16 * cc + tc:16 * cc + tc + 1],
                            src_arena[:, cc * TL + 128 * tc:cc * TL + 128 * (tc + 1)],
                            ONES[:, 0:1], start=True, stop=True),
                           need=[("SE", s_sq)] if tc == 0 and cc == 0 else [])
                        te(lambda e, tc=tc, cc=cc: e.matmul(
                            PB[5][:, 256 + 16 * cc + tc:256 + 16 * cc + tc + 1],
                            A3[:, 128 * tc:128 * (tc + 1)],
                            ONES[:, 0:1], start=True, stop=True),
                           need=[("SE", s_sq)] if tc == 0 else [])
                    t_last = S.val("TE")
                s_ev = se(lambda e: e.copy(RW[:, 3072:3328] if False else RW[:, 3392:3904],
                                           PB[5][:, 0:512]), need=[("TE", t_last)])
                v_f = 0
                for half, base0 in ((64, 3392), (64, 3648)):
                    b0 = base0
                    for wdt in (64, 32, 16):
                        v_f = ve(lambda e, b0=b0, wdt=wdt: e.tensor_tensor(
                            out=RW[:, b0:b0 + wdt], in0=RW[:, b0:b0 + wdt],
                            in1=RW[:, b0 + wdt:b0 + 2 * wdt], op=ALU.add),
                            need=[("SE", s_ev)] if wdt == 64 else [])
                ve(lambda e: e.tensor_scalar(out=RW[:, 3072:3088], in0=RW[:, 3392:3408],
                                             scalar1=1.0 / 1024.0, scalar2=None,
                                             op0=ALU.mult))
                ve(lambda e: e.tensor_scalar(out=RW[:, 3088:3104], in0=RW[:, 3648:3664],
                                             scalar1=1.0 / 1024.0, scalar2=None,
                                             op0=ALU.mult), pre_drain=False)
                ve(lambda e: e.tensor_mul(RW[:, 3104:3120], RW[:, 3072:3088],
                                          RW[:, 3072:3088]))
                v_var = ve(lambda e: e.tensor_tensor(out=RW[:, 3088:3104],
                                                     in0=RW[:, 3088:3104],
                                                     in1=RW[:, 3104:3120],
                                                     op=ALU.subtract))
                s_sd = se(lambda e: e.activation(out=RW[:, 3088:3104], in_=RW[:, 3088:3104],
                                                 func=AF.Sqrt, bias=EPS, scale=1.0),
                          need=[("VE", v_var)])
                v_rc = ve(lambda e: e.reciprocal(RW[:, 3088:3104], RW[:, 3088:3104]),
                          need=[("SE", s_sd)])
                t_t1 = te(lambda e: e.transpose(PB[5][0:16, 0:128], RW[:, 3072:3088], ID_),
                          need=[("VE", v_rc)])
                t_t2 = te(lambda e: e.transpose(PB[6][0:16, 0:128], RW[:, 3088:3104], ID_))
                s_t1 = se(lambda e: e.copy(RW[0:16, 3136:3264], PB[5][0:16, 0:128]),
                          need=[("TE", t_t2)])
                s_t2 = se(lambda e: e.copy(RW[0:16, 3264:3392], PB[6][0:16, 0:128]))
                d_m = dma("sync", A3[0:1, 0:2048], RW[0:16, 3136:3264],
                          need=[("SE", s_t2)])
                d_r = dma("sync", A3[0:1, 2048:4096], RW[0:16, 3264:3392])
                s_bc_prev = 0
                last_v = 0
                for tt4 in range(4):
                    t_b1 = te(lambda e, tt4=tt4: e.matmul(
                        PB[5][:, :], ONESR, A3[0:1, 512 * tt4:512 * (tt4 + 1)],
                        start=True, stop=True),
                        need=[d_m, d_r] + ([("SE", s_bc_prev)] if s_bc_prev else [("SE", s_t2)]))
                    t_b2 = te(lambda e, tt4=tt4: e.matmul(
                        PB[6][:, :], ONESR, A3[0:1, 2048 + 512 * tt4:2048 + 512 * (tt4 + 1)],
                        start=True, stop=True))
                    s_m = se(lambda e, tt4=tt4: e.copy(
                        A3[:, 4096 + 512 * tt4:4096 + 512 * (tt4 + 1)], PB[5][:, :]),
                        need=[("TE", t_b2)])
                    s_bc_prev = se(lambda e, tt4=tt4: e.copy(
                        A3[:, 6144 + 512 * tt4:6144 + 512 * (tt4 + 1)], PB[6][:, :]))
                last_v = ve(lambda e: e.nop(), need=[("SE", s_bc_prev)], pre_drain=False)
                return last_v

            rs_all = rs_cc[-1]
            s_prev = 0
            for tt4 in range(4):
                ld_refs = []
                for p4 in range(4):
                    nd = [("CC", rs_all)] if tt4 == 0 and p4 == 0 else \
                         ([("SE", s_prev)] if p4 == 0 else [])
                    ld_refs.append(dma(
                        "sync",
                        A3[:, 0:4096].rearrange("p (blk j t) -> p blk j t",
                                                blk=2, j=4, t=512)[:, :, p4, :],
                        rsout[p4].rearrange("(blk p) t -> p blk t", p=128)[:, :, 512 * tt4:512 * (tt4 + 1)],
                        need=nd))
                tls = [A3[:, j * 512:j * 512 + 512] for j in range(8)]
                for cc in range(8):
                    t_m = 0
                    for j in range(8):
                        buf = tls[j]
                        t_m = te(lambda e, buf=buf, j=j, cc=cc: e.matmul(
                            PB[4][:, :], A1[:, 8192 + (j * 8 + cc) * 128:8192 + (j * 8 + cc) * 128 + 128],
                            buf, start=(j == 0), stop=(j == 7)),
                            need=[ld_wp] + ld_refs + ([("SE", s_prev)] if j == 0 else []))
                    s_prev = se(lambda e, cc=cc, tt4=tt4: e.add(
                        A2[:, cc * TL + 512 * tt4:cc * TL + 512 * tt4 + 512], PB[4][:, :],
                        SM[:, BP0 + cc:BP0 + cc + 1]), need=[("TE", t_m)])

            ln3_v = ln_stats(A2, need0_se=s_prev)
            x1_v = 0
            ld_prev_v = ln3_v
            for cc in range(8):
                xb = A3[:, 1024:3072]
                ld = dma("sync", xb, xtokT[128 * cc:128 * (cc + 1), :],
                         need=[("VE", ld_prev_v)])
                ve(lambda e, cc=cc: e.tensor_tensor(
                    out=A1[:, cc * TL:(cc + 1) * TL], in0=A2[:, cc * TL:(cc + 1) * TL],
                    in1=A3[:, 4096:6144], op=ALU.subtract),
                   need=[("VE", ln3_v), ld], pre_drain=(cc == 0))
                ve(lambda e, cc=cc: e.tensor_tensor(
                    out=A1[:, cc * TL:(cc + 1) * TL], in0=A1[:, cc * TL:(cc + 1) * TL],
                    in1=A3[:, 6144:8192], op=ALU.mult))
                x1_v = ve(lambda e, cc=cc, xb=xb: e.tensor_tensor(
                    out=A1[:, cc * TL:(cc + 1) * TL], in0=A1[:, cc * TL:(cc + 1) * TL],
                    in1=xb, op=ALU.add))
                ld_prev_v = x1_v

            ln2_v = ln_stats(A1, need0_v=x1_v)
            v_mx = ve(lambda e: e.memset(SM[:, WMX:WMX + 1], 0.0), pre_drain=False)
            xln_v = 0
            for cc in range(8):
                ve(lambda e, cc=cc: e.tensor_tensor(
                    out=A2[:, cc * TL:(cc + 1) * TL], in0=A1[:, cc * TL:(cc + 1) * TL],
                    in1=A3[:, 4096:6144], op=ALU.subtract),
                   need=[("VE", ln2_v)] if cc == 0 else (), pre_drain=(cc == 0))
                ve(lambda e, cc=cc: e.tensor_tensor(
                    out=A2[:, cc * TL:(cc + 1) * TL], in0=A2[:, cc * TL:(cc + 1) * TL],
                    in1=A3[:, 6144:8192], op=ALU.mult))
                ve0(lambda e, cc=cc: e.reduce_max(SM[:, WMX + 8:WMX + 9],
                                                  A2[:, cc * TL:(cc + 1) * TL],
                                                  axis=AX.X, apply_absolute_value=True),
                    pre_drain=True)
                xln_v = ve(lambda e: e.tensor_max(SM[:, WMX:WMX + 1], SM[:, WMX:WMX + 1],
                                                  SM[:, WMX + 8:WMX + 9]))

            ar_state = {"s_b": 0}

            def allreduce_scale(round_i, colmax_col, qs_off, extra_pre_scale=None, need_v=0):
                t_t = te(lambda e, colmax_col=colmax_col: e.transpose(
                    PB[7][0:1, 0:128], SM[:, colmax_col:colmax_col + 1], ID_),
                    need=[("VE", need_v), ("SE", ar_state["s_b"])])
                s_t = se(lambda e: e.copy(RW[0:1, 4096:4224], PB[7][0:1, 0:128]),
                         need=[("TE", t_t)])
                d0 = dma("sync", arin_d[round_i], RW[0:1, 4096:4224], need=[("SE", s_t)])
                cc_i = gp(lambda e, round_i=round_i: e.collective_compute(
                    "AllReduce", ALU.max, ins=[arin_d[round_i]], outs=[arout_d[round_i]],
                    replica_groups=RG_ALL), need=[d0])
                d1 = dma("sync", RW[0:1, 4096:4224], arout_d[round_i], need=[("CC", cc_i)])
                v1 = ve(lambda e: e.reduce_max(RW[0:1, 4270:4271], RW[0:1, 4096:4224], axis=AX.X),
                        need=[d1])
                if extra_pre_scale is not None:
                    v1 = ve(lambda e, x=extra_pre_scale: e.tensor_scalar(
                        out=RW[0:1, 4270:4271], in0=RW[0:1, 4270:4271], scalar1=x,
                        scalar2=None, op0=ALU.mult))
                v2 = ve(lambda e: e.tensor_scalar(out=RW[0:1, 4271:4272], in0=RW[0:1, 4270:4271],
                                                  scalar1=1.0 / 32767.0, scalar2=1e-12,
                                                  op0=ALU.mult, op1=ALU.max))
                v3 = ve(lambda e: e.reciprocal(RW[0:1, 4272:4273], RW[0:1, 4271:4272]))
                t_b = te(lambda e: e.matmul(PB[7][:, 0:2], ONESR, RW[0:1, 4271:4273],
                                            start=True, stop=True),
                         need=[("VE", v3), ("SE", s_t)])
                s_b = se(lambda e, qs_off=qs_off: e.copy(SM[:, QS0 + qs_off:QS0 + qs_off + 2],
                                                         PB[7][:, 0:2]), need=[("TE", t_b)])
                ar_state["s_b"] = s_b
                return ve(lambda e: e.nop(), need=[("SE", s_b)], pre_drain=False)

            ar0_v = allreduce_scale(0, WMX, 0, need_v=xln_v)

            xq_v = 0
            for cc in range(8):
                ve(lambda e, cc=cc: e.tensor_scalar(
                    out=A2[:, cc * TL:(cc + 1) * TL], in0=A2[:, cc * TL:(cc + 1) * TL],
                    scalar1=SM[:, QS0 + 1:QS0 + 2], scalar2=MAGIC, op0=ALU.mult, op1=ALU.add),
                   need=[("VE", ar0_v)] if cc == 0 else (), pre_drain=(cc == 0))
                xq_v = ve(lambda e, cc=cc: e.tensor_scalar(
                    out=A2[:, cc * TL:(cc + 1) * TL], in0=A2[:, cc * TL:(cc + 1) * TL],
                    scalar1=MAGIC, scalar2=SM[:, QS0:QS0 + 1], op0=ALU.subtract, op1=ALU.mult))

            if stop_after == "D": raise _StopEmit
            SB_ = WA[:, 6144:6176]
            SI_ = WA[:, 6176:6208]

            SIE = RW[:, 1024:2048]
            SBE = RW[:, 2048:3072]

            def dequant_1024(nat, need0, need_mults=(), out=None):
                s_abs = se(lambda e, nat=nat: e.activation(out=WA[:, 2048:3072], in_=nat,
                                                           func=AF.Abs), need=list(need0))
                w_ = 512
                nd_f = [("SE", s_abs)]
                while w_ >= 32:
                    ve(lambda e, w_=w_: e.tensor_tensor(
                        out=WA[:, 2048:2048 + w_], in0=WA[:, 2048:2048 + w_],
                        in1=WA[:, 2048 + w_:2048 + 2 * w_], op=ALU.max), need=nd_f)
                    nd_f = []
                    w_ //= 2
                ve(lambda e: e.tensor_scalar(out=SB_[:, :], in0=WA[:, 2048:2080],
                                             scalar1=1.0 / 7.0,
                                             scalar2=1e-12, op0=ALU.mult, op1=ALU.max))
                v_rc = ve(lambda e: e.reciprocal(SI_[:, :], SB_[:, :]))
                d_e1 = dma("sync", SIE[:].rearrange("p (j n) -> p j n", n=32),
                           SI_[:].rearrange("p n -> p () n").broadcast_to((128, 32, 32)),
                           need=[("VE", v_rc), ("VE", st["dq_last"])])
                d_e2 = dma("sync", SBE[:].rearrange("p (j n) -> p j n", n=32),
                           SB_[:].rearrange("p n -> p () n").broadcast_to((128, 32, 32)))
                if out is None:
                    out = WA[:, 3072:4096]
                ve(lambda e, nat=nat, out=out: e.tensor_tensor(out=out, in0=nat,
                                                               in1=SIE[:], op=ALU.mult),
                   need=[d_e1, d_e2] + list(need_mults), pre_drain=False)
                ve(lambda e, out=out: e.tensor_scalar(out=out, in0=out,
                                                      scalar1=MAGIC, scalar2=MAGIC,
                                                      op0=ALU.add, op1=ALU.subtract))
                ve(lambda e, out=out: e.tensor_tensor(out=out, in0=out,
                                                      in1=SBE[:], op=ALU.mult))
                return ve(lambda e: e.nop(), pre_drain=True)

            sp_prev = {0: None, 1: None}
            v_red_prev = {0: 0, 1: 0}
            wmax_done = {}
            st = {"dq_hist": {0: 0, 1: 0}, "dq_last": 0, "stg_mm": {0: 0, 1: 0},
                  "tr_te": 0, "stg_se": 0, "s_ev": 0, "dq": {}}

            def emit_f2_prep(wsrc, ic, first):
                par = ic % 2
                nat = WA[:, par * 1024:par * 1024 + 1024]
                ld = dma("sync", nat, wsrc[128 * ic:128 * (ic + 1), :],
                         need=[("VE", st["dq_hist"][par])] if st["dq_hist"][par] else
                              ([("VE", xq_v)] if first else []))
                dq_v = dequant_1024(nat, [ld, ("VE", st["dq_last"])],
                                    need_mults=[("TE", st["tr_te"])])
                st["dq_hist"][par] = dq_v
                st["dq_last"] = dq_v
                st["dq"][ic] = dq_v

            def emit_f2_stage(ic):
                par = ic % 2
                dq_v = st["dq"][ic]
                stg0 = 4096 + par * 1024
                s_t_hist = {}
                for cc in range(8):
                    pbt = PB[5 + (cc % 2)]
                    t_t = te(lambda e, cc=cc, pbt=pbt: e.transpose(
                        pbt[:, 0:128], WA[:, 3072 + cc * 128:3072 + cc * 128 + 128], ID_),
                        need=[("VE", dq_v),
                              ("SE", s_t_hist.get(cc - 2, st["stg_se"]))])
                    s_t = se(lambda e, cc=cc, stg0=stg0, pbt=pbt: e.copy(
                        WA[:, stg0 + cc * 128:stg0 + cc * 128 + 128], pbt[:, 0:128]),
                        need=[("TE", max(t_t, st["stg_mm"][par] if cc == 0 else 0))])
                    s_t_hist[cc] = s_t
                st["stg_se"] = S.val("SE")
                st["tr_te"] = S.val("TE")

            def emit_f2_mms(wdst, mxc, ic):
                par = ic % 2
                stg0 = 4096 + par * 1024
                s_stg = st["stg_se"]
                for tt4 in range(4):
                    t_m = 0
                    for cc in range(8):
                        t_m = te(lambda e, cc=cc, tt4=tt4, stg0=stg0: e.matmul(
                            PB[4][:, :], WA[:, stg0 + cc * 128:stg0 + cc * 128 + 128],
                            A2[:, cc * TL + 512 * tt4:cc * TL + 512 * tt4 + 512],
                            start=(cc == 0), stop=(cc == 7)),
                            need=([("SE", max(s_stg, st["s_ev"])), ("VE", xq_v)] if cc == 0 else []))
                    sp = RW[:, (tt4 % 2) * 512:(tt4 % 2) * 512 + 512]
                    s_ev = se(lambda e, sp=sp: e.copy(sp, PB[4][:, :]),
                              need=[("TE", t_m), sp_prev[tt4 % 2],
                                    ("VE", v_red_prev[tt4 % 2])])
                    st["s_ev"] = s_ev
                    ve0(lambda e, sp=sp, mxc=mxc: e.reduce_max(
                        SM[:, mxc + 8:mxc + 9], sp, axis=AX.X, apply_absolute_value=True),
                        need=[("SE", s_ev)], pre_drain=True)
                    v_red_prev[tt4 % 2] = ve(lambda e, mxc=mxc: e.tensor_max(
                        SM[:, mxc:mxc + 1], SM[:, mxc:mxc + 1], SM[:, mxc + 8:mxc + 9]))
                    sp_prev[tt4 % 2] = dma("scalar", wdst[128 * ic:128 * (ic + 1),
                                                          512 * tt4:512 * (tt4 + 1)],
                                           sp, need=[("SE", s_ev)])
                st["stg_mm"][par] = S.val("TE")

            ve(lambda e: e.memset(SM[:, WMX + 1:WMX + 2], 0.0), pre_drain=False)
            emit_f2_prep(wgate, 0, first=True)
            for ic in range(24):
                emit_f2_stage(ic)
                if ic + 1 < 24:
                    emit_f2_prep(wgate, ic + 1, first=False)
                else:
                    emit_f2_prep(wup, 0, first=False)
                emit_f2_mms(gate_d, WMX + 1, ic)
            wmax_done[0] = S.val("VE")
            gate_sp_refs = qall("scalar")
            ve(lambda e: e.memset(SM[:, WMX + 2:WMX + 3], 0.0), pre_drain=False)
            ve(lambda e: e.memset(SM[:, WMX + 3:WMX + 4], 0.0), pre_drain=False)
            v_hist3 = {0: 0, 1: 0}
            st_h3 = {0: None, 1: None}

            f3_gq = {}

            def emit_f3_head(ic):
                par = ic % 2
                gb = A3[:, par * 2048:par * 2048 + 2048]
                gq = A3[:, 4096 + par * 2048:4096 + par * 2048 + 2048]
                ld = dma("sync", gb, gate_d[128 * ic:128 * (ic + 1), :],
                         need=gate_sp_refs if ic == 0 else [("VE", v_hist3[par])])
                ve(lambda e, gb=gb, gq=gq: e.tensor_scalar(
                    out=gq, in0=gb, scalar1=SM[:, QS0 + 3:QS0 + 4], scalar2=MAGIC,
                    op0=ALU.mult, op1=ALU.add),
                   need=[ld, ("VE", ar1_v), st_h3[par]], pre_drain=False)
                f3_gq[ic] = ve(lambda e, gq=gq: e.tensor_scalar(
                    out=gq, in0=gq, scalar1=MAGIC, scalar2=SM[:, QS0 + 2:QS0 + 3],
                    op0=ALU.subtract, op1=ALU.mult))

            def emit_f3_ic(ic):
                par = ic % 2
                gb = A3[:, par * 2048:par * 2048 + 2048]
                gq = A3[:, 4096 + par * 2048:4096 + par * 2048 + 2048]
                sg = gb
                v_gq = f3_gq[ic]
                s_sig = se(lambda e, gq=gq, sg=sg: e.activation(out=sg, in_=gq,
                                                                func=AF.Sigmoid),
                           need=[("VE", v_gq)])
                ve(lambda e, sg=sg: e.tensor_scalar(
                    out=sg, in0=sg, scalar1=65536.0, scalar2=MAGIC, op0=ALU.mult,
                    op1=ALU.add), need=[("SE", s_sig)], pre_drain=False)
                ve(lambda e, sg=sg: e.tensor_scalar(
                    out=sg, in0=sg, scalar1=MAGIC, scalar2=65535.0, op0=ALU.subtract,
                    op1=ALU.min))
                ve(lambda e, gq=gq, sg=sg: e.tensor_tensor(out=gq, in0=gq, in1=sg,
                                                           op=ALU.mult))
                ve0(lambda e, gq=gq: e.reduce_max(SM[:, WMX + 9:WMX + 10], gq, axis=AX.X,
                                                  apply_absolute_value=True), pre_drain=True)
                v_m3 = ve(lambda e: e.tensor_max(SM[:, WMX + 3:WMX + 4],
                                                 SM[:, WMX + 3:WMX + 4],
                                                 SM[:, WMX + 9:WMX + 10]))
                v_hist3[par] = v_m3
                st_h3[par] = dma("gpsimd", g_d[128 * ic:128 * (ic + 1), :], gq,
                                 need=[("VE", v_m3)])

            ar1_v = None
            for ic in range(24):
                emit_f2_stage(ic)
                if ic + 1 < 24:
                    emit_f2_prep(wup, ic + 1, first=False)
                if ic == 0:
                    ar1_v = allreduce_scale(1, WMX + 1, 2, need_v=wmax_done[0])
                emit_f3_head(ic)
                emit_f2_mms(up_d, WMX + 2, ic)
                emit_f3_ic(ic)
            wmax_done[1] = S.val("VE")
            ar2_v = allreduce_scale(2, WMX + 2, 4, need_v=wmax_done[1])
            ar3_v = allreduce_scale(3, WMX + 3, 6, extra_pre_scale=1.0 / 65536.0,
                                    need_v=S.val("VE"))
            v_adj = ve(lambda e: e.tensor_scalar(out=SM[:, QS0 + 7:QS0 + 8],
                                                 in0=SM[:, QS0 + 7:QS0 + 8],
                                                 scalar1=1.0 / 65536.0, scalar2=None,
                                                 op0=ALU.mult), need=[("VE", ar3_v)])

            if stop_after == "F2": raise _StopEmit
            if stop_after == "F3": raise _StopEmit
            st5 = {"nat": {0: 0, 1: 0}, "ote": {0: 0, 1: 0, 2: 0}, "mm": None,
                   "dq": {}, "acc_se": 0, "acc_v": 0, "pb56": None}
            WDQ2 = 6144

            def emit_f5_prep_h(hg, part, hh):
                hc = 2 * hg + hh
                nat = WA[:, hh * 1024:hh * 1024 + 1024]
                ld = dma("sync", nat,
                         wdown[128 * hc:128 * (hc + 1), 1024 * part:1024 * (part + 1)],
                         need=[("VE", st5["nat"][hh] if st5["nat"][hh]
                                else st["dq_last"])])
                dq_v = dequant_1024(
                    nat, [ld, ("VE", st["dq_last"])],
                    need_mults=[("TE", st["tr_te"])],
                    out=(WA[:, 3072:4096] if hh == 0 else A3[:, WDQ2:WDQ2 + 1024]))
                st5["nat"][hh] = dq_v
                st["dq_last"] = dq_v
                st5["dq"][hh] = dq_v

            def emit_f5_stage(part):
                mm_g = st5["mm"] if st5["mm"] is not None else S.val("TE")
                s_t_hist = {}
                for u in range(16):
                    hh, q8 = u // 8, u % 8
                    srcb = (WA[:, 3072 + q8 * 128:3072 + q8 * 128 + 128] if hh == 0
                            else A3[:, WDQ2 + q8 * 128:WDQ2 + q8 * 128 + 128])
                    pbt = PB[5 + (u % 2)]
                    nd = [("VE", st5["dq"][hh])] if q8 == 0 else []
                    if u < 2:
                        nd.append(st5["pb56"] if st5["pb56"] is not None
                                  else ("SE", st["stg_se"]))
                    else:
                        nd.append(("SE", s_t_hist[u - 2]))
                    t_t = te(lambda e, pbt=pbt, srcb=srcb: e.transpose(
                        pbt[:, 0:128], srcb, ID_), need=nd)
                    s_t = se(lambda e, u=u, hh=hh, q8=q8, pbt=pbt: e.copy(
                        WA[:, 4096 + hh * 1024 + q8 * 128:
                              4096 + hh * 1024 + q8 * 128 + 128], pbt[:, 0:128]),
                        need=[("TE", max(t_t, mm_g if u == 0 else 0))])
                    s_t_hist[u] = s_t
                st["stg_se"] = S.val("SE")
                st["tr_te"] = S.val("TE")

            def emit_f5_mms(hg, part):
                for q8 in range(8):
                    ici = part * 8 + q8
                    slot = ici % 3
                    obuf = A3[:, slot * 2048:slot * 2048 + 2048]
                    nd_ld = ([("TE", st5["ote"][slot])] if st5["ote"][slot]
                             else qall("scalar") + qall("gpsimd"))
                    ld_o = dma("sync", obuf, opre_d[128 * ici:128 * (ici + 1), :],
                               need=nd_ld)
                    ve(lambda e, obuf=obuf: e.tensor_scalar(
                        out=obuf, in0=obuf, scalar1=SM[:, QS0 + 9:QS0 + 10],
                        scalar2=MAGIC, op0=ALU.mult, op1=ALU.add),
                       need=[ld_o, ("VE", ar4_v)], pre_drain=False)
                    v_q = ve(lambda e, obuf=obuf: e.tensor_scalar(
                        out=obuf, in0=obuf, scalar1=MAGIC,
                        scalar2=SM[:, QS0 + 8:QS0 + 9],
                        op0=ALU.subtract, op1=ALU.mult))
                    for hh in range(2):
                        for tt4 in range(4):
                            nd = [("VE", v_q)] if hh == 0 and tt4 == 0 else []
                            if q8 == 0 and hh == 0 and tt4 == 0:
                                nd += [("SE", st["stg_se"]), ("SE", st5["acc_se"]),
                                       ("VE", st5["acc_v"])]
                            t_m = te(lambda e, obuf=obuf, hh=hh, q8=q8, tt4=tt4:
                                     e.matmul(
                                PB[hh * 4 + tt4][:, :],
                                WA[:, 4096 + hh * 1024 + q8 * 128:
                                      4096 + hh * 1024 + q8 * 128 + 128],
                                obuf[:, 512 * tt4:512 * (tt4 + 1)],
                                start=(q8 == 0), stop=(q8 == 7)), need=nd)
                    st5["ote"][slot] = t_m
                    if q8 == 2:
                        emit_f5_prep_next(hg, part, 0)
                    elif q8 == 5:
                        emit_f5_prep_next(hg, part, 1)
                st5["mm"] = S.val("TE")

            def emit_f5_prep_next(hg, part, hh):
                if part < 2:
                    emit_f5_prep_h(hg, part + 1, hh)
                elif hg < 3:
                    emit_f5_prep_h(hg + 1, 0, hh)

            F5_BANKS = [(1, 1), (1, 2), (1, 0), (1, 3), (0, 0), (0, 1), (0, 2), (0, 3)]

            def emit_f5_acc(hg, part):
                first = True
                v56 = 0
                for hh, tt4 in F5_BANKS:
                    hc = 2 * hg + hh
                    dst = A2[:, hc * TL + 512 * tt4:hc * TL + 512 * tt4 + 512]
                    if part == 0:
                        r = st5["acc_se"] = se(
                            lambda e, dst=dst, hh=hh, tt4=tt4: e.copy(
                                dst, PB[hh * 4 + tt4][:, :]),
                            need=[("TE", st5["mm"])] if first else [])
                    else:
                        r = st5["acc_v"] = ve(
                            lambda e, dst=dst, hh=hh, tt4=tt4: e.tensor_tensor(
                                out=dst, in0=PB[hh * 4 + tt4][:, :], in1=dst,
                                op=ALU.add),
                            need=[("TE", st5["mm"])] if first else [],
                            pre_drain=False)
                    if (hh, tt4) == (1, 2):
                        v56 = r
                    first = False
                st5["pb56"] = (("SE" if part == 0 else "VE"), v56)

            ve(lambda e: e.memset(SM[:, WMX + 4:WMX + 5], 0.0), pre_drain=False)
            v_hist4 = {0: 0, 1: 0}
            gp_hist4 = {0: 0, 1: 0}
            st_h4 = {0: None, 1: None}
            for ic in range(24):
                par = ic % 2
                if ic == 22:
                    emit_f5_prep_h(0, 0, 0)
                Gb = A3[:, par * 2048:par * 2048 + 2048]
                ub = A3[:, 4096 + par * 2048:4096 + par * 2048 + 2048]
                ld1 = dma("sync", Gb, g_d[128 * ic:128 * (ic + 1), :],
                          need=(qall("scalar") + qall("gpsimd") if ic == 0 else
                                [("VE", v_hist4[par]), ("CC", gp_hist4[par])]) +
                               [st_h4[par]])
                ld2 = dma("sync", ub, up_d[128 * ic:128 * (ic + 1), :])
                ve(lambda e, Gb=Gb: e.tensor_scalar(
                    out=Gb, in0=Gb, scalar1=SM[:, QS0 + 7:QS0 + 8], scalar2=MAGIC,
                    op0=ALU.mult, op1=ALU.add),
                   need=[ld1, ld2, ("VE", max(v_adj, ar2_v))], pre_drain=False)
                ve(lambda e, Gb=Gb: e.tensor_scalar(
                    out=Gb, in0=Gb, scalar1=MAGIC, scalar2=SM[:, QS0 + 6:QS0 + 7],
                    op0=ALU.subtract, op1=ALU.mult))
                gp_q = gp(lambda e, ub=ub: e.tensor_scalar(
                    out=ub, in0=ub, scalar1=SM[:, QS0 + 5:QS0 + 6], scalar2=MAGIC,
                    op0=ALU.mult, op1=ALU.add), need=[ld2, ("VE", ar2_v)])
                gp_hist4[par] = gp_q
                ve(lambda e, ub=ub: e.tensor_scalar(
                    out=ub, in0=ub, scalar1=MAGIC, scalar2=SM[:, QS0 + 4:QS0 + 5],
                    op0=ALU.subtract, op1=ALU.mult), need=[("CC", gp_q)],
                   pre_drain=False)
                ve(lambda e, Gb=Gb, ub=ub: e.tensor_tensor(out=Gb, in0=Gb, in1=ub, op=ALU.mult))
                ve0(lambda e, Gb=Gb: e.reduce_max(SM[:, WMX + 10:WMX + 11], Gb, axis=AX.X,
                                                  apply_absolute_value=True), pre_drain=True)
                v_m4 = ve(lambda e: e.tensor_max(SM[:, WMX + 4:WMX + 5], SM[:, WMX + 4:WMX + 5],
                                                 SM[:, WMX + 10:WMX + 11]))
                v_hist4[par] = v_m4
                st_h4[par] = dma("scalar", opre_d[128 * ic:128 * (ic + 1), :], Gb,
                                 need=[("VE", v_m4)])
            ar4_v = allreduce_scale(4, WMX + 4, 8, need_v=S.val("VE"))

            if stop_after == "F4": raise _StopEmit
            emit_f5_prep_h(0, 0, 1)
            for hg in range(4):
                for part in range(3):
                    emit_f5_stage(part)
                    emit_f5_mms(hg, part)
                    emit_f5_acc(hg, part)

            if stop_after == "F5b": raise _StopEmit
            ln4_v = ln_stats(A2, need0_se=st5["acc_se"], need0_v=st5["acc_v"])
            st_out = None
            out_v_prev = ln4_v
            for cc in range(8):
                ob = A3[:, 1024:3072]
                ve(lambda e, cc=cc, ob=ob: e.tensor_tensor(
                    out=ob, in0=A2[:, cc * TL:(cc + 1) * TL], in1=A3[:, 4096:6144],
                    op=ALU.subtract),
                   need=[("VE", out_v_prev), st_out], pre_drain=True)
                ve(lambda e, ob=ob: e.tensor_tensor(out=ob, in0=ob, in1=A3[:, 6144:8192],
                                                    op=ALU.mult))
                v_o = ve(lambda e, cc=cc, ob=ob: e.tensor_tensor(
                    out=ob, in0=ob, in1=A1[:, cc * TL:(cc + 1) * TL], op=ALU.add))
                out_v_prev = v_o
                st_out = dma("sync", out_d[128 * cc:128 * (cc + 1), :], ob,
                             need=[("VE", v_o)])


        except _StopEmit:
            truncated[0] = True
            st_out = dma("sync", out_d[0:128, 0:128], SM[:, 0:128], need=())
        tapsrc = {"xhat": xhat_d, "gate": gate_d, "up": up_d, "G": g_d, "opre": opre_d,
                  "rsout0": rsout[0], "rsout1": rsout[1],
                  "rsout2": rsout[2], "rsout3": rsout[3]}
        for nm in dbg_d:
            st_out = dma("sync", dbg_d[nm], tapsrc[nm], need=[st_out])

        final_refs = qall("sync") + qall("scalar") + qall("gpsimd")

        def fin(e, refs=tuple(final_refs)):
            waits(e, refs)
        S.ops["sync"].append(fin)

        @blk.sync
        def _(eng):
            for fn in S.ops["sync"]:
                fn(eng)

        @blk.tensor
        def _(eng):
            for fn in S.ops["tensor"]:
                fn(eng)

        @blk.vector
        def _(eng):
            for fn in S.ops["vector"]:
                fn(eng)

        @blk.scalar
        def _(eng):
            for fn in S.ops["scalar"]:
                fn(eng)

        @blk.gpsimd
        def _(eng):
            for fn in S.ops["gpsimd"]:
                fn(eng)

    return nc


PC = np.arange(1024).reshape(32, 32).T.reshape(-1).copy()
PI3 = np.concatenate([PC, PC + 1024, PC + 2048])


def make_inputs(inp, c):
    b, g = c // 2, c % 2
    x = np.asarray(inp["L0_attn_x"], np.float32)
    Wq = np.asarray(inp["L0_attn_Wq"], np.float32)[:, PC, :]
    Wk = np.asarray(inp["L0_attn_Wk"], np.float32)[:, PC, :]
    Wv = np.asarray(inp["L0_attn_Wv"], np.float32)[:, PC, :]
    Wp = np.asarray(inp["L0_attn_Wp"], np.float32)[:, PC]
    bp = np.asarray(inp["L0_attn_bp"], np.float32)[PC]
    rt = np.asarray(inp["L0_attn_rel_table"], np.float32)
    dmv = np.zeros((17, 128, 144), np.float32)
    for v in range(17):
        for i in range(128):
            if i + v < 144:
                dmv[v, i, i + v] = 1.0
    mneg = np.zeros((128, 144), np.float32)
    for i in range(128):
        mneg[i, :i] = -1.0e30
    rt16 = np.zeros((16, 33), np.float32)
    rt16[0:8] = rt.T[8 * g:8 * g + 8]
    gm = np.zeros((1, 2), np.float32)
    gm[0, g] = 1.0

    def pair(W):
        return np.stack([np.concatenate([W[8 * g + 2 * p], W[8 * g + 2 * p + 1]], axis=1)
                         for p in range(4)])
    return {
        "xbT": np.ascontiguousarray(x[b].T[PC]),
        "xtokT": np.ascontiguousarray(x[b, 2048 * g:2048 * (g + 1)].T[PC]),
        "wq": pair(Wq), "wk": pair(Wk), "wv": pair(Wv),
        "wp": np.ascontiguousarray(Wp),
        "bpc": np.ascontiguousarray(bp[:, None]),
        "rtT": rt16,
        "wgate": np.ascontiguousarray(np.asarray(inp["L0_ffn_w_gate"], np.float32)[PI3][:, PC]),
        "wup": np.ascontiguousarray(np.asarray(inp["L0_ffn_w_up"], np.float32)[PI3][:, PC]),
        "wdown": np.ascontiguousarray(np.asarray(inp["L0_ffn_w_down"], np.float32)[PC][:, PI3]),
        "ident": np.eye(128, dtype=np.float32),
        "dmasks": dmv, "maskneg": mneg, "gmask": gm,
    }


_NC_CACHE = {}


def kernel(**inputs):
    if "main" not in _NC_CACHE:
        _NC_CACHE["main"] = build_program()
    nc = _NC_CACHE["main"]
    in_maps = [make_inputs(inputs, c) for c in range(8)]
    res = run_bass_kernel_spmd(nc, in_maps, core_ids=list(range(8)))
    out = np.zeros((B, T, C), np.float32)
    for c in range(8):
        b, g = c // 2, c % 2
        out[b, 2048 * g:2048 * (g + 1), :] = np.asarray(res.results[c]["out"])[PC].T
    return out


# version 14 (final)
# speedup vs baseline: 1.1200x; 1.0237x over previous
import numpy as np
import contextlib
import concourse.bass as bass
import concourse.mybir as mybir
from concourse.bass_utils import run_bass_kernel_spmd

F32 = mybir.dt.float32
AF = mybir.ActivationFunctionType
ALU = mybir.AluOpType
AX = mybir.AxisListType
MAGIC = 12582912.0
NEG_BIG = -3.0e38

B, T, C, H, D, I = 4, 4096, 1024, 16, 64, 3072
TL = 2048
RG_PAIR = [[0, 1], [2, 3], [4, 5], [6, 7]]
RG_ALL = [[0, 1, 2, 3, 4, 5, 6, 7]]


class Sched:
    def __init__(self):
        self.ops = {k: [] for k in ("sync", "tensor", "vector", "scalar", "gpsimd")}
        self.cnt = {}

    def inc(self, name, by):
        self.cnt[name] = self.cnt.get(name, 0) + by
        return self.cnt[name]

    def val(self, name):
        return self.cnt.get(name, 0)


def build_program(debug=(), stop_after=None):
    nc = bass.Bass("TRN2", target_bir_lowering=False)

    def din(name, shape):
        return nc.dram_tensor(name, shape, F32, kind="ExternalInput").ap()

    def dint(name, shape, **kw):
        return nc.dram_tensor(name, shape, F32, kind="Internal", **kw).ap()

    xbT = din("xbT", [C, T])
    xtokT = din("xtokT", [C, TL])
    wq = din("wq", [4, C, 128])
    wk = din("wk", [4, C, 128])
    wv = din("wv", [4, C, 128])
    wp = din("wp", [C, C])
    bpc = din("bpc", [C, 1])
    rtT = din("rtT", [16, 33])
    wgate = din("wgate", [I, C])
    wup = din("wup", [I, C])
    wdown = din("wdown", [C, I])
    ident = din("ident", [128, 128])
    dmasks = din("dmasks", [17, 128, 144])
    maskneg = din("maskneg", [128, 144])
    gmask = din("gmask", [1, 2])
    out_d = nc.dram_tensor("out", [C, TL], F32, kind="ExternalOutput").ap()

    xhat_d = dint("xhat_d", [C, T])
    rsin = [dint(f"rsin{p}", [2, 256, TL]) for p in range(4)]
    rsout = [dint(f"rsout{p}", [256, TL]) for p in range(4)]
    arin_d = [dint(f"arin{k}", [1, 128]) for k in range(6)]
    arout_d = [dint(f"arout{k}", [1, 128], addr_space="Shared") for k in range(6)]
    gate_d = dint("gate_d", [I, TL])
    up_d = dint("up_d", [I, TL])
    g_d = dint("g_d", [I, TL])
    opre_d = dint("opre_d", [I, TL])
    dbg_d = {nm: nc.dram_tensor("dbg_" + nm, list(shp), F32, kind="ExternalOutput").ap()
             for nm, shp in debug}

    S = Sched()
    es = contextlib.ExitStack()
    with es:
        A1 = es.enter_context(nc.sbuf_tensor("A1", [128, 16384], F32))
        A2 = es.enter_context(nc.sbuf_tensor("A2", [128, 16384], F32))
        A3 = es.enter_context(nc.sbuf_tensor("A3", [128, 8192], F32))
        WA = es.enter_context(nc.sbuf_tensor("WA", [128, 6208], F32))
        SM = es.enter_context(nc.sbuf_tensor("SM", [128, 1600], F32))
        RW = es.enter_context(nc.sbuf_tensor("RW", [128, 4300], F32))
        PB = [es.enter_context(nc.psum_tensor(f"pb{i}", [128, 512], F32)) for i in range(8)]
        sems = {nm: es.enter_context(nc.semaphore(nm))
                for nm in ("LD0", "LD1", "LD2", "LD3", "LD4", "LD5",
                           "SP0", "SP1", "SP2", "GD0", "GD1", "TE", "VE", "SE", "CC")}
        rot = {"sync": ["LD0", "LD1", "LD2", "LD3", "LD4", "LD5"],
               "scalar": ["SP0", "SP1", "SP2"], "gpsimd": ["GD0", "GD1"]}
        rot_i = {"sync": 0, "scalar": 0, "gpsimd": 0}
        blk = es.enter_context(nc.Block())

        def waits(e, need):
            flat = {}
            for it in need:
                if it is None:
                    continue
                items = it if isinstance(it, list) else [it]
                for ent in items:
                    if ent is None:
                        continue
                    nm, v = ent
                    if v > 0:
                        flat[nm] = max(flat.get(nm, 0), v)
            for nm, v in flat.items():
                e.wait_ge(sems[nm], v)

        def dma(queue, out, in_, need=(), sem=None, slow=False):
            if sem is None:
                sem = rot[queue][rot_i[queue]]
                rot_i[queue] = (rot_i[queue] + 1) % len(rot[queue])
            prev = S.val(sem)
            tgt = S.inc(sem, 16)
            nd = tuple(need)

            def fn(e, out=out, in_=in_, nd=nd, sem=sem, slow=slow, prev=prev):
                waits(e, nd)
                if prev > 0:
                    e.wait_ge(sems[sem], prev)
                e.dma_start(out=out, in_=in_,
                            allow_slow_non_contiguous=slow).then_inc(sems[sem], 16)
            S.ops[queue].append(fn)
            return (sem, tgt)

        def qall(queue):
            return [(s, S.val(s)) for s in rot[queue] if S.val(s) > 0]

        def te(body, need=()):
            tgt = S.inc("TE", 1)
            nd = tuple(need)

            def fn(e, nd=nd, body=body):
                waits(e, nd)
                body(e).then_inc(sems["TE"], 1)
            S.ops["tensor"].append(fn)
            return tgt

        def ve(body, need=(), pre_drain=True):
            tgt = S.inc("VE", 1)
            nd = tuple(need)

            def fn(e, nd=nd, body=body, pre_drain=pre_drain):
                waits(e, nd)
                if pre_drain:
                    e.drain()
                body(e).then_inc(sems["VE"], 1)
            S.ops["vector"].append(fn)
            return tgt

        def ve0(body, need=(), pre_drain=False):
            nd = tuple(need)

            def fn(e, nd=nd, body=body, pre_drain=pre_drain):
                waits(e, nd)
                if pre_drain:
                    e.drain()
                body(e)
            S.ops["vector"].append(fn)

        def se(body, need=()):
            tgt = S.inc("SE", 1)
            nd = tuple(need)

            def fn(e, nd=nd, body=body):
                waits(e, nd)
                body(e).then_inc(sems["SE"], 1)
            S.ops["scalar"].append(fn)
            return tgt

        def gp(body, need=()):
            tgt = S.inc("CC", 1)
            nd = tuple(need)

            def fn(e, nd=nd, body=body):
                waits(e, nd)
                body(e).then_inc(sems["CC"], 1)
            S.ops["gpsimd"].append(fn)
            return tgt

        class _StopEmit(Exception):
            pass

        ID_ = SM[:, 0:128]
        BANDS = 128
        RTT = SM[:, 1280:1313]
        DMAT = SM[:, 1313:1330]
        ONES = SM[:, 1330:1331]
        BP0 = 1331
        EB0 = 1344
        DT0 = 1346
        SC0 = 1362
        EPS = SM[:, 1394:1395]
        ONESR = SM[0:1, 1440:1568]
        GMB = SM[:, 1396:1398]
        QS0 = 1400
        WMX = 1412

        truncated = [False]
        try:
            dma("sync", ID_, ident)
            dma("sync", RTT[0:16, :], rtT)
            ld_masks = dma("sync", A3[:, 0:2448].rearrange("p (v j) -> p v j", j=144),
                           dmasks.rearrange("v p j -> p v j"))
            ld_mneg = dma("sync", A3[:, 2448:2592], maskneg)
            ld_gm = dma("sync", RW[0:1, 4224:4226], gmask)
            dma("sync", SM[:, BP0:BP0 + 8], bpc.rearrange("(cc p) o -> p (cc o)", p=128), slow=True)
            ld_const = qall("sync")

            ve(lambda e: e.memset(ONES, 1.0), pre_drain=False)
            ve(lambda e: e.memset(ONESR, 1.0), pre_drain=False)
            ve(lambda e: e.memset(EPS, 1e-5), pre_drain=False)
            v_neg = ve(lambda e: e.memset(RW[0:1, 4096:4224], NEG_BIG), pre_drain=False)
            t_gm = te(lambda e: e.matmul(PB[7][:, 0:2], ONESR, RW[0:1, 4224:4226],
                                         start=True, stop=True),
                      need=[ld_const, ("VE", v_neg)])
            s_gm = se(lambda e: e.copy(GMB, PB[7][:, 0:2]), need=[("TE", t_gm)])

            v_dmat = ve(lambda e: e.tensor_scalar(
                out=DMAT[0:16, :], in0=RTT[0:16, 16::-1], scalar1=RTT[0:16, 0:1],
                scalar2=None, op0=ALU.subtract), need=[ld_const])
            v_mh = ve(lambda e: e.reduce_max(SM[0:16, SC0 + 20:SC0 + 21], RTT[0:16, 0:17],
                                             axis=AX.X))
            v_pre = ve(lambda e: e.tensor_tensor(
                out=SM[0:16, SC0 + 21:SC0 + 22], in0=RTT[0:16, 0:1],
                in1=SM[0:16, SC0 + 20:SC0 + 21], op=ALU.subtract))
            t_dt = te(lambda e: e.transpose(PB[7][0:17, 0:16], DMAT[0:16, :], ID_[0:16, 0:16]),
                      need=[("VE", v_dmat), ("SE", s_gm)])
            s_dt = se(lambda e: e.copy(SM[0:17, DT0:DT0 + 16], PB[7][0:17, 0:16]),
                      need=[("TE", t_dt)])
            t_pre = te(lambda e: e.transpose(PB[7][0:1, 0:16], SM[0:16, SC0 + 21:SC0 + 22], ID_[0:16, 0:16]),
                       need=[("VE", v_pre), ("SE", s_dt)])
            s_pre = se(lambda e: e.copy(RW[0:1, 4243:4259], PB[7][0:1, 0:16]),
                       need=[("TE", t_pre)])
            PREROW = RW[0:1, 4243:4259]

            pa, pb_ = A3[:, 2592:2736], A3[:, 2736:2880]
            s_r_prev = s_pre
            band_v = 0
            for hl in range(8):
                t_r = te(lambda e, hl=hl: e.transpose(PB[7][0:1, 0:17],
                                                      SM[0:17, DT0 + hl:DT0 + hl + 1],
                                                      ID_[0:17, 0:17]),
                         need=[("SE", s_r_prev)])
                s_r = se(lambda e: e.copy(RW[0:1, 4226:4243], PB[7][0:1, 0:17]),
                         need=[("TE", t_r)])
                s_r_prev = s_r
                t_b = te(lambda e: e.matmul(PB[6][:, 0:17], ONESR,
                                            RW[0:1, 4226:4243], start=True, stop=True),
                         need=[("SE", s_r)])
                s_b = se(lambda e: e.copy(SM[:, SC0:SC0 + 17], PB[6][:, 0:17]),
                         need=[("TE", t_b), ("VE", band_v)])
                ve(lambda e: e.tensor_copy(pa, A3[:, 2448:2592]),
                   need=[ld_const, ("SE", s_b)])
                for v_i in range(17):
                    src, dst = (pa, pb_) if v_i % 2 == 0 else (pb_, pa)
                    ve(lambda e, v_i=v_i, src=src, dst=dst: e.scalar_tensor_tensor(
                        out=dst, in0=A3[:, v_i * 144:(v_i + 1) * 144],
                        scalar=SM[:, SC0 + v_i:SC0 + v_i + 1], in1=src,
                        op0=ALU.mult, op1=ALU.add))
                band_v = ve(lambda e, hl=hl: e.tensor_copy(
                    SM[:, BANDS + hl * 144:BANDS + (hl + 1) * 144], pb_))

            d_w0 = dma("sync", arin_d[5], RW[0:1, 4096:4224], need=[("VE", v_neg)])
            gp(lambda e: e.collective_compute("AllReduce", ALU.max, ins=[arin_d[5]],
                                              outs=[arout_d[5]], replica_groups=RG_ALL),
               need=[d_w0])

            if stop_after == "startup": raise _StopEmit
            st_prev_A = {0: None, 1: None}
            prev_apply = {0: 0, 1: 0}
            prev_bc_se = 0
            last_tq = 0
            ap_i = 0
            ld_bA = {}

            def emit_ldA(tt):
                par = tt % 2
                base3 = par * 4096
                nd = [("VE", band_v)] if tt == 0 else [("VE", prev_apply[par])]
                ld_bA[tt] = dma("sync",
                                A3[:, base3:base3 + 4096].rearrange("p (cc t) -> p cc t", t=512),
                                xbT.rearrange("(cc p) t -> p cc t", p=128)[:, :, 512 * tt:512 * (tt + 1)],
                                need=nd)

            emit_ldA(0)
            for tt in range(8):
                par = tt % 2
                base3 = par * 4096
                ld_batch = ld_bA[tt]
                tiles = [A3[:, base3 + cc * 512:base3 + cc * 512 + 512] for cc in range(8)]
                t_s = t_q = 0
                for cc in range(8):
                    buf = tiles[cc]
                    t_s = te(lambda e, buf=buf, cc=cc: e.matmul(
                        PB[5][0:1, :], ONES[:, 0:1], buf, start=(cc == 0), stop=(cc == 7)),
                        need=[ld_batch] + ([("SE", prev_bc_se)] if cc == 0 else []))
                    s_q = se(lambda e, buf=buf: e.activation(
                        out=A1[:, 0:512], in_=buf, func=AF.Square),
                        need=[ld_batch, ("TE", t_q if cc > 0 else last_tq)])
                    t_q = te(lambda e, cc=cc: e.matmul(
                        PB[6][0:1, :], ONES[:, 0:1], A1[:, 0:512],
                        start=(cc == 0), stop=(cc == 7)), need=[("SE", s_q)])
                last_tq = t_q
                if tt + 1 < 8:
                    emit_ldA(tt + 1)
                s_e1 = se(lambda e: e.copy(RW[0:1, 0:512], PB[5][0:1, :]), need=[("TE", t_q)])
                s_e2 = se(lambda e: e.copy(RW[0:1, 512:1024], PB[6][0:1, :]))
                prev_bc_se = s_e2
                t_b1 = te(lambda e: e.matmul(PB[7][:, :], ONESR, RW[0:1, 0:512],
                                             start=True, stop=True), need=[("SE", s_e2)])
                t_b2 = te(lambda e: e.matmul(PB[4][:, :], ONESR, RW[0:1, 512:1024],
                                             start=True, stop=True))
                mslc = A1[:, 512 + par * 1024:1024 + par * 1024]
                rslc = A1[:, 1024 + par * 1024:1536 + par * 1024]
                s_m = se(lambda e, mslc=mslc: e.mul(mslc, PB[7][:, :], 1.0 / 1024.0),
                         need=[("TE", t_b2), ("VE", prev_apply[par])])
                s_v2 = se(lambda e, rslc=rslc: e.mul(rslc, PB[4][:, :], 1.0 / 1024.0))
                ve(lambda e, mslc=mslc: e.tensor_mul(A1[:, 2560:3072], mslc, mslc),
                   need=[("SE", s_v2)])
                v_t2 = ve(lambda e, rslc=rslc: e.tensor_tensor(
                    out=rslc, in0=rslc, in1=A1[:, 2560:3072], op=ALU.subtract))
                s_sd = se(lambda e, rslc=rslc: e.activation(
                    out=rslc, in_=rslc, func=AF.Sqrt, bias=EPS, scale=1.0),
                    need=[("VE", v_t2)])
                v_r = ve(lambda e, rslc=rslc: e.reciprocal(rslc, rslc), need=[("SE", s_sd)])
                for cc in range(8):
                    buf = tiles[cc]
                    ob = A1[:, 3072 + (ap_i % 2) * 512:3072 + (ap_i % 2) * 512 + 512]
                    ve(lambda e, buf=buf, ob=ob, mslc=mslc: e.tensor_tensor(
                        out=ob, in0=buf, in1=mslc, op=ALU.subtract),
                       need=([("VE", v_r)] if cc == 0 else []) + [st_prev_A[ap_i % 2]],
                       pre_drain=(cc == 0))
                    v_a2 = ve(lambda e, ob=ob, rslc=rslc: e.tensor_tensor(
                        out=ob, in0=ob, in1=rslc, op=ALU.mult))
                    prev_apply[par] = v_a2
                    st_prev_A[ap_i % 2] = dma(
                        "sync", xhat_d[128 * cc:128 * (cc + 1), 512 * tt:512 * (tt + 1)],
                        ob, need=[("VE", v_a2)])
                    ap_i += 1

            if stop_after == "A": raise _StopEmit
            xhat_all = qall("sync")
            Q0, V0, AO0 = 0, 4096, 8256
            KR = RW[:, 0:4096]
            rs_cc = []
            prev_pair_se = 0
            prev_pair_te = 0
            prev_stg_refs = []
            for p in range(4):
                lw = []
                for wsrc, off in ((wq, 0), (wk, 1024), (wv, 2048)):
                    lw.append(dma("sync", WA[:, off:off + 1024].rearrange("p (cc m) -> p cc m", m=128),
                                  wsrc[p].rearrange("(cc p) m -> p cc m", p=128),
                                  need=[("TE", prev_pair_te)]))
                v_wq = ve(lambda e: e.tensor_scalar(out=WA[:, 0:1024], in0=WA[:, 0:1024],
                                                    scalar1=0.125, scalar2=None, op0=ALU.mult),
                          need=[lw[0]])
                for ch in range(32):
                    ve0(lambda e, ch=ch: e.memset(A1[:, V0 + ch * 130 + 64:V0 + ch * 130 + 65], 1.0),
                        need=[("TE", prev_pair_te)] if ch == 0 else ())
                    ve0(lambda e, ch=ch: e.memset(A1[:, V0 + ch * 130 + 129:V0 + ch * 130 + 130], 1.0))
                v_vones = ve(lambda e: e.nop(), pre_drain=True)

                prev_evac_se = {0: prev_pair_se, 1: prev_pair_se}
                for tt in range(8):
                    if p == 0 and tt == 0:
                        nd = [xhat_all]
                    else:
                        nd = [("SE", prev_evac_se[tt % 2])]
                    if tt < 2:
                        nd = nd + prev_stg_refs
                    base3 = (tt % 2) * 4096
                    pbq, pbk, pbv = (PB[0], PB[1], PB[2]) if tt % 2 == 0 else (PB[3], PB[6], PB[7])
                    ld_batch = dma(
                        "sync", A3[:, base3:base3 + 4096].rearrange("p (cc t) -> p cc t", t=512),
                        xhat_d.rearrange("(cc p) t -> p cc t", p=128)[:, :, 512 * tt:512 * (tt + 1)],
                        need=nd)
                    xt_tiles = [A3[:, base3 + cc * 512:base3 + cc * 512 + 512] for cc in range(8)]
                    q_need = [("VE", v_wq), ("SE", prev_evac_se[tt % 2])] + lw
                    for woff, pbi in ((0, pbq), (1024, pbk)):
                        for cc in range(8):
                            buf = xt_tiles[cc]
                            te(lambda e, buf=buf, woff=woff, cc=cc, pbi=pbi: e.matmul(
                                pbi[:, :], WA[:, woff + cc * 128:woff + cc * 128 + 128], buf,
                                start=(cc == 0), stop=(cc == 7)),
                               need=[ld_batch] + (q_need if cc == 0 else []))
                    for j in range(4):
                        for cc in range(8):
                            buf = xt_tiles[cc]
                            te(lambda e, buf=buf, j=j, cc=cc, pbv=pbv: e.matmul(
                                pbv[:, j * 128:(j + 1) * 128], buf[:, j * 128:(j + 1) * 128],
                                WA[:, 2048 + cc * 128:2048 + cc * 128 + 128],
                                start=(cc == 0), stop=(cc == 7)),
                               need=[ld_batch] + ([("SE", prev_evac_se[tt % 2])] if cc == 0 and j == 0 else []))
                    t_all = S.val("TE")
                    se(lambda e, tt=tt, pbq=pbq: e.copy(
                        A1[:, Q0 + 512 * tt:Q0 + 512 * (tt + 1)], pbq[:, :]),
                       need=[("TE", t_all), ("VE", v_vones)])
                    se(lambda e, tt=tt, pbk=pbk: e.copy(KR[:, 512 * tt:512 * (tt + 1)], pbk[:, :]))
                    for j in range(4):
                        ch = tt * 4 + j
                        se(lambda e, j=j, ch=ch, pbv=pbv: e.copy(
                            A1[:, V0 + ch * 130:V0 + ch * 130 + 64], pbv[:, j * 128:j * 128 + 64]))
                        se(lambda e, j=j, ch=ch, pbv=pbv: e.copy(
                            A1[:, V0 + ch * 130 + 65:V0 + ch * 130 + 129],
                            pbv[:, j * 128 + 64:j * 128 + 128]))
                    prev_evac_se[tt % 2] = S.val("SE")
                qkv_se = S.val("SE")
                if stop_after == f"qkv{p}":
                    raise _StopEmit

                t_n = 0
                s_sq_prev = 0
                for src_i, tag in enumerate(("q", "k")):
                    s_sq = se(lambda e, src_i=src_i: e.activation(
                        out=A3[:, 4096:8192],
                        in_=(A1[:, Q0:Q0 + 4096] if src_i == 0 else KR[:, 0:4096]),
                        func=AF.Square),
                        need=([("SE", qkv_se), ("VE", S.val("VE")), ("TE", S.val("TE"))]
                              if src_i == 0 else [("TE", t_n)]))
                    for hh in range(2):
                        base = 64 * hh
                        for tc in range(32):
                            col = 64 * src_i + 32 * hh + tc
                            t_n = te(lambda e, base=base, tc=tc, col=col: e.matmul(
                                PB[4][0:128, col:col + 1],
                                A3[base:base + 64, 4096 + 128 * tc:4096 + 128 * (tc + 1)],
                                ONES[base:base + 64, 0:1], start=True, stop=True),
                                need=[("SE", s_sq)] if hh == 0 and tc == 0 else [])
                    s_sq_prev = s_sq
                s_nc = se(lambda e: e.copy(A3[:, 3840:3968], PB[4][:, 0:128]),
                          need=[("TE", t_n)])
                v_red = 0
                for g4 in range(4):
                    v_red = ve(lambda e, g4=g4: e.reduce_max(
                        SM[:, SC0 + 20 + g4:SC0 + 21 + g4],
                        A3[:, 3840 + 32 * g4:3840 + 32 * (g4 + 1)], axis=AX.X),
                        need=[("SE", s_nc)] if g4 == 0 else [], pre_drain=(g4 == 0))
                t_t1 = te(lambda e: e.transpose(PB[5][0:4, 0:128], SM[:, SC0 + 20:SC0 + 24],
                                                ID_), need=[("VE", v_red)])
                s_t1 = se(lambda e: e.copy(A3[0:4, 3968:4096], PB[5][0:4, 0:128]),
                          need=[("TE", t_t1)])
                v_c = ve(lambda e: e.reduce_max(A3[0:4, 4000 - 32:4000 - 31][0:4, 0:1]
                                                if False else SM[0:4, SC0 + 24:SC0 + 25],
                                                A3[0:4, 3968:4096], axis=AX.X),
                         need=[("SE", s_t1)])
                t_t2 = te(lambda e: e.transpose(PB[5][0:1, 0:4], SM[0:4, SC0 + 24:SC0 + 25],
                                                ID_[0:4, 0:4]), need=[("VE", v_c)])
                s_t2 = se(lambda e: e.copy(RW[0:1, 4280:4284], PB[5][0:1, 0:4]),
                          need=[("TE", t_t2)])
                v_p1 = ve(lambda e: e.tensor_tensor(out=RW[0:1, 4284:4286],
                                                    in0=RW[0:1, 4280:4282],
                                                    in1=RW[0:1, 4282:4284], op=ALU.mult),
                          need=[("SE", s_t2)])
                s_sq2 = se(lambda e: e.activation(out=RW[0:1, 4286:4288], in_=RW[0:1, 4284:4286],
                                                  func=AF.Sqrt), need=[("VE", v_p1)])
                v_eb2 = ve(lambda e, p=p: e.tensor_tensor(
                    out=RW[0:1, 4288:4290], in0=PREROW[0:1, 2 * p:2 * p + 2],
                    in1=RW[0:1, 4286:4288], op=ALU.subtract), need=[("SE", s_sq2)])
                t_eb = te(lambda e: e.matmul(PB[5][:, 0:2], ONESR, RW[0:1, 4288:4290],
                                             start=True, stop=True), need=[("VE", v_eb2)])
                s_eb = se(lambda e: e.copy(SM[:, EB0:EB0 + 2], PB[5][:, 0:2]),
                          need=[("TE", t_eb)])
                eb_v = ve(lambda e: e.nop(), need=[("SE", s_eb)], pre_drain=False)
                head_ebias_v = [eb_v, eb_v]
                if stop_after == f"stab{p}":
                    raise _StopEmit

                den = [A3[0:1, 0:4096], A3[0:1, 4096:8192]]
                exp_hist = {}
                av_hist = {}
                prev_evac = S.val("SE")
                sc_banks = {0: (PB[0], PB[1], PB[4]), 1: (PB[6], PB[7], PB[5])}
                av_banks = {0: PB[2], 1: PB[3]}
                EXP0 = 12352
                LOOK = 2
                den_rd = 0
                for tt in range(8):
                    n_sc = 4 * (tt + 1)
                    last_av = {0: 0, 1: 0}

                    def emit_score(sc, hh, tt=tt):
                        so = 128 * sc
                        t_start = max(512 * tt, so)
                        N = 512 * (tt + 1) - t_start
                        bi = sc % 3
                        off = t_start - 512 * tt
                        base = 64 * hh
                        hl = 2 * p + hh
                        band = SM[:, BANDS + hl * 144:BANDS + (hl + 1) * 144]
                        pb_s = sc_banks[hh][bi]
                        t_sc = te(lambda e, base=base, so=so, t_start=t_start, N=N,
                                  pb_s=pb_s: e.matmul(
                            pb_s[:, 0:N], KR[base:base + 64, so:so + 128],
                            A1[base:base + 64, Q0 + t_start:Q0 + t_start + N],
                            start=True, stop=True),
                            need=[("SE", max(exp_hist.get((hh, bi), 0), prev_evac))])
                        v_band = 0
                        if so >= 512 * tt:
                            w = min(144, N)
                            v_band = ve(lambda e, pb_s=pb_s, w=w, band=band: e.tensor_tensor(
                                out=pb_s[:, 0:w], in0=pb_s[:, 0:w], in1=band[:, 0:w],
                                op=ALU.add), need=[("TE", t_sc)], pre_drain=False)
                        elif so == 512 * tt - 128:
                            v_band = ve(lambda e, pb_s=pb_s, band=band: e.tensor_tensor(
                                out=pb_s[:, 0:16], in0=pb_s[:, 0:16], in1=band[:, 128:144],
                                op=ALU.add), need=[("TE", t_sc)], pre_drain=False)
                        ebuf = A1[:, EXP0 + (3 * hh + bi) * 512:EXP0 + (3 * hh + bi) * 512 + 512]
                        v_z = 0
                        if off > 0:
                            v_z = ve(lambda e, ebuf=ebuf, off=off: e.memset(ebuf[:, 0:off], 0.0),
                                     need=[("TE", av_hist.get((hh, bi), 0))], pre_drain=False)
                        nd = [("TE", max(t_sc, av_hist.get((hh, bi), 0))),
                              ("VE", max(v_band, v_z, head_ebias_v[hh]))]
                        s_e = se(lambda e, ebuf=ebuf, off=off, N=N, pb_s=pb_s, hh=hh:
                                 e.activation(out=ebuf[:, off:off + N], in_=pb_s[:, 0:N],
                                              func=AF.Exp,
                                              bias=SM[:, EB0 + hh:EB0 + hh + 1],
                                              scale=1.0), need=nd)
                        exp_hist[(hh, bi)] = s_e
                        return (sc, hh, bi, ebuf, s_e)

                    def emit_av(tok, tt=tt, n_sc=n_sc):
                        sc, hh, bi, ebuf, s_e = tok
                        vch = V0 + sc * 130
                        lhs_off = vch if hh == 0 else vch + 65
                        t_av = te(lambda e, lhs_off=lhs_off, hh=hh, ebuf=ebuf, sc=sc,
                                  n_sc=n_sc: e.matmul(
                            av_banks[hh][0:65, :], A1[:, lhs_off:lhs_off + 65], ebuf,
                            start=(sc == 0), stop=(sc == n_sc - 1)),
                            need=[("SE", s_e)])
                        av_hist[(hh, bi)] = t_av
                        last_av[hh] = t_av

                    pend = []
                    for sc in range(n_sc):
                        for hh in range(2):
                            pend.append(emit_score(sc, hh))
                        if sc >= LOOK:
                            emit_av(pend.pop(0))
                            emit_av(pend.pop(0))
                    while pend:
                        emit_av(pend.pop(0))
                    for hh in range(2):
                        base = 64 * hh
                        se(lambda e, hh=hh, tt=tt, base=base: e.copy(
                            A1[base:base + 64, AO0 + 512 * tt:AO0 + 512 * (tt + 1)],
                            av_banks[hh][0:64, :]), need=[("TE", last_av[hh])])
                        s_dn = se(lambda e, hh=hh, tt=tt: e.copy(
                            den[hh][0:1, 512 * tt:512 * (tt + 1)], av_banks[hh][64:65, :]))
                        den_rd = ve(lambda e, hh=hh, tt=tt: e.reciprocal(
                            den[hh][0:1, 512 * tt:512 * (tt + 1)],
                            den[hh][0:1, 512 * tt:512 * (tt + 1)]),
                            need=[("SE", s_dn)])
                    prev_evac = S.val("SE")
                if stop_after == f"sc{p}":
                    raise _StopEmit

                RCP0 = EXP0 + 3072
                s_rb_prev = 0
                for tt in range(8):
                    t_b0 = te(lambda e, tt=tt: e.matmul(PB[4][0:64, :], ONESR[0:1, 0:64],
                                                        den[0][0:1, 512 * tt:512 * (tt + 1)],
                                                        start=True, stop=True),
                              need=[("VE", den_rd)] + ([("SE", s_rb_prev)] if s_rb_prev else []))
                    t_b1 = te(lambda e, tt=tt: e.matmul(PB[4][64:128, :], ONESR[0:1, 0:64],
                                                        den[1][0:1, 512 * tt:512 * (tt + 1)],
                                                        start=True, stop=True))
                    s_rb = se(lambda e: e.copy(A1[:, RCP0:RCP0 + 512], PB[4][:, :]),
                              need=[("TE", t_b1), ("VE", S.val("VE"))])
                    s_rb_prev = s_rb
                    ve0(lambda e, tt=tt: e.tensor_tensor(
                        out=A1[0:64, AO0 + 512 * tt:AO0 + 512 * (tt + 1)],
                        in0=A1[0:64, AO0 + 512 * tt:AO0 + 512 * (tt + 1)],
                        in1=A1[0:64, RCP0:RCP0 + 512], op=ALU.mult), need=[("SE", s_rb)])
                    v_n1 = ve(lambda e, tt=tt: e.tensor_tensor(
                        out=A1[64:128, AO0 + 512 * tt:AO0 + 512 * (tt + 1)],
                        in0=A1[64:128, AO0 + 512 * tt:AO0 + 512 * (tt + 1)],
                        in1=A1[64:128, RCP0:RCP0 + 512], op=ALU.mult), pre_drain=False)
                v_norm = S.val("VE")
                st_stg = {0: None, 1: None}
                stg_refs = []
                for r in range(2):
                    for blkb in range(2):
                        stg = A3[:, blkb * 2048:blkb * 2048 + 2048]
                        v_m = ve(lambda e, stg=stg, r=r, blkb=blkb: e.tensor_scalar(
                            out=stg, in0=A1[:, AO0 + 2048 * r:AO0 + 2048 * r + 2048],
                            scalar1=GMB[:, blkb:blkb + 1], scalar2=None, op0=ALU.mult),
                            need=[("VE", v_norm), st_stg[blkb]])
                        st_stg[blkb] = dma("sync", rsin[p][r, 128 * blkb:128 * (blkb + 1), :],
                                           stg, need=[("VE", v_m)])
                        stg_refs.append(st_stg[blkb])
                rs_cc.append(gp(lambda e, p=p: e.collective_compute(
                    "ReduceScatter", ALU.add, ins=[rsin[p]], outs=[rsout[p]],
                    replica_groups=RG_PAIR), need=list(stg_refs)))
                prev_pair_se = S.val("SE")
                prev_pair_te = S.val("TE")
                prev_stg_refs = list(stg_refs)
                if stop_after == f"pair{p}":
                    raise _StopEmit

            if stop_after == "BC": raise _StopEmit
            ld_wp = dma("sync", A1[:, 8192:16384].rearrange("p (jc cc m) -> p jc cc m", cc=8, m=128),
                        wp.rearrange("(jc p) (cc m) -> p jc cc m", p=128, m=128),
                        need=[("VE", S.val("VE"))])

            def ln_stats(src_arena, need0_se=0, need0_v=0):
                t_last = 0
                for cc in range(8):
                    nd = [("SE", need0_se), ("TE", t_last)]
                    if cc == 0 and need0_v:
                        nd.append(("VE", need0_v))
                    s_sq = se(lambda e, cc=cc: e.activation(
                        out=A3[:, 0:2048], in_=src_arena[:, cc * TL:(cc + 1) * TL],
                        func=AF.Square), need=nd)
                    for tc in range(16):
                        te(lambda e, cc=cc, tc=tc: e.matmul(
                            PB[5][:, 16 * cc + tc:16 * cc + tc + 1],
                            src_arena[:, cc * TL + 128 * tc:cc * TL + 128 * (tc + 1)],
                            ONES[:, 0:1], start=True, stop=True),
                           need=[("SE", s_sq)] if tc == 0 and cc == 0 else [])
                        te(lambda e, tc=tc, cc=cc: e.matmul(
                            PB[5][:, 256 + 16 * cc + tc:256 + 16 * cc + tc + 1],
                            A3[:, 128 * tc:128 * (tc + 1)],
                            ONES[:, 0:1], start=True, stop=True),
                           need=[("SE", s_sq)] if tc == 0 else [])
                    t_last = S.val("TE")
                s_ev = se(lambda e: e.copy(RW[:, 3072:3328] if False else RW[:, 3392:3904],
                                           PB[5][:, 0:512]), need=[("TE", t_last)])
                v_f = 0
                for half, base0 in ((64, 3392), (64, 3648)):
                    b0 = base0
                    for wdt in (64, 32, 16):
                        v_f = ve(lambda e, b0=b0, wdt=wdt: e.tensor_tensor(
                            out=RW[:, b0:b0 + wdt], in0=RW[:, b0:b0 + wdt],
                            in1=RW[:, b0 + wdt:b0 + 2 * wdt], op=ALU.add),
                            need=[("SE", s_ev)] if wdt == 64 else [])
                ve(lambda e: e.tensor_scalar(out=RW[:, 3072:3088], in0=RW[:, 3392:3408],
                                             scalar1=1.0 / 1024.0, scalar2=None,
                                             op0=ALU.mult))
                ve(lambda e: e.tensor_scalar(out=RW[:, 3088:3104], in0=RW[:, 3648:3664],
                                             scalar1=1.0 / 1024.0, scalar2=None,
                                             op0=ALU.mult), pre_drain=False)
                ve(lambda e: e.tensor_mul(RW[:, 3104:3120], RW[:, 3072:3088],
                                          RW[:, 3072:3088]))
                v_var = ve(lambda e: e.tensor_tensor(out=RW[:, 3088:3104],
                                                     in0=RW[:, 3088:3104],
                                                     in1=RW[:, 3104:3120],
                                                     op=ALU.subtract))
                s_sd = se(lambda e: e.activation(out=RW[:, 3088:3104], in_=RW[:, 3088:3104],
                                                 func=AF.Sqrt, bias=EPS, scale=1.0),
                          need=[("VE", v_var)])
                v_rc = ve(lambda e: e.reciprocal(RW[:, 3088:3104], RW[:, 3088:3104]),
                          need=[("SE", s_sd)])
                t_t1 = te(lambda e: e.transpose(PB[5][0:16, 0:128], RW[:, 3072:3088], ID_),
                          need=[("VE", v_rc)])
                t_t2 = te(lambda e: e.transpose(PB[6][0:16, 0:128], RW[:, 3088:3104], ID_))
                s_t1 = se(lambda e: e.copy(RW[0:16, 3136:3264], PB[5][0:16, 0:128]),
                          need=[("TE", t_t2)])
                s_t2 = se(lambda e: e.copy(RW[0:16, 3264:3392], PB[6][0:16, 0:128]))
                d_m = dma("sync", A3[0:1, 0:2048], RW[0:16, 3136:3264],
                          need=[("SE", s_t2)])
                d_r = dma("sync", A3[0:1, 2048:4096], RW[0:16, 3264:3392])
                s_bc_prev = 0
                last_v = 0
                for tt4 in range(4):
                    t_b1 = te(lambda e, tt4=tt4: e.matmul(
                        PB[5][:, :], ONESR, A3[0:1, 512 * tt4:512 * (tt4 + 1)],
                        start=True, stop=True),
                        need=[d_m, d_r] + ([("SE", s_bc_prev)] if s_bc_prev else [("SE", s_t2)]))
                    t_b2 = te(lambda e, tt4=tt4: e.matmul(
                        PB[6][:, :], ONESR, A3[0:1, 2048 + 512 * tt4:2048 + 512 * (tt4 + 1)],
                        start=True, stop=True))
                    s_m = se(lambda e, tt4=tt4: e.copy(
                        A3[:, 4096 + 512 * tt4:4096 + 512 * (tt4 + 1)], PB[5][:, :]),
                        need=[("TE", t_b2)])
                    s_bc_prev = se(lambda e, tt4=tt4: e.copy(
                        A3[:, 6144 + 512 * tt4:6144 + 512 * (tt4 + 1)], PB[6][:, :]))
                last_v = ve(lambda e: e.nop(), need=[("SE", s_bc_prev)], pre_drain=False)
                return last_v

            rs_all = rs_cc[-1]
            s_prev = 0
            for tt4 in range(4):
                ld_refs = []
                for p4 in range(4):
                    nd = [("CC", rs_all)] if tt4 == 0 and p4 == 0 else \
                         ([("SE", s_prev)] if p4 == 0 else [])
                    ld_refs.append(dma(
                        "sync",
                        A3[:, 0:4096].rearrange("p (blk j t) -> p blk j t",
                                                blk=2, j=4, t=512)[:, :, p4, :],
                        rsout[p4].rearrange("(blk p) t -> p blk t", p=128)[:, :, 512 * tt4:512 * (tt4 + 1)],
                        need=nd))
                tls = [A3[:, j * 512:j * 512 + 512] for j in range(8)]
                for cc in range(8):
                    t_m = 0
                    for j in range(8):
                        buf = tls[j]
                        t_m = te(lambda e, buf=buf, j=j, cc=cc: e.matmul(
                            PB[4][:, :], A1[:, 8192 + (j * 8 + cc) * 128:8192 + (j * 8 + cc) * 128 + 128],
                            buf, start=(j == 0), stop=(j == 7)),
                            need=[ld_wp] + ld_refs + ([("SE", s_prev)] if j == 0 else []))
                    s_prev = se(lambda e, cc=cc, tt4=tt4: e.add(
                        A2[:, cc * TL + 512 * tt4:cc * TL + 512 * tt4 + 512], PB[4][:, :],
                        SM[:, BP0 + cc:BP0 + cc + 1]), need=[("TE", t_m)])

            ln3_v = ln_stats(A2, need0_se=s_prev)
            x1_v = 0
            ld_prev_v = ln3_v
            for cc in range(8):
                xb = A3[:, 1024:3072]
                ld = dma("sync", xb, xtokT[128 * cc:128 * (cc + 1), :],
                         need=[("VE", ld_prev_v)])
                ve(lambda e, cc=cc: e.tensor_tensor(
                    out=A1[:, cc * TL:(cc + 1) * TL], in0=A2[:, cc * TL:(cc + 1) * TL],
                    in1=A3[:, 4096:6144], op=ALU.subtract),
                   need=[("VE", ln3_v), ld], pre_drain=(cc == 0))
                ve(lambda e, cc=cc: e.tensor_tensor(
                    out=A1[:, cc * TL:(cc + 1) * TL], in0=A1[:, cc * TL:(cc + 1) * TL],
                    in1=A3[:, 6144:8192], op=ALU.mult))
                x1_v = ve(lambda e, cc=cc, xb=xb: e.tensor_tensor(
                    out=A1[:, cc * TL:(cc + 1) * TL], in0=A1[:, cc * TL:(cc + 1) * TL],
                    in1=xb, op=ALU.add))
                ld_prev_v = x1_v

            ln2_v = ln_stats(A1, need0_v=x1_v)
            v_mx = ve(lambda e: e.memset(SM[:, WMX:WMX + 1], 0.0), pre_drain=False)
            xln_v = 0
            for cc in range(8):
                ve(lambda e, cc=cc: e.tensor_tensor(
                    out=A2[:, cc * TL:(cc + 1) * TL], in0=A1[:, cc * TL:(cc + 1) * TL],
                    in1=A3[:, 4096:6144], op=ALU.subtract),
                   need=[("VE", ln2_v)] if cc == 0 else (), pre_drain=(cc == 0))
                ve(lambda e, cc=cc: e.tensor_tensor(
                    out=A2[:, cc * TL:(cc + 1) * TL], in0=A2[:, cc * TL:(cc + 1) * TL],
                    in1=A3[:, 6144:8192], op=ALU.mult))
                ve0(lambda e, cc=cc: e.reduce_max(SM[:, WMX + 8:WMX + 9],
                                                  A2[:, cc * TL:(cc + 1) * TL],
                                                  axis=AX.X, apply_absolute_value=True),
                    pre_drain=True)
                xln_v = ve(lambda e: e.tensor_max(SM[:, WMX:WMX + 1], SM[:, WMX:WMX + 1],
                                                  SM[:, WMX + 8:WMX + 9]))

            ar_state = {"s_b": 0}

            def allreduce_scale(round_i, colmax_col, qs_off, extra_pre_scale=None, need_v=0):
                t_t = te(lambda e, colmax_col=colmax_col: e.transpose(
                    PB[7][0:1, 0:128], SM[:, colmax_col:colmax_col + 1], ID_),
                    need=[("VE", need_v), ("SE", ar_state["s_b"])])
                s_t = se(lambda e: e.copy(RW[0:1, 4096:4224], PB[7][0:1, 0:128]),
                         need=[("TE", t_t)])
                d0 = dma("sync", arin_d[round_i], RW[0:1, 4096:4224], need=[("SE", s_t)])
                cc_i = gp(lambda e, round_i=round_i: e.collective_compute(
                    "AllReduce", ALU.max, ins=[arin_d[round_i]], outs=[arout_d[round_i]],
                    replica_groups=RG_ALL), need=[d0])
                d1 = dma("sync", RW[0:1, 4096:4224], arout_d[round_i], need=[("CC", cc_i)])
                v1 = ve(lambda e: e.reduce_max(RW[0:1, 4270:4271], RW[0:1, 4096:4224], axis=AX.X),
                        need=[d1])
                if extra_pre_scale is not None:
                    v1 = ve(lambda e, x=extra_pre_scale: e.tensor_scalar(
                        out=RW[0:1, 4270:4271], in0=RW[0:1, 4270:4271], scalar1=x,
                        scalar2=None, op0=ALU.mult))
                v2 = ve(lambda e: e.tensor_scalar(out=RW[0:1, 4271:4272], in0=RW[0:1, 4270:4271],
                                                  scalar1=1.0 / 32767.0, scalar2=1e-12,
                                                  op0=ALU.mult, op1=ALU.max))
                v3 = ve(lambda e: e.reciprocal(RW[0:1, 4272:4273], RW[0:1, 4271:4272]))
                t_b = te(lambda e: e.matmul(PB[7][:, 0:2], ONESR, RW[0:1, 4271:4273],
                                            start=True, stop=True),
                         need=[("VE", v3), ("SE", s_t)])
                s_b = se(lambda e, qs_off=qs_off: e.copy(SM[:, QS0 + qs_off:QS0 + qs_off + 2],
                                                         PB[7][:, 0:2]), need=[("TE", t_b)])
                ar_state["s_b"] = s_b
                return ve(lambda e: e.nop(), need=[("SE", s_b)], pre_drain=False)

            ar0_v = allreduce_scale(0, WMX, 0, need_v=xln_v)

            xq_v = 0
            for cc in range(8):
                ve(lambda e, cc=cc: e.tensor_scalar(
                    out=A2[:, cc * TL:(cc + 1) * TL], in0=A2[:, cc * TL:(cc + 1) * TL],
                    scalar1=SM[:, QS0 + 1:QS0 + 2], scalar2=MAGIC, op0=ALU.mult, op1=ALU.add),
                   need=[("VE", ar0_v)] if cc == 0 else (), pre_drain=(cc == 0))
                xq_v = ve(lambda e, cc=cc: e.tensor_scalar(
                    out=A2[:, cc * TL:(cc + 1) * TL], in0=A2[:, cc * TL:(cc + 1) * TL],
                    scalar1=MAGIC, scalar2=SM[:, QS0:QS0 + 1], op0=ALU.subtract, op1=ALU.mult))

            if stop_after == "D": raise _StopEmit
            SB_ = WA[:, 6144:6176]
            SI_ = WA[:, 6176:6208]

            SIE = RW[:, 1024:2048]
            SBE = RW[:, 2048:3072]

            def dequant_1024(nat, need0, need_mults=(), out=None):
                s_abs = se(lambda e, nat=nat: e.activation(out=WA[:, 2048:3072], in_=nat,
                                                           func=AF.Abs), need=list(need0))
                w_ = 512
                nd_f = [("SE", s_abs)]
                while w_ >= 32:
                    ve(lambda e, w_=w_: e.tensor_tensor(
                        out=WA[:, 2048:2048 + w_], in0=WA[:, 2048:2048 + w_],
                        in1=WA[:, 2048 + w_:2048 + 2 * w_], op=ALU.max), need=nd_f)
                    nd_f = []
                    w_ //= 2
                ve(lambda e: e.tensor_scalar(out=SB_[:, :], in0=WA[:, 2048:2080],
                                             scalar1=1.0 / 7.0,
                                             scalar2=1e-12, op0=ALU.mult, op1=ALU.max))
                v_rc = ve(lambda e: e.reciprocal(SI_[:, :], SB_[:, :]))
                d_e1 = dma("sync", SIE[:].rearrange("p (j n) -> p j n", n=32),
                           SI_[:].rearrange("p n -> p () n").broadcast_to((128, 32, 32)),
                           need=[("VE", v_rc), ("VE", st["dq_last"])])
                d_e2 = dma("sync", SBE[:].rearrange("p (j n) -> p j n", n=32),
                           SB_[:].rearrange("p n -> p () n").broadcast_to((128, 32, 32)))
                if out is None:
                    out = WA[:, 3072:4096]
                ve(lambda e, nat=nat, out=out: e.tensor_tensor(out=out, in0=nat,
                                                               in1=SIE[:], op=ALU.mult),
                   need=[d_e1, d_e2] + list(need_mults), pre_drain=False)
                ve(lambda e, out=out: e.tensor_scalar(out=out, in0=out,
                                                      scalar1=MAGIC, scalar2=MAGIC,
                                                      op0=ALU.add, op1=ALU.subtract))
                ve(lambda e, out=out: e.tensor_tensor(out=out, in0=out,
                                                      in1=SBE[:], op=ALU.mult))
                return ve(lambda e: e.nop(), pre_drain=True)

            sp_prev = {0: None, 1: None}
            v_red_prev = {0: 0, 1: 0}
            wmax_done = {}
            st = {"dq_hist": {0: 0, 1: 0}, "dq_last": 0, "stg_mm": {0: 0, 1: 0},
                  "tr_te": 0, "stg_se": 0, "s_ev": 0, "dq": {}}

            def emit_f2_prep(wsrc, ic, first):
                par = ic % 2
                nat = WA[:, par * 1024:par * 1024 + 1024]
                ld = dma("sync", nat, wsrc[128 * ic:128 * (ic + 1), :],
                         need=[("VE", st["dq_hist"][par])] if st["dq_hist"][par] else
                              ([("VE", xq_v)] if first else []))
                dq_v = dequant_1024(nat, [ld, ("VE", st["dq_last"])],
                                    need_mults=[("TE", st["tr_te"])])
                st["dq_hist"][par] = dq_v
                st["dq_last"] = dq_v
                st["dq"][ic] = dq_v

            def emit_f2_stage(ic):
                par = ic % 2
                dq_v = st["dq"][ic]
                stg0 = 4096 + par * 1024
                s_t_hist = {}
                for cc in range(8):
                    pbt = PB[5 + (cc % 2)]
                    t_t = te(lambda e, cc=cc, pbt=pbt: e.transpose(
                        pbt[:, 0:128], WA[:, 3072 + cc * 128:3072 + cc * 128 + 128], ID_),
                        need=[("VE", dq_v),
                              ("SE", s_t_hist.get(cc - 2, st["stg_se"]))])
                    s_t = se(lambda e, cc=cc, stg0=stg0, pbt=pbt: e.copy(
                        WA[:, stg0 + cc * 128:stg0 + cc * 128 + 128], pbt[:, 0:128]),
                        need=[("TE", max(t_t, st["stg_mm"][par] if cc == 0 else 0))])
                    s_t_hist[cc] = s_t
                st["stg_se"] = S.val("SE")
                st["tr_te"] = S.val("TE")

            def emit_f2_mms(wdst, mxc, ic):
                par = ic % 2
                stg0 = 4096 + par * 1024
                s_stg = st["stg_se"]
                for tt4 in range(4):
                    t_m = 0
                    for cc in range(8):
                        t_m = te(lambda e, cc=cc, tt4=tt4, stg0=stg0: e.matmul(
                            PB[4][:, :], WA[:, stg0 + cc * 128:stg0 + cc * 128 + 128],
                            A2[:, cc * TL + 512 * tt4:cc * TL + 512 * tt4 + 512],
                            start=(cc == 0), stop=(cc == 7)),
                            need=([("SE", max(s_stg, st["s_ev"])), ("VE", xq_v)] if cc == 0 else []))
                    sp = RW[:, (tt4 % 2) * 512:(tt4 % 2) * 512 + 512]
                    s_ev = se(lambda e, sp=sp: e.copy(sp, PB[4][:, :]),
                              need=[("TE", t_m), sp_prev[tt4 % 2],
                                    ("VE", v_red_prev[tt4 % 2])])
                    st["s_ev"] = s_ev
                    ve0(lambda e, sp=sp, mxc=mxc: e.reduce_max(
                        SM[:, mxc + 8:mxc + 9], sp, axis=AX.X, apply_absolute_value=True),
                        need=[("SE", s_ev)], pre_drain=True)
                    v_red_prev[tt4 % 2] = ve(lambda e, mxc=mxc: e.tensor_max(
                        SM[:, mxc:mxc + 1], SM[:, mxc:mxc + 1], SM[:, mxc + 8:mxc + 9]))
                    sp_prev[tt4 % 2] = dma("scalar", wdst[128 * ic:128 * (ic + 1),
                                                          512 * tt4:512 * (tt4 + 1)],
                                           sp, need=[("SE", s_ev)])
                st["stg_mm"][par] = S.val("TE")

            ve(lambda e: e.memset(SM[:, WMX + 1:WMX + 2], 0.0), pre_drain=False)
            emit_f2_prep(wgate, 0, first=True)
            for ic in range(24):
                emit_f2_stage(ic)
                if ic + 1 < 24:
                    emit_f2_prep(wgate, ic + 1, first=False)
                else:
                    emit_f2_prep(wup, 0, first=False)
                emit_f2_mms(gate_d, WMX + 1, ic)
            wmax_done[0] = S.val("VE")
            gate_sp_refs = qall("scalar")
            ve(lambda e: e.memset(SM[:, WMX + 2:WMX + 3], 0.0), pre_drain=False)
            ve(lambda e: e.memset(SM[:, WMX + 3:WMX + 4], 0.0), pre_drain=False)
            v_hist3 = {0: 0, 1: 0}
            st_h3 = {0: None, 1: None}

            f3_gq = {}

            def emit_f3_head(ic):
                par = ic % 2
                gb = A3[:, par * 2048:par * 2048 + 2048]
                gq = A3[:, 4096 + par * 2048:4096 + par * 2048 + 2048]
                ld = dma("sync", gb, gate_d[128 * ic:128 * (ic + 1), :],
                         need=gate_sp_refs if ic == 0 else [("VE", v_hist3[par])])
                ve(lambda e, gb=gb, gq=gq: e.tensor_scalar(
                    out=gq, in0=gb, scalar1=SM[:, QS0 + 3:QS0 + 4], scalar2=MAGIC,
                    op0=ALU.mult, op1=ALU.add),
                   need=[ld, ("VE", ar1_v), st_h3[par]], pre_drain=False)
                f3_gq[ic] = ve(lambda e, gq=gq: e.tensor_scalar(
                    out=gq, in0=gq, scalar1=MAGIC, scalar2=SM[:, QS0 + 2:QS0 + 3],
                    op0=ALU.subtract, op1=ALU.mult))

            def emit_f3_ic(ic):
                par = ic % 2
                gb = A3[:, par * 2048:par * 2048 + 2048]
                gq = A3[:, 4096 + par * 2048:4096 + par * 2048 + 2048]
                sg = gb
                v_gq = f3_gq[ic]
                s_sig = se(lambda e, gq=gq, sg=sg: e.activation(out=sg, in_=gq,
                                                                func=AF.Sigmoid),
                           need=[("VE", v_gq)])
                ve(lambda e, sg=sg: e.tensor_scalar(
                    out=sg, in0=sg, scalar1=65536.0, scalar2=MAGIC, op0=ALU.mult,
                    op1=ALU.add), need=[("SE", s_sig)], pre_drain=False)
                ve(lambda e, sg=sg: e.tensor_scalar(
                    out=sg, in0=sg, scalar1=MAGIC, scalar2=65535.0, op0=ALU.subtract,
                    op1=ALU.min))
                ve(lambda e, gq=gq, sg=sg: e.tensor_tensor(out=gq, in0=gq, in1=sg,
                                                           op=ALU.mult))
                ve0(lambda e, gq=gq: e.reduce_max(SM[:, WMX + 9:WMX + 10], gq, axis=AX.X,
                                                  apply_absolute_value=True), pre_drain=True)
                v_m3 = ve(lambda e: e.tensor_max(SM[:, WMX + 3:WMX + 4],
                                                 SM[:, WMX + 3:WMX + 4],
                                                 SM[:, WMX + 9:WMX + 10]))
                v_hist3[par] = v_m3
                st_h3[par] = dma("gpsimd", g_d[128 * ic:128 * (ic + 1), :], gq,
                                 need=[("VE", v_m3)])

            ar1_v = None
            for ic in range(24):
                emit_f2_stage(ic)
                if ic + 1 < 24:
                    emit_f2_prep(wup, ic + 1, first=False)
                if ic == 0:
                    ar1_v = allreduce_scale(1, WMX + 1, 2, need_v=wmax_done[0])
                emit_f3_head(ic)
                emit_f2_mms(up_d, WMX + 2, ic)
                emit_f3_ic(ic)
            wmax_done[1] = S.val("VE")
            ar2_v = allreduce_scale(2, WMX + 2, 4, need_v=wmax_done[1])
            ar3_v = allreduce_scale(3, WMX + 3, 6, extra_pre_scale=1.0 / 65536.0,
                                    need_v=S.val("VE"))
            v_adj = ve(lambda e: e.tensor_scalar(out=SM[:, QS0 + 7:QS0 + 8],
                                                 in0=SM[:, QS0 + 7:QS0 + 8],
                                                 scalar1=1.0 / 65536.0, scalar2=None,
                                                 op0=ALU.mult), need=[("VE", ar3_v)])

            if stop_after == "F2": raise _StopEmit
            if stop_after == "F3": raise _StopEmit
            st5 = {"nat": {0: 0, 1: 0}, "ote": {0: 0, 1: 0, 2: 0}, "mm": None,
                   "dq": {}, "acc_se": 0, "acc_v": 0, "pb56": None}
            WDQ2 = 6144

            def emit_f5_prep_h(hg, part, hh):
                hc = 2 * hg + hh
                nat = WA[:, hh * 1024:hh * 1024 + 1024]
                ld = dma("sync", nat,
                         wdown[128 * hc:128 * (hc + 1), 1024 * part:1024 * (part + 1)],
                         need=[("VE", st5["nat"][hh] if st5["nat"][hh]
                                else st["dq_last"])])
                dq_v = dequant_1024(
                    nat, [ld, ("VE", st["dq_last"])],
                    need_mults=[("TE", st["tr_te"])],
                    out=(WA[:, 3072:4096] if hh == 0 else A3[:, WDQ2:WDQ2 + 1024]))
                st5["nat"][hh] = dq_v
                st["dq_last"] = dq_v
                st5["dq"][hh] = dq_v

            def emit_f5_stage(part):
                mm_g = st5["mm"] if st5["mm"] is not None else S.val("TE")
                s_t_hist = {}
                for u in range(16):
                    hh, q8 = u // 8, u % 8
                    srcb = (WA[:, 3072 + q8 * 128:3072 + q8 * 128 + 128] if hh == 0
                            else A3[:, WDQ2 + q8 * 128:WDQ2 + q8 * 128 + 128])
                    pbt = PB[5 + (u % 2)]
                    nd = [("VE", st5["dq"][hh])] if q8 == 0 else []
                    if u < 2:
                        nd.append(st5["pb56"] if st5["pb56"] is not None
                                  else ("SE", st["stg_se"]))
                    else:
                        nd.append(("SE", s_t_hist[u - 2]))
                    t_t = te(lambda e, pbt=pbt, srcb=srcb: e.transpose(
                        pbt[:, 0:128], srcb, ID_), need=nd)
                    s_t = se(lambda e, u=u, hh=hh, q8=q8, pbt=pbt: e.copy(
                        WA[:, 4096 + hh * 1024 + q8 * 128:
                              4096 + hh * 1024 + q8 * 128 + 128], pbt[:, 0:128]),
                        need=[("TE", max(t_t, mm_g if u == 0 else 0))])
                    s_t_hist[u] = s_t
                st["stg_se"] = S.val("SE")
                st["tr_te"] = S.val("TE")

            def emit_f5_mms(hg, part):
                for q8 in range(8):
                    ici = part * 8 + q8
                    slot = ici % 3
                    obuf = A3[:, slot * 2048:slot * 2048 + 2048]
                    nd_ld = ([("TE", st5["ote"][slot])] if st5["ote"][slot]
                             else qall("scalar") + qall("gpsimd"))
                    ld_o = dma("sync", obuf, opre_d[128 * ici:128 * (ici + 1), :],
                               need=nd_ld)
                    ve(lambda e, obuf=obuf: e.tensor_scalar(
                        out=obuf, in0=obuf, scalar1=SM[:, QS0 + 9:QS0 + 10],
                        scalar2=MAGIC, op0=ALU.mult, op1=ALU.add),
                       need=[ld_o, ("VE", ar4_v)], pre_drain=False)
                    v_q = ve(lambda e, obuf=obuf: e.tensor_scalar(
                        out=obuf, in0=obuf, scalar1=MAGIC,
                        scalar2=SM[:, QS0 + 8:QS0 + 9],
                        op0=ALU.subtract, op1=ALU.mult))
                    for hh in range(2):
                        for tt4 in range(4):
                            nd = [("VE", v_q)] if hh == 0 and tt4 == 0 else []
                            if q8 == 0 and hh == 0 and tt4 == 0:
                                nd += [("SE", st["stg_se"]), ("SE", st5["acc_se"]),
                                       ("VE", st5["acc_v"])]
                            t_m = te(lambda e, obuf=obuf, hh=hh, q8=q8, tt4=tt4:
                                     e.matmul(
                                PB[hh * 4 + tt4][:, :],
                                WA[:, 4096 + hh * 1024 + q8 * 128:
                                      4096 + hh * 1024 + q8 * 128 + 128],
                                obuf[:, 512 * tt4:512 * (tt4 + 1)],
                                start=(q8 == 0), stop=(q8 == 7)), need=nd)
                    st5["ote"][slot] = t_m
                    if q8 == 2:
                        emit_f5_prep_next(hg, part, 0)
                    elif q8 == 5:
                        emit_f5_prep_next(hg, part, 1)
                st5["mm"] = S.val("TE")

            def emit_f5_prep_next(hg, part, hh):
                if part < 2:
                    emit_f5_prep_h(hg, part + 1, hh)
                elif hg < 3:
                    emit_f5_prep_h(hg + 1, 0, hh)

            F5_BANKS = [(1, 1), (1, 2), (1, 0), (1, 3), (0, 0), (0, 1), (0, 2), (0, 3)]

            def emit_f5_acc(hg, part):
                first = True
                v56 = 0
                for hh, tt4 in F5_BANKS:
                    hc = 2 * hg + hh
                    dst = A2[:, hc * TL + 512 * tt4:hc * TL + 512 * tt4 + 512]
                    if part == 0:
                        r = st5["acc_se"] = se(
                            lambda e, dst=dst, hh=hh, tt4=tt4: e.copy(
                                dst, PB[hh * 4 + tt4][:, :]),
                            need=[("TE", st5["mm"])] if first else [])
                    else:
                        r = st5["acc_v"] = ve(
                            lambda e, dst=dst, hh=hh, tt4=tt4: e.tensor_tensor(
                                out=dst, in0=PB[hh * 4 + tt4][:, :], in1=dst,
                                op=ALU.add),
                            need=[("TE", st5["mm"])] if first else [],
                            pre_drain=False)
                    if (hh, tt4) == (1, 2):
                        v56 = r
                    first = False
                st5["pb56"] = (("SE" if part == 0 else "VE"), v56)

            ve(lambda e: e.memset(SM[:, WMX + 4:WMX + 5], 0.0), pre_drain=False)
            v_hist4 = {0: 0, 1: 0}
            gp_hist4 = {0: 0, 1: 0}
            st_h4 = {0: None, 1: None}
            for ic in range(24):
                par = ic % 2
                if ic == 22:
                    emit_f5_prep_h(0, 0, 0)
                Gb = A3[:, par * 2048:par * 2048 + 2048]
                ub = A3[:, 4096 + par * 2048:4096 + par * 2048 + 2048]
                ld1 = dma("sync", Gb, g_d[128 * ic:128 * (ic + 1), :],
                          need=(qall("scalar") + qall("gpsimd") if ic == 0 else
                                [("VE", v_hist4[par]), ("CC", gp_hist4[par])]) +
                               [st_h4[par]])
                ld2 = dma("sync", ub, up_d[128 * ic:128 * (ic + 1), :])
                ve(lambda e, Gb=Gb: e.tensor_scalar(
                    out=Gb, in0=Gb, scalar1=SM[:, QS0 + 7:QS0 + 8], scalar2=MAGIC,
                    op0=ALU.mult, op1=ALU.add),
                   need=[ld1, ld2, ("VE", max(v_adj, ar2_v))], pre_drain=False)
                ve(lambda e, Gb=Gb: e.tensor_scalar(
                    out=Gb, in0=Gb, scalar1=MAGIC, scalar2=SM[:, QS0 + 6:QS0 + 7],
                    op0=ALU.subtract, op1=ALU.mult))
                gp_q = gp(lambda e, ub=ub: e.tensor_scalar(
                    out=ub, in0=ub, scalar1=SM[:, QS0 + 5:QS0 + 6], scalar2=MAGIC,
                    op0=ALU.mult, op1=ALU.add), need=[ld2, ("VE", ar2_v)])
                gp_hist4[par] = gp_q
                ve(lambda e, ub=ub: e.tensor_scalar(
                    out=ub, in0=ub, scalar1=MAGIC, scalar2=SM[:, QS0 + 4:QS0 + 5],
                    op0=ALU.subtract, op1=ALU.mult), need=[("CC", gp_q)],
                   pre_drain=False)
                ve(lambda e, Gb=Gb, ub=ub: e.tensor_tensor(out=Gb, in0=Gb, in1=ub, op=ALU.mult))
                ve0(lambda e, Gb=Gb: e.reduce_max(SM[:, WMX + 10:WMX + 11], Gb, axis=AX.X,
                                                  apply_absolute_value=True), pre_drain=True)
                v_m4 = ve(lambda e: e.tensor_max(SM[:, WMX + 4:WMX + 5], SM[:, WMX + 4:WMX + 5],
                                                 SM[:, WMX + 10:WMX + 11]))
                v_hist4[par] = v_m4
                st_h4[par] = dma("scalar", opre_d[128 * ic:128 * (ic + 1), :], Gb,
                                 need=[("VE", v_m4)])
            ar4_v = allreduce_scale(4, WMX + 4, 8, need_v=S.val("VE"))

            if stop_after == "F4": raise _StopEmit
            emit_f5_prep_h(0, 0, 1)
            for hg in range(4):
                for part in range(3):
                    emit_f5_stage(part)
                    emit_f5_mms(hg, part)
                    emit_f5_acc(hg, part)

            if stop_after == "F5b": raise _StopEmit
            ln4_v = ln_stats(A2, need0_se=st5["acc_se"], need0_v=st5["acc_v"])
            st_out = None
            out_v_prev = ln4_v
            for cc in range(8):
                ob = A3[:, 1024:3072]
                ve(lambda e, cc=cc, ob=ob: e.tensor_tensor(
                    out=ob, in0=A2[:, cc * TL:(cc + 1) * TL], in1=A3[:, 4096:6144],
                    op=ALU.subtract),
                   need=[("VE", out_v_prev), st_out], pre_drain=True)
                ve(lambda e, ob=ob: e.tensor_tensor(out=ob, in0=ob, in1=A3[:, 6144:8192],
                                                    op=ALU.mult))
                v_o = ve(lambda e, cc=cc, ob=ob: e.tensor_tensor(
                    out=ob, in0=ob, in1=A1[:, cc * TL:(cc + 1) * TL], op=ALU.add))
                out_v_prev = v_o
                st_out = dma("sync", out_d[128 * cc:128 * (cc + 1), :], ob,
                             need=[("VE", v_o)])


        except _StopEmit:
            truncated[0] = True
            st_out = dma("sync", out_d[0:128, 0:128], SM[:, 0:128], need=())
        tapsrc = {"xhat": xhat_d, "gate": gate_d, "up": up_d, "G": g_d, "opre": opre_d,
                  "rsout0": rsout[0], "rsout1": rsout[1],
                  "rsout2": rsout[2], "rsout3": rsout[3]}
        for nm in dbg_d:
            st_out = dma("sync", dbg_d[nm], tapsrc[nm], need=[st_out])

        final_refs = qall("sync") + qall("scalar") + qall("gpsimd")

        def fin(e, refs=tuple(final_refs)):
            waits(e, refs)
        S.ops["sync"].append(fin)

        @blk.sync
        def _(eng):
            for fn in S.ops["sync"]:
                fn(eng)

        @blk.tensor
        def _(eng):
            for fn in S.ops["tensor"]:
                fn(eng)

        @blk.vector
        def _(eng):
            for fn in S.ops["vector"]:
                fn(eng)

        @blk.scalar
        def _(eng):
            for fn in S.ops["scalar"]:
                fn(eng)

        @blk.gpsimd
        def _(eng):
            for fn in S.ops["gpsimd"]:
                fn(eng)

    return nc


PC = np.arange(1024).reshape(32, 32).T.reshape(-1).copy()
PI3 = np.concatenate([PC, PC + 1024, PC + 2048])


def make_inputs(inp, c):
    b, g = c // 2, c % 2
    x = np.asarray(inp["L0_attn_x"], np.float32)
    Wq = np.asarray(inp["L0_attn_Wq"], np.float32)[:, PC, :]
    Wk = np.asarray(inp["L0_attn_Wk"], np.float32)[:, PC, :]
    Wv = np.asarray(inp["L0_attn_Wv"], np.float32)[:, PC, :]
    Wp = np.asarray(inp["L0_attn_Wp"], np.float32)[:, PC]
    bp = np.asarray(inp["L0_attn_bp"], np.float32)[PC]
    rt = np.asarray(inp["L0_attn_rel_table"], np.float32)
    dmv = np.zeros((17, 128, 144), np.float32)
    for v in range(17):
        for i in range(128):
            if i + v < 144:
                dmv[v, i, i + v] = 1.0
    mneg = np.zeros((128, 144), np.float32)
    for i in range(128):
        mneg[i, :i] = -1.0e30
    rt16 = np.zeros((16, 33), np.float32)
    rt16[0:8] = rt.T[8 * g:8 * g + 8]
    gm = np.zeros((1, 2), np.float32)
    gm[0, g] = 1.0

    def pair(W):
        return np.stack([np.concatenate([W[8 * g + 2 * p], W[8 * g + 2 * p + 1]], axis=1)
                         for p in range(4)])
    return {
        "xbT": np.ascontiguousarray(x[b].T[PC]),
        "xtokT": np.ascontiguousarray(x[b, 2048 * g:2048 * (g + 1)].T[PC]),
        "wq": pair(Wq), "wk": pair(Wk), "wv": pair(Wv),
        "wp": np.ascontiguousarray(Wp),
        "bpc": np.ascontiguousarray(bp[:, None]),
        "rtT": rt16,
        "wgate": np.ascontiguousarray(np.asarray(inp["L0_ffn_w_gate"], np.float32)[PI3][:, PC]),
        "wup": np.ascontiguousarray(np.asarray(inp["L0_ffn_w_up"], np.float32)[PI3][:, PC]),
        "wdown": np.ascontiguousarray(np.asarray(inp["L0_ffn_w_down"], np.float32)[PC][:, PI3]),
        "ident": np.eye(128, dtype=np.float32),
        "dmasks": dmv, "maskneg": mneg, "gmask": gm,
    }


_NC_CACHE = {}


def kernel(**inputs):
    if "main" not in _NC_CACHE:
        _NC_CACHE["main"] = build_program()
    nc = _NC_CACHE["main"]
    in_maps = [make_inputs(inputs, c) for c in range(8)]
    res = run_bass_kernel_spmd(nc, in_maps, core_ids=list(range(8)))
    out = np.zeros((B, T, C), np.float32)
    for c in range(8):
        b, g = c // 2, c % 2
        out[b, 2048 * g:2048 * (g + 1), :] = np.asarray(res.results[c]["out"])[PC].T
    return out
